# version 04
# baseline (speedup 1.0000x reference)
import sys
sys.path.insert(0, '/opt/trn_rl_repo')
import numpy as np
import concourse.bass as bass
import concourse.mybir as mybir
from concourse.bass_utils import run_bass_kernel_spmd

F32 = mybir.dt.float32
BF16 = mybir.dt.bfloat16
I32 = mybir.dt.int32
AF = mybir.ActivationFunctionType
ALU = mybir.AluOpType
AX = mybir.AxisListType

NCORES = 8
D = 1024
T = 2048
DC = 8
TC = 4
LN_EPS = 1e-5

DEBUG = False
DBG_NOROPE = False
DBG_ROPEONLY = False


class Ctx:
    def __init__(self, nc, blk):
        self.nc, self.blk = nc, blk
        self.sem_h, self.cnt = {}, {}
        self.buf_tok = {}

    def buf_pre(self, name):
        return self.buf_tok.get(name, [])

    def buf_mark(self, name):
        self.buf_tok[name] = self.tok("mm")

    def sem(self, name):
        if name not in self.sem_h:
            self.sem_h[name] = self.nc.alloc_semaphore(name)
            self.cnt[name] = 0
        return self.sem_h[name]

    def inc(self, name, instr, k=1):
        instr.then_inc(self.sem(name), k)
        self.cnt[name] += k
        return (name, self.cnt[name])

    def tok(self, *names):
        return [(n, self.cnt.get(n, 0)) for n in names]

    def wait(self, eng, toks):
        for (n, v) in toks:
            if v > 0:
                eng.wait_ge(self.sem(n), v)


class FM:
    def __init__(self, tile, nch, tlen, col0=0):
        self.t, self.nch, self.T, self.c0 = tile, nch, tlen, col0

    def ch(self, c, t0=0, tw=None, p0=0, pw=128):
        tw = self.T - t0 if tw is None else tw
        o = self.c0 + c * self.T + t0
        return self.t[p0:p0 + pw, o: o + tw]


def sweep(ctx, *, jobs, banks, evac_eng, entry=(), mm_sem="mm"):
    blk = ctx.blk
    nb = len(banks)
    ev_sem = "evs" if evac_eng == "scalar" else ("evv" if evac_eng == "vector" else "evp")
    ev_base = ctx.cnt.get(ev_sem, 0)
    ctx.sem(ev_sem)
    mm_toks = []

    @blk.tensor
    def _(e):
        for i, job in enumerate(jobs):
            ctx.wait(e, job.get("pre", ()))
            if i < nb:
                ctx.wait(e, entry)
            else:
                e.wait_ge(ctx.sem(ev_sem), ev_base + i - nb + 1)
            ps = job["psum"](banks[i % nb])
            instr = None
            for mm in job["mms"]:
                instr = mm(e, ps)
            mm_toks.append(ctx.inc(mm_sem, instr))

    @getattr(blk, evac_eng)
    def _(e):
        for i, job in enumerate(jobs):
            ctx.wait(e, [mm_toks[i]])
            instr = job["evac"](e, job["psum"](banks[i % nb]))
            ctx.inc(ev_sem, instr)

    return ctx.tok(ev_sem)


def mm_acc(pairs):
    n = len(pairs)
    out = []
    for j, item in enumerate(pairs):
        lt, rh = item[0], item[1]
        tp = item[2] if len(item) > 2 else None
        def mk(lt=lt, rh=rh, j=j, tp=tp):
            return lambda e, ps: e.matmul(ps, lt, rh, start=(j == 0),
                                          stop=(j == n - 1), tile_position=tp)
        out.append(mk())
    return out


def load_w(ctx, dram_ap, dst_tile, *, col0=0, group=8, sem="ldg", pre=()):
    blk = ctx.blk
    K, M = dram_ap.shape
    kch = K // 128

    @blk.gpsimd
    def _(e):
        ctx.wait(e, pre)
        c = 0
        while c < kch:
            g = min(group, kch - c)
            s3 = dram_ap[c * 128:(c + g) * 128, :].rearrange(
                "(c p) m -> p c m", p=128)
            d3 = dst_tile[:, col0 + c * M: col0 + (c + g) * M].rearrange(
                "p (c m) -> p c m", m=M)
            instr = e.dma_start(d3, s3)
            ctx.inc(sem, instr, 16)
            c += g
    return ctx.tok(sem)


def emit_ln(ctx, R, x_fm, xoff, tlen, out_fm, ooff, *, rms=False, eps=LN_EPS,
            in_f32=True, gain_col=None, bias_col=None, nch=DC):
    blk = ctx.blk
    inv = 1.0 / (nch * 128)
    ones_s = (R["ones_m1f"] if in_f32 else R["ones_m1b"])[:]
    psA, psB = R["pb"][6], R["pb"][7]
    sq = R["sq"]
    rA, rB, r1, r2 = R["row_a"], R["row_b"], R["row_t1"], R["row_t2"]

    for t in range(tlen // 512):
        t0 = xoff + t * 512
        o0 = ooff + t * 512
        groups = [list(range(0, min(4, nch)))]
        if nch > 4:
            groups.append(list(range(4, nch)))
        for gi, grp in enumerate(groups):
            @blk.scalar
            def _(e, grp=grp, gi=gi):
                ctx.wait(e, ctx.tok("ln_done" if gi == 0 else "statmm"))
                instr = None
                for c in grp:
                    instr = e.activation(sq[:, (c % 4) * 512:(c % 4 + 1) * 512],
                                         x_fm.ch(c, t0, 512), AF.Square)
                ctx.inc("sq", instr)
            sq_tok = ctx.tok("sq")

            @blk.tensor
            def _(e, grp=grp, gi=gi, sq_tok=sq_tok):
                ctx.wait(e, sq_tok)
                if gi == 0:
                    ctx.wait(e, ctx.tok("ln_done") + ctx.tok("evs", "evv"))
                instr = None
                if not rms:
                    for c in grp:
                        instr = e.matmul(psA[0:1, :], ones_s, x_fm.ch(c, t0, 512),
                                         start=(c == grp[0] and gi == 0),
                                         stop=(c == grp[-1] and gi == len(groups) - 1))
                for c in grp:
                    instr = e.matmul(psB[0:1, :], R["ones_m1f"][:],
                                     sq[:, (c % 4) * 512:(c % 4 + 1) * 512],
                                     start=(c == grp[0] and gi == 0),
                                     stop=(c == grp[-1] and gi == len(groups) - 1))
                ctx.inc("statmm", instr)

        @blk.vector
        def _(e):
            ctx.wait(e, ctx.tok("statmm"))
            if not rms:
                e.tensor_single_scalar(r1[:], psA[0:1, :], inv, ALU.mult)
            instr = e.tensor_single_scalar(r2[:], psB[0:1, :], inv, ALU.mult)
            ctx.inc("lnv", instr)
        @blk.scalar
        def _(e):
            ctx.wait(e, ctx.tok("lnv"))
            if not rms:
                instr = e.activation(rA[:], r1[:], AF.Square)
                ctx.inc("lns", instr)
        if not rms:
            @blk.vector
            def _(e):
                ctx.wait(e, ctx.tok("lns"))
                instr = e.tensor_tensor(r2[:], r2[:], rA[:], ALU.subtract)
                ctx.inc("lnv", instr)
        @blk.scalar
        def _(e):
            ctx.wait(e, ctx.tok("lnv"))
            e.activation(rA[:], r2[:], AF.Ln, bias=cb(R, eps, 1))
            instr = e.activation(rA[:], rA[:], AF.Exp, scale=-0.5)
            ctx.inc("lns", instr)
        @blk.vector
        def _(e):
            ctx.wait(e, ctx.tok("lns"))
            if not rms:
                instr = e.scalar_tensor_tensor(rB[:], r1[:], -1.0, rA[:],
                                               ALU.mult, ALU.mult)
                ctx.inc("lnv", instr)
        @blk.tensor
        def _(e):
            ctx.wait(e, ctx.tok("lnv", "lns"))
            e.matmul(psA[:, :], R["ones_k1f"][:], rA[:], start=True, stop=True)
            instr = e.matmul(psB[:, :], R["ones_k1f"][:],
                             rA[:] if rms else rB[:], start=True, stop=True)
            ctx.inc("statmm", instr)
        bc_tok = ctx.tok("statmm")

        @blk.vector
        def _(e):
            ctx.wait(e, bc_tok + ctx.tok("mm", "ldg2"))
            for c in range(nch):
                tmp = sq[:, (c % 4) * 512:(c % 4 + 1) * 512]
                ctx.inc("evv", e.tensor_tensor(tmp, x_fm.ch(c, t0, 512),
                                               psA[:, :], ALU.mult))
                ctx.wait(e, ctx.tok("evv"))
                if not rms:
                    ctx.inc("evv", e.tensor_tensor(tmp, tmp, psB[:, :], ALU.add))
                    ctx.wait(e, ctx.tok("evv"))
                if gain_col is not None and bias_col is not None:
                    instr = e.tensor_scalar(out_fm.ch(c, o0, 512), tmp,
                                            gain_col(c), bias_col(c),
                                            ALU.mult, ALU.add)
                elif gain_col is not None:
                    instr = e.tensor_scalar_mul(out_fm.ch(c, o0, 512), tmp,
                                                gain_col(c))
                else:
                    instr = e.tensor_copy(out_fm.ch(c, o0, 512), tmp)
                ctx.inc("evv", instr)
                ctx.wait(e, ctx.tok("evv"))
            ctx.inc("ln_done", e.nop())
    return ctx.tok("ln_done")


def _prep_common():
    import ml_dtypes
    C = {}
    C["iden_bf16"] = np.eye(128, dtype=ml_dtypes.bfloat16)
    C["causal"] = np.where(np.arange(128)[None, :] <= np.arange(128)[:, None],
                           0.0, -1e30).astype(np.float32)
    sel = np.zeros((8, 8 * 128), dtype=np.float32)
    for e in range(8):
        sel[e, e * 128:(e + 1) * 128] = 1.0
    C["sel"] = sel.astype(ml_dtypes.bfloat16)
    return C


def _col_layout(v, nch):
    return np.ascontiguousarray(np.asarray(v, np.float32).reshape(nch, 128).T)


def alloc_resources(nc):
    sb = lambda n, s, dt: nc.alloc_sbuf_tensor(n, s, dt)
    R = {}
    R["pb"] = [nc.alloc_psum_tensor(f"pb{i}", [128, 512], F32) for i in range(8)]
    R["x"] = sb("x", [128, DC * T], F32)
    R["xn"] = sb("xn", [128, DC * T], BF16)
    R["arena_a"] = sb("arena_a", [128, 16 * 1024], BF16)
    R["arena_b"] = sb("arena_b", [128, 8 * 1024], BF16)
    R["arena_c"] = sb("arena_c", [128, 8 * 512], F32)
    R["arena_d"] = sb("arena_d", [128, 16 * 512], BF16)
    R["sq"] = sb("sq", [128, 4 * 512], F32)
    R["xn2"] = sb("xn2", [128, DC * 512], BF16)
    R["sbuf_g"] = R["xn2"]
    R["sbuf_s"] = R["arena_c"].bitcast(BF16)
    R["row_a"] = sb("row_a", [1, 512], F32)
    R["row_b"] = sb("row_b", [1, 512], F32)
    R["row_t1"] = sb("row_t1", [1, 512], F32)
    R["row_t2"] = sb("row_t2", [1, 512], F32)
    R["ones_m1f"] = sb("ones_m1f", [128, 1], F32)
    R["ones_m1b"] = sb("ones_m1b", [128, 1], BF16)
    R["ones_k1f"] = sb("ones_k1f", [1, 128], F32)
    R["cols"] = sb("cols", [128, 8], F32)
    R["cbias"] = sb("cbias", [128, 8], F32)
    R["smf"] = sb("smf", [128, 192], F32)
    R["smb"] = sb("smb", [128, 2304], BF16)
    R["cbias_vals"] = {1e-5: 0, 1e-6: 1, 1e-8: 2, 1.0 + 1e-8: 3,
                       0.5 + 1e-8: 4, 1.0: 5}
    R["iden_b"] = sb("iden_b", [128, 128], BF16)
    R["causal_t"] = sb("causal_t", [128, 128], F32)
    R["cfm"] = R["sq"].bitcast(BF16)[0:8, 512:1024]
    R["ltm"] = sb("ltm", [128, 4 * 8], F32)
    R["ltm_bf"] = sb("ltm_bf", [128, 4 * 8], BF16)
    R["coeffB"] = R["sq"].bitcast(BF16)[0:128, 0:512]
    return R


def cb(R, v, pw=128):
    c = R["cbias_vals"][v]
    return R["cbias"][0:pw, c:c + 1]


def emit_init(ctx, R, d, xin):
    blk = ctx.blk

    @blk.sync
    def _(e):
        instr = e.dma_start(R["x"][:].rearrange("p (c t) -> p c t", t=T),
                            xin.rearrange("(c p) t -> p c t", p=128))
        ctx.inc("ld", instr, 16)
        for src, dst in [("iden_bf16", R["iden_b"]),
                         ("causal", R["causal_t"])]:
            if src in d:
                instr = e.dma_start(dst[:], d[src][:])
                ctx.inc("ld", instr, 16)

    @blk.gpsimd
    def _(e):
        e.memset(R["ones_m1f"][:], 1.0)
        e.memset(R["ones_m1b"][:], 1.0)
        e.memset(R["ones_k1f"][:], 1.0)
        instr = None
        for v, c in R["cbias_vals"].items():
            instr = e.memset(R["cbias"][:, c:c + 1], float(v))
        ctx.inc("evp", instr)

    @blk.scalar
    def _(e):
        ctx.wait(e, ctx.tok("ld", "evp"))
        ctx.inc("ln_done", e.nop())
    return ctx.tok("ld", "evp")


def emit_combine(ctx, R, t):
    blk = ctx.blk
    t0 = t * 512
    s_fm, g_fm = FM(R["sbuf_s"], DC, 512), FM(R["sbuf_g"], DC, 512)
    x_fm = FM(R["x"], DC, T)

    tmp_ap = R["arena_c"][:, 2048:2560]
    @blk.vector
    def _(e):
        ctx.wait(e, ctx.tok("evs", "evv", "ldg2"))
        for c in range(DC):
            ctx.inc("evv", e.tensor_tensor(tmp_ap, s_fm.ch(c), g_fm.ch(c),
                                           ALU.mult))
            ctx.wait(e, ctx.tok("evv"))
            ctx.inc("evv", e.tensor_tensor(x_fm.ch(c, t0, 512),
                                           x_fm.ch(c, t0, 512), tmp_ap, ALU.add))
            ctx.wait(e, ctx.tok("evv"))
        ctx.inc("comb", e.nop())
        ctx.inc("ln_done", e.nop())
    return ctx.tok("comb")


def emit_combine_spill(ctx, R, g_spill, s_spill):
    blk = ctx.blk
    s_fm = FM(R["sbuf_s"], DC, 512)
    g_fm = FM(R["sbuf_g"], DC, 512)
    for t in range(TC):
        t0 = t * 512
        @blk.gpsimd
        def _(e, t0=t0):
            ctx.wait(e, ctx.tok("comb") + ctx.tok("mm", "evv", "evs", "ldg2"))
            for c in range(DC):
                ctx.inc("ldg2", e.dma_start(
                    g_fm.ch(c), g_spill[:, c * T + t0: c * T + t0 + 512]), 16)
                ctx.inc("ldg2", e.dma_start(
                    s_fm.ch(c), s_spill[:, c * T + t0: c * T + t0 + 512]), 16)
        ld_tok = ctx.tok("ldg2")
        @blk.vector
        def _(e, ld_tok=ld_tok):
            ctx.wait(e, ld_tok)
            ctx.inc("evv", e.nop())
        emit_combine(ctx, R, t)


def l0_prep(I, m):
    m["l0_wq"] = np.ascontiguousarray(I["L0_attn_Wq"])
    m["l0_wk"] = np.ascontiguousarray(I["L0_attn_Wk"])
    m["l0_wv"] = np.ascontiguousarray(I["L0_attn_Wv"])
    m["l0_wo"] = np.ascontiguousarray(I["L0_attn_Wo"])
    m["l0_wr"] = np.ascontiguousarray(I["L0_ffn_Wr"])
    m["l0_br"] = np.asarray(I["L0_ffn_br"], np.float32).reshape(1, 8)
    m["l0_lng"] = _col_layout(I["L0_ffn_ln_g"], DC)
    m["l0_lnb"] = _col_layout(I["L0_ffn_ln_b"], DC)
    m["l0_w1"] = np.ascontiguousarray(I["L0_ffn_W1"])
    m["l0_b1"] = np.stack([_col_layout(I["L0_ffn_b1"][e], 32) for e in range(8)])
    m["l0_w2"] = np.ascontiguousarray(I["L0_ffn_W2"])
    m["l0_b2"] = np.asarray(I["L0_ffn_b2"], np.float32)


def l0_declare(nc, d):
    def din(name, shape, dt=F32):
        d[name] = nc.dram_tensor(name, shape, dt, kind="ExternalInput").ap()
    din("l0_wq", [1024, 1024]); din("l0_wk", [1024, 1024])
    din("l0_wv", [1024, 1024]); din("l0_wo", [1024, 1024])
    din("l0_wr", [1024, 8]); din("l0_br", [1, 8])
    din("l0_lng", [128, DC]); din("l0_lnb", [128, DC])
    din("l0_w1", [8, 1024, 4096]); din("l0_b1", [8, 128, 32])
    din("l0_w2", [8, 4096, 1024]); din("l0_b2", [8, 1024])


def emit_l0(ctx, R, d, g_spill, dbg=None):
    blk = ctx.blk
    nc = ctx.nc
    x_fm = FM(R["x"], DC, T)
    xn_fm = FM(R["xn"], DC, T)

    smf, smb = R["smf"], R["smb"]
    lng = smf[0:128, 0:8]
    lnb = smf[0:128, 8:16]
    b1_t = smf[0:128, 16:48]
    br_t = smf[0:1, 48:56]
    brB = smf[0:128, 56:64]
    wr_t = smb[0:128, 0:64]
    b2_t = smb[0:8, 64:1088]
    sel_v = smb[0:8, 1088:2112]

    @blk.sync
    def _(e):
        ctx.wait(e, ctx.tok("comb"))
        for src, dst in [("l0_lng", lng), ("l0_lnb", lnb), ("l0_br", br_t),
                         ("sel", sel_v)]:
            instr = e.dma_start(dst, d[src][:])
            ctx.inc("ld", instr, 16)
    @blk.gpsimd
    def _(e):
        instr = e.dma_start(b2_t, d["l0_b2"][:])
        ctx.inc("ldg", instr, 16)
    load_w(ctx, d["l0_wr"], wr_t)
    small_tok = ctx.tok("ld", "ldg")

    ln_tok = emit_ln(ctx, R, x_fm, 0, T, xn_fm, 0, rms=False, in_f32=True)

    @blk.tensor
    def _(e):
        ctx.wait(e, small_tok + ln_tok)
        instr = e.matmul(R["pb"][5][:, 0:8], R["ones_k1f"][:], br_t,
                         start=True, stop=True)
        ctx.inc("mm", instr)
    @blk.vector
    def _(e):
        ctx.wait(e, ctx.tok("mm"))
        instr = e.tensor_copy(brB, R["pb"][5][:, 0:8])
        ctx.inc("evv", instr)

    if dbg and "dbg_xn" in dbg:
        @blk.gpsimd
        def _(e):
            ctx.wait(e, ln_tok)
            instr = e.dma_start(dbg["dbg_xn"].rearrange("(c p) t -> p c t", p=128),
                                R["xn"][:].rearrange("p (c t) -> p c t", t=T))
            ctx.inc("out", instr, 16)

    h_fm = FM(R["arena_d"], 16, 512)
    y_fm = FM(R["arena_c"], DC, 512)
    wb1 = R["arena_b"]
    wb2 = R["arena_a"]
    xn2_fm = FM(R["xn2"], DC, 512)

    for t in range(TC):
        t0 = t * 512
        emit_ln(ctx, R, xn_fm, t0, 512, xn2_fm, 0, rms=False, in_f32=False,
                gain_col=lambda c: lng[:, c:c + 1],
                bias_col=lambda c: lnb[:, c:c + 1])
        xn2_tok = ctx.tok("ln_done")

        rjobs = []
        for q in range(4):
            rjobs.append(dict(
                pre=small_tok + ctx.tok("evv"),
                mms=mm_acc([(xn_fm.ch(c, t0 + q * 128, 128),
                             wr_t[:, c * 8:(c + 1) * 8]) for c in range(DC)]),
                psum=lambda b: b[:, 0:8],
                evac=(lambda q: lambda e, ps: e.tensor_tensor(
                    R["ltm"][:, q * 8:(q + 1) * 8], ps, brB, ALU.add))(q),
            ))
        sweep(ctx, jobs=rjobs, banks=R["pb"][0:2], evac_eng="vector",
              entry=ctx.tok("evs", "evv"))

        cols = R["cols"]
        for q in range(4):
            lg = R["ltm"][:, q * 8:(q + 1) * 8]
            @blk.vector
            def _(e, q=q, lg=lg):
                ctx.wait(e, ctx.tok("evs"))
                instr = e.tensor_reduce(cols[:, 0:1], lg, AX.X, ALU.max,
                                        negate=True)
                ctx.inc("evv", instr)
            @blk.scalar
            def _(e, q=q, lg=lg):
                ctx.wait(e, ctx.tok("evv"))
                instr = e.activation(lg, lg, AF.Exp, bias=cols[:, 0:1])
                ctx.inc("evs", instr)
            @blk.vector
            def _(e, q=q, lg=lg):
                ctx.wait(e, ctx.tok("evs"))
                ctx.inc("evv", e.tensor_reduce(cols[:, 1:2], lg, AX.X, ALU.add))
                ctx.inc("evv", e.tensor_reduce(cols[:, 3:4], lg, AX.X, ALU.max))
                ctx.wait(e, ctx.tok("evv"))
                ctx.inc("evv", e.reciprocal(cols[:, 2:3], cols[:, 1:2]))
                ctx.wait(e, ctx.tok("evv"))
                ctx.inc("evv", e.tensor_tensor(cols[:, 4:5], cols[:, 3:4],
                                               cols[:, 2:3], ALU.mult))
                ctx.wait(e, ctx.tok("evv"))
                instr = e.tensor_scalar(lg, lg, cols[:, 3:4], cols[:, 4:5],
                                        ALU.is_ge, ALU.mult)
                ctx.inc("evv", instr)
            @blk.scalar
            def _(e, q=q):
                ctx.wait(e, ctx.tok("evv"))
                ctx.inc("evs", e.nop())

        @blk.vector
        def _(e):
            ctx.wait(e, ctx.tok("evs"))
            instr = e.tensor_copy(R["ltm_bf"][:], R["ltm"][:])
            ctx.inc("evv", instr)
        tjobs = []
        for q in range(4):
            tjobs.append(dict(
                pre=ctx.tok("evv"),
                mms=[(lambda q: lambda e, ps: e.transpose(
                    ps, R["ltm_bf"][:, q * 8:(q + 1) * 8], R["iden_b"][:]))(q)],
                psum=lambda b: b[0:8, 0:64].bitcast(BF16),
                evac=(lambda q: lambda e, ps: e.tensor_copy(
                    R["cfm"][:, q * 128: (q + 1) * 128], ps))(q),
            ))
        sweep(ctx, jobs=tjobs, banks=R["pb"][0:2], evac_eng="vector",
              entry=ctx.tok("evs", "evv"))
        cfm_tok = ctx.tok("evv")
        if dbg and "dbg_coeff" in dbg:
            @blk.gpsimd
            def _(e, t=t):
                ctx.wait(e, cfm_tok)
                instr = e.dma_start(dbg["dbg_coeff"][0:8, t * 512:(t + 1) * 512],
                                    R["cfm"][0:8, 0:512])
                ctx.inc("out", instr, 16)
                instr = e.dma_start(dbg["dbg_coeff"][8:8 + 128,
                                                     t * 32:(t + 1) * 32],
                                    R["ltm"][:])
                ctx.inc("out", instr, 16)

        bjobs = []
        for mch in range(DC):
            bjobs.append(dict(
                pre=cfm_tok + small_tok,
                mms=mm_acc([(b2_t[0:8, mch * 128:(mch + 1) * 128],
                             R["cfm"][0:8, 0:512])]),
                psum=lambda b: b[:, :],
                evac=(lambda mch: lambda e, ps: e.tensor_copy(
                    y_fm.ch(mch), ps))(mch),
            ))
        sweep(ctx, jobs=bjobs, banks=R["pb"][0:2], evac_eng="vector",
              entry=ctx.tok("evs", "evv"))

        for ex in range(8):
            cjobs = [dict(
                pre=cfm_tok,
                mms=mm_acc([(sel_v[0:8, ex * 128:(ex + 1) * 128],
                             R["cfm"][0:8, 0:512])]),
                psum=lambda b: b[:, :],
                evac=lambda e, ps: e.tensor_copy(R["coeffB"], ps),
            )]
            sweep(ctx, jobs=cjobs, banks=[R["pb"][2]], evac_eng="vector",
                  entry=ctx.tok("evs", "evv"))
            coefb_tok = ctx.tok("evv")

            @blk.sync
            def _(e, ex=ex):
                ctx.wait(e, ctx.tok("evs"))
                instr = e.dma_start(b1_t, d["l0_b1"][ex])
                ctx.inc("ld", instr, 16)
            b1_tok = ctx.tok("ld")

            for fc in range(4):
                f0 = fc * 1024
                slot = fc % 2
                sl_w1 = wb2[:, slot * 8192: slot * 8192 + 8192]
                w1_tok = load_w(ctx, d["l0_w1"][ex][:, f0:f0 + 1024], wb2,
                                col0=slot * 8192,
                                pre=ctx.buf_pre(("l0w", slot)))
                hjobs = []
                for fs in range(8):
                    hjobs.append(dict(
                        pre=w1_tok + xn2_tok + b1_tok,
                        mms=mm_acc([(sl_w1[:, c * 1024 + fs * 128:
                                          c * 1024 + (fs + 1) * 128],
                                     xn2_fm.ch(c)) for c in range(DC)]),
                        psum=lambda b: b[:, :],
                        evac=(lambda fc, fs: lambda e, ps:
                              e.activation(h_fm.ch(fs), ps, AF.Relu,
                                           bias=b1_t[:, fc * 8 + fs:
                                                     fc * 8 + fs + 1]))(fc, fs),
                    ))
                sweep(ctx, jobs=hjobs, banks=R["pb"][0:4], evac_eng="scalar",
                      entry=ctx.tok("evs", "evv"))
                ctx.buf_mark(("l0w", slot))
                h_tok = ctx.tok("evs")
                @blk.vector
                def _(e, h_tok=h_tok):
                    ctx.wait(e, h_tok + coefb_tok)
                    instr = None
                    for fs in range(8):
                        instr = e.tensor_tensor(h_fm.ch(fs), h_fm.ch(fs),
                                                R["coeffB"], ALU.mult)
                    ctx.inc("evv", instr)
                hc_tok = ctx.tok("evv")
                slot2 = (fc + 1) % 2
                sl_w2 = wb2[:, slot2 * 8192: slot2 * 8192 + 8192]
                w2_tok = load_w(ctx, d["l0_w2"][ex][f0:f0 + 1024, :], wb2,
                                col0=slot2 * 8192,
                                pre=ctx.buf_pre(("l0w", slot2)))
                yjobs = []
                for mch in range(DC):
                    yjobs.append(dict(
                        pre=w2_tok + hc_tok,
                        mms=mm_acc([(sl_w2[:, k * 1024 + mch * 128:
                                          k * 1024 + (mch + 1) * 128],
                                     h_fm.ch(k)) for k in range(DC)]),
                        psum=lambda b: b[:, :],
                        evac=(lambda mch: lambda e, ps: e.tensor_tensor(
                            y_fm.ch(mch), y_fm.ch(mch), ps, ALU.add))(mch),
                    ))
                sweep(ctx, jobs=yjobs, banks=R["pb"][0:4], evac_eng="vector",
                      entry=ctx.tok("evs", "evv"))
                ctx.buf_mark(("l0w", slot2))

        y_tok = ctx.tok("evv")
        if dbg and "dbg_y" in dbg:
            @blk.gpsimd
            def _(e, t=t, y_tok=y_tok):
                ctx.wait(e, y_tok)
                for c in range(DC):
                    instr = e.dma_start(
                        dbg["dbg_y"][c * 128:(c + 1) * 128, t * 512:(t + 1) * 512],
                        y_fm.ch(c))
                    ctx.inc("out", instr, 16)
        @blk.scalar
        def _(e, t=t):
            ctx.wait(e, y_tok + ctx.tok("ldg2"))
            instr = None
            for mch in range(DC):
                instr = e.activation(FM(R["sbuf_g"], DC, 512).ch(mch),
                                     y_fm.ch(mch), AF.Silu)
            ctx.inc("evs", instr)
        @blk.gpsimd
        def _(e, t=t):
            ctx.wait(e, ctx.tok("evs"))
            for mch in range(DC):
                instr = e.dma_start(
                    g_spill[:, mch * T + t0: mch * T + t0 + 512],
                    FM(R["sbuf_g"], DC, 512).ch(mch))
                ctx.inc("ldg2", instr, 16)

    if dbg and "dbg_g" in dbg:
        @blk.gpsimd
        def _(e):
            ctx.wait(e, ctx.tok("ldg2"))
            instr = e.dma_start(dbg["dbg_g"].rearrange("(c p) t -> p c t", p=128),
                                g_spill[:].rearrange("p (c t) -> p c t", t=T))
            ctx.inc("out", instr, 16)

    o_all = FM(R["arena_a"], DC, T)
    attnT = R["arena_b"]
    ad = R["arena_d"]
    QO, KO, VO, AO = 0, 2048, 4096, 6144
    sc_row = R["arena_c"]
    wsl = R["sq"].bitcast(BF16)

    for hg in range(8):
        load_w(ctx, d["l0_wq"][:, hg * 128:(hg + 1) * 128], wsl, col0=0,
               pre=ctx.tok("mm", "statmm", "ln_done"))
        load_w(ctx, d["l0_wk"][:, hg * 128:(hg + 1) * 128], wsl, col0=1024)
        load_w(ctx, d["l0_wv"][:, hg * 128:(hg + 1) * 128], wsl, col0=2048)
        w_tok = ctx.tok("ldg")

        qk_jobs = []
        for dst_col0, wcol in [(QO, 0), (KO, 1024)]:
            for tb in range(4):
                qk_jobs.append(dict(
                    pre=w_tok,
                    mms=mm_acc([(wsl[:, wcol + c * 128: wcol + (c + 1) * 128],
                                 xn_fm.ch(c, tb * 512, 512)) for c in range(DC)]),
                    psum=lambda b: b[:, :],
                    evac=(lambda dst_col0, tb: lambda e, ps: e.tensor_copy(
                        ad[:, dst_col0 + tb * 512: dst_col0 + (tb + 1) * 512], ps))(
                            dst_col0, tb),
                ))
        for ts_ in range(16):
            qk_jobs.append(dict(
                pre=w_tok,
                mms=mm_acc([(xn_fm.ch(c, ts_ * 128, 128),
                             wsl[:, 2048 + c * 128: 2048 + (c + 1) * 128])
                            for c in range(DC)]),
                psum=lambda b: b[:, 0:128],
                evac=(lambda ts_: lambda e, ps: e.tensor_copy(
                    ad[:, VO + ts_ * 128: VO + (ts_ + 1) * 128], ps))(ts_),
            ))
        sweep(ctx, jobs=qk_jobs, banks=R["pb"][0:3], evac_eng="vector",
              entry=ctx.tok("evs", "evv"))
        qkv_tok = ctx.tok("evv")

        xn2t = R["xn2"]
        sc_tok_h = [ctx.tok("evv"), ctx.tok("evv")]
        exp_tok_h = [ctx.tok("evs"), ctx.tok("evs")]
        tr_tok_h = [ctx.tok("evv"), ctx.tok("evv")]
        av_tok_h = [ctx.tok("mm"), ctx.tok("mm")]
        for tq in range(8):
            for qs in range(2):
                for h2 in range(2):
                    base = 64 * h2
                    q0 = tq * 256 + qs * 128
                    L = q0 + 128
                    arow = (lambda a, b_: ad[:, AO + a:AO + b_]) if h2 == 0                         else (lambda a, b_: xn2t[:, a:b_])
                    scr = sc_row[:, h2 * 2048: h2 * 2048 + 2048]
                    col = R["cols"][:, h2 * 4: h2 * 4 + 4]
                    sjobs = []
                    nblk = (L + 511) // 512
                    for kb in range(nblk):
                        k0 = kb * 512
                        w = min(512, L - k0)
                        has_diag = (k0 + w == L)
                        def mk_evac(k0=k0, w=w, has_diag=has_diag, L=L, scr=scr):
                            def f(e, ps):
                                if has_diag:
                                    if w > 128:
                                        e.tensor_single_scalar(
                                            scr[:, k0:k0 + w - 128],
                                            ps[:, 0:w - 128], 0.125, ALU.mult)
                                    return e.scalar_tensor_tensor(
                                        scr[:, L - 128:L], ps[:, w - 128:w],
                                        0.125, R["causal_t"][:], ALU.mult, ALU.add)
                                return e.tensor_single_scalar(
                                    scr[:, k0:k0 + w], ps[:, 0:w], 0.125,
                                    ALU.mult)
                            return f
                        sjobs.append(dict(
                            pre=qkv_tok + exp_tok_h[h2],
                            mms=[(lambda k0=k0, w=w, q0=q0, base=base:
                                  lambda e, ps: e.matmul(
                                      ps[:, 0:w],
                                      ad[base:base + 64, QO + q0:QO + q0 + 128],
                                      ad[base:base + 64, KO + k0:KO + k0 + w],
                                      start=True, stop=True))()],
                            psum=lambda b: b,
                            evac=mk_evac(),
                        ))
                    sweep(ctx, jobs=sjobs, banks=R["pb"][h2 * 2: h2 * 2 + 2],
                          evac_eng="vector", entry=sc_tok_h[h2])
                    sc_tok_h[h2] = ctx.tok("evv")

                    @blk.vector
                    def _(e, L=L, scr=scr, col=col, tok=sc_tok_h[h2]):
                        ctx.wait(e, tok)
                        instr = e.tensor_reduce(col[:, 0:1], scr[:, 0:L],
                                                AX.X, ALU.max, negate=True)
                        ctx.inc("evv", instr)
                    @blk.scalar
                    def _(e, L=L, scr=scr, col=col, arow=arow):
                        ctx.wait(e, ctx.tok("evv"))
                        instr = e.activation(arow(0, L), scr[:, 0:L],
                                             AF.Exp, bias=col[:, 0:1])
                        ctx.inc("evs", instr)
                    exp_tok_h[h2] = ctx.tok("evs")
                    @blk.vector
                    def _(e, L=L, col=col, arow=arow):
                        ctx.wait(e, ctx.tok("evs"))
                        ctx.inc("evv", e.tensor_reduce(
                            col[:, 1:2], arow(0, L), AX.X, ALU.add))
                        ctx.wait(e, ctx.tok("evv"))
                        ctx.inc("evv", e.reciprocal(col[:, 2:3], col[:, 1:2]))
                        ctx.wait(e, ctx.tok("evv"))
                        instr = e.tensor_scalar_mul(arow(0, L), arow(0, L),
                                                    col[:, 2:3])
                        ctx.inc("evv", instr)
                    sm_tok = ctx.tok("evv")

                    tjobs = []
                    for tkc in range(L // 128):
                        tjobs.append(dict(
                            pre=sm_tok + av_tok_h[h2],
                            mms=[(lambda tkc, arow=arow: lambda e, ps: e.transpose(
                                ps, arow(tkc * 128, (tkc + 1) * 128),
                                R["iden_b"][:]))(tkc)],
                            psum=lambda b: b[:, 0:64].bitcast(BF16),
                            evac=(lambda tkc, qs=qs, h2=h2: lambda e, ps:
                                  e.tensor_copy(
                                      attnT[:, h2 * 4096 + tkc * 256 + qs * 128:
                                            h2 * 4096 + tkc * 256 + qs * 128 + 128],
                                      ps))(tkc),
                        ))
                    sweep(ctx, jobs=tjobs, banks=R["pb"][4 + h2 * 2: 6 + h2 * 2],
                          evac_eng="vector", entry=tr_tok_h[h2])
                    tr_tok_h[h2] = ctx.tok("evv")

            for h2 in range(2):
                base = 64 * h2
                at_tok = tr_tok_h[h2]
                nk = 2 * (tq + 1)
                avmms = []
                for tkc in range(nk):
                    noff = max(0, (tkc - 2 * tq)) * 128
                    def mk(tkc=tkc, noff=noff, first=(tkc == 0),
                           last=(tkc == nk - 1), base=base, h2=h2):
                        def f(e, ps):
                            return e.matmul(
                                ps[0:64, noff:256],
                                ad[:, VO + tkc * 128 + base:
                                   VO + tkc * 128 + base + 64],
                                attnT[:, h2 * 4096 + tkc * 256 + noff:
                                      h2 * 4096 + (tkc + 1) * 256],
                                start=first, stop=last)
                        return f
                    avmms.append(mk())
                avjobs = [dict(
                    pre=at_tok,
                    mms=avmms,
                    psum=lambda b: b,
                    evac=(lambda hg=hg, h2=h2, tq=tq: lambda e, ps: e.tensor_copy(
                        o_all.ch(hg, tq * 256, 256, p0=64 * h2, pw=64),
                        ps[0:64, 0:256]))(),
                )]
                sweep(ctx, jobs=avjobs, banks=[R["pb"][4 + h2 * 2]],
                      evac_eng="vector", entry=tr_tok_h[h2])
                av_tok_h[h2] = ctx.tok("mm")

    wo_t = R["arena_d"]
    wo_tok = load_w(ctx, d["l0_wo"], wo_t, pre=ctx.tok("mm"))
    s_fm = FM(R["sbuf_s"], DC, 512)
    g_fm = FM(R["sbuf_g"], DC, 512)
    for t in range(TC):
        t0 = t * 512
        @blk.gpsimd
        def _(e, t=t):
            ctx.wait(e, ctx.tok("comb"))
            for c in range(DC):
                instr = e.dma_start(g_fm.ch(c),
                                    g_spill[:, c * T + t0: c * T + t0 + 512])
                ctx.inc("ldg2", instr, 16)
        g_tok = ctx.tok("ldg2")

        sjobs = []
        for mch in range(DC):
            sjobs.append(dict(
                pre=wo_tok + ctx.tok("evv"),
                mms=mm_acc([(wo_t[:, c * 1024 + mch * 128: c * 1024 + (mch + 1) * 128],
                             o_all.ch(c, t0, 512)) for c in range(DC)]),
                psum=lambda b: b[:, :],
                evac=(lambda mch, t0=t0: lambda e, ps: e.tensor_tensor(
                    s_fm.ch(mch), ps, xn_fm.ch(mch, t0, 512), ALU.add))(mch),
            ))
        sweep(ctx, jobs=sjobs, banks=R["pb"][5:7], evac_eng="vector",
              entry=ctx.tok("evs", "evv"))
        if dbg and "dbg_s" in dbg:
            @blk.gpsimd
            def _(e, t=t):
                ctx.wait(e, ctx.tok("evv"))
                for c in range(DC):
                    instr = e.dma_start(
                        dbg["dbg_s"][c * 128:(c + 1) * 128, t0:t0 + 512],
                        s_fm.ch(c))
                    ctx.inc("out", instr, 16)
        @blk.vector
        def _(e, t=t):
            ctx.wait(e, g_tok)
            ctx.inc("evv", e.nop())
        emit_combine(ctx, R, t)




def l1_prep(I, m):
    m["l1_ln1"] = _col_layout(I["L1_attn_ln1_w"], DC)
    m["l1_ln2"] = _col_layout(I["L1_attn_ln2_w"], DC)
    m["l1_winT"] = np.ascontiguousarray(np.asarray(I["L1_attn_w_in"]).T)
    m["l1_wgateT"] = np.ascontiguousarray(np.asarray(I["L1_attn_w_gate"]).T)
    m["l1_aparam"] = _col_layout(I["L1_attn_a_param"], 16)
    m["l1_waT"] = np.ascontiguousarray(np.asarray(I["L1_attn_w_a"]).T)
    m["l1_woutT"] = np.ascontiguousarray(np.asarray(I["L1_attn_w_out"]).T)
    m["l1_wmgT"] = np.ascontiguousarray(np.asarray(I["L1_attn_w_mlp_gate"]).T)
    m["l1_wmuT"] = np.ascontiguousarray(np.asarray(I["L1_attn_w_mlp_up"]).T)
    m["l1_wmdT"] = np.ascontiguousarray(np.asarray(I["L1_attn_w_mlp_down"]).T)
    m["l1_fwgT"] = np.ascontiguousarray(np.asarray(I["L1_ffn_w_gate"]).T)
    m["l1_fwuT"] = np.ascontiguousarray(np.asarray(I["L1_ffn_w_up"]).T)
    m["l1_fwdT"] = np.ascontiguousarray(np.asarray(I["L1_ffn_w_down"]).T)


def l1_declare(nc, d):
    def din(name, shape, dt=F32):
        d[name] = nc.dram_tensor(name, shape, dt, kind="ExternalInput").ap()
    din("l1_ln1", [128, DC]); din("l1_ln2", [128, DC])
    din("l1_winT", [1024, 2048]); din("l1_wgateT", [1024, 2048])
    din("l1_aparam", [128, 16]); din("l1_waT", [2048, 2048])
    din("l1_woutT", [2048, 1024])
    din("l1_wmgT", [1024, 4096]); din("l1_wmuT", [1024, 4096])
    din("l1_wmdT", [4096, 1024])
    din("l1_fwgT", [1024, 3072], I32); din("l1_fwuT", [1024, 3072], I32)
    din("l1_fwdT", [3072, 1024], I32)


def emit_l1(ctx, R, d, g_spill, s_spill, x_spill, dbg=None):
    blk = ctx.blk
    nc = ctx.nc
    x_fm = FM(R["x"], DC, T)
    xn_fm = FM(R["xn"], DC, T)

    smf = R["smf"]
    ln1 = smf[0:128, 0:8]
    ln2 = smf[0:128, 8:16]
    apar = smf[0:128, 16:32]
    state = smf[0:128, 32:48]

    @blk.sync
    def _(e):
        ctx.wait(e, ctx.tok("comb", "evv", "evs"))
        for src, dst in [("l1_ln1", ln1), ("l1_ln2", ln2), ("l1_aparam", apar)]:
            ctx.inc("ld", e.dma_start(dst, d[src][:]), 16)
    small_tok = ctx.tok("ld")

    emit_ln(ctx, R, x_fm, 0, T, xn_fm, 0, rms=False, in_f32=True)
    if dbg and "dbg_xn" in dbg:
        @blk.gpsimd
        def _(e):
            ctx.wait(e, ctx.tok("ln_done"))
            ctx.inc("out", e.dma_start(
                dbg["dbg_xn"].rearrange("(c p) t -> p c t", p=128),
                R["xn"][:].rearrange("p (c t) -> p c t", t=T)), 16)

    wbuf = R["arena_a"]
    xin_fm = FM(R["arena_d"], 16, 256)
    a_fm = FM(R["arena_d"], 16, 256, col0=4096)
    gt_fm = FM(R["arena_b"], 16, 256)
    y_fm = FM(R["arena_b"], 16, 256, col0=4096)
    apart_fm = FM(R["arena_c"], 16, 256)
    h1_fm = FM(R["xn2"], DC, 512)
    sq = R["sq"]

    @blk.gpsimd
    def _(e):
        ctx.wait(e, ctx.tok("comb", "ldg2"))
        ctx.inc("evp", e.memset(state, 0.0))

    for tm in range(8):
        t0 = tm * 256
        if tm % 2 == 0:
            emit_ln(ctx, R, xn_fm, t0, 512, h1_fm, 0, rms=True, eps=1e-6,
                    in_f32=False, gain_col=lambda c: ln1[:, c:c + 1])
        h1off = (tm % 2) * 256
        h1_tok = ctx.tok("ln_done")

        for wname, dst_fm, act in [("l1_winT", xin_fm, None),
                                   ("l1_wgateT", gt_fm, AF.Sigmoid)]:
            w_tok = load_w(ctx, d[wname], wbuf, pre=ctx.tok("mm"))
            jobs = []
            for mch in range(16):
                def mk_ev(dst_fm=dst_fm, act=act, mch=mch):
                    if act is None:
                        return lambda e, ps: e.tensor_copy(dst_fm.ch(mch), ps[:, 0:256])
                    return lambda e, ps: e.activation(dst_fm.ch(mch), ps[:, 0:256], act)
                jobs.append(dict(
                    pre=w_tok + h1_tok,
                    mms=mm_acc([(wbuf[:, c * 2048 + mch * 128:
                                      c * 2048 + (mch + 1) * 128],
                                 h1_fm.ch(c, h1off, 256)) for c in range(DC)]),
                    psum=lambda b: b[:, 0:256],
                    evac=mk_ev(),
                ))
            ev = "scalar" if act else "vector"
            sweep(ctx, jobs=jobs, banks=R["pb"][0:4], evac_eng=ev,
                  entry=ctx.tok("evs", "evv"))
        xin_tok = ctx.tok("evv", "evs")

        for kh in range(2):
            w_tok = load_w(ctx, d["l1_waT"][kh * 1024:(kh + 1) * 1024, :], wbuf,
                           pre=ctx.tok("mm"))
            jobs = []
            for mch in range(16):
                def mk_ev(kh=kh, mch=mch):
                    if kh == 0:
                        return lambda e, ps: e.tensor_copy(apart_fm.ch(mch), ps[:, 0:256])
                    return lambda e, ps: e.tensor_tensor(
                        apart_fm.ch(mch), apart_fm.ch(mch), ps[:, 0:256], ALU.add)
                jobs.append(dict(
                    pre=w_tok + xin_tok,
                    mms=mm_acc([(wbuf[:, c * 2048 + mch * 128:
                                      c * 2048 + (mch + 1) * 128],
                                 xin_fm.ch(c + kh * 8)) for c in range(DC)]),
                    psum=lambda b: b[:, 0:256],
                    evac=mk_ev(),
                ))
            sweep(ctx, jobs=jobs, banks=R["pb"][0:4], evac_eng="vector",
                  entry=ctx.tok("evs", "evv"))
        ap_tok = ctx.tok("evv")

        @blk.scalar
        def _(e, ap_tok=ap_tok):
            ctx.wait(e, ap_tok + small_tok)
            instr = None
            for c in range(16):
                instr = e.activation(a_fm.ch(c), apart_fm.ch(c), AF.Sigmoid,
                                     bias=apar[:, c:c + 1])
            ctx.inc("evs", instr)
        a_tok = ctx.tok("evs")

        for c in range(16):
            s1 = sq[:, (c % 4) * 512:(c % 4) * 512 + 256]
            s2 = sq[:, (c % 4) * 512 + 256:(c % 4) * 512 + 512]
            @blk.scalar
            def _(e, c=c, s1=s1, s2=s2, a_tok=a_tok):
                ctx.wait(e, a_tok + ctx.tok("scan"))
                e.activation(s1, a_fm.ch(c), AF.Square)
                instr = e.activation(s2, s1, AF.Sqrt, scale=-1.0,
                                     bias=cb(R, 1.0 + 1e-8))
                ctx.inc("evs", instr)
            @blk.vector
            def _(e, c=c, s2=s2):
                ctx.wait(e, ctx.tok("evs"))
                ctx.inc("evv", e.tensor_tensor(xin_fm.ch(c), s2, xin_fm.ch(c),
                                               ALU.mult))
                ctx.wait(e, ctx.tok("evv", "scan"))
                ctx.inc("scan", e.tensor_tensor_scan(
                    y_fm.ch(c), a_fm.ch(c), xin_fm.ch(c),
                    state[:, c:c + 1], ALU.mult, ALU.add))
                ctx.wait(e, ctx.tok("scan"))
                ctx.inc("scan", e.tensor_copy(state[:, c:c + 1],
                                              y_fm.ch(c, 255, 1)))
                ctx.inc("evv", e.tensor_tensor(y_fm.ch(c), y_fm.ch(c),
                                               gt_fm.ch(c), ALU.mult))
        scan_tok = ctx.tok("evv", "scan")

        w_tok = load_w(ctx, d["l1_woutT"], wbuf, pre=ctx.tok("mm"))
        xp_fm = FM(R["sbuf_s"], DC, 256)
        jobs = []
        for mch in range(DC):
            jobs.append(dict(
                pre=w_tok + scan_tok,
                mms=mm_acc([(wbuf[:, k * 1024 + mch * 128:
                                  k * 1024 + (mch + 1) * 128],
                             y_fm.ch(k)) for k in range(16)]),
                psum=lambda b: b[:, 0:256],
                evac=(lambda mch, t0=t0: lambda e, ps: e.tensor_tensor(
                    xp_fm.ch(mch), ps[:, 0:256], xn_fm.ch(mch, t0, 256),
                    ALU.add))(mch),
            ))
        sweep(ctx, jobs=jobs, banks=R["pb"][4:6], evac_eng="vector",
              entry=ctx.tok("evs", "evv"))
        @blk.gpsimd
        def _(e, t0=t0):
            ctx.wait(e, ctx.tok("evv"))
            for c in range(DC):
                ctx.inc("ldg2", e.dma_start(
                    x_spill[:, c * T + t0: c * T + t0 + 256], xp_fm.ch(c)), 16)

    xp_fm = FM(R["arena_d"], DC, 512)
    h2_fm = FM(R["xn2"], DC, 512)
    mg_fm = FM(R["arena_b"], 16, 512)
    acc_fm = FM(R["arena_c"], DC, 512)
    for t in range(TC):
        t0 = t * 512
        @blk.gpsimd
        def _(e, t0=t0):
            ctx.wait(e, ctx.tok("mm", "evv", "ldg2"))
            for c in range(DC):
                ctx.inc("ldg2", e.dma_start(
                    xp_fm.ch(c), x_spill[:, c * T + t0: c * T + t0 + 512]), 16)
        xp_tok = ctx.tok("ldg2")
        @blk.scalar
        def _(e, xp_tok=xp_tok):
            ctx.wait(e, xp_tok)
            ctx.inc("ln_done", e.nop())
        emit_ln(ctx, R, xp_fm, 0, 512, h2_fm, 0, rms=True, eps=1e-6,
                in_f32=False, gain_col=lambda c: ln2[:, c:c + 1])
        h2_tok = ctx.tok("ln_done")

        for fh in range(2):
            w_tok = load_w(ctx, d["l1_wmgT"][:, fh * 2048:(fh + 1) * 2048],
                           wbuf, pre=ctx.tok("mm"))
            jobs = []
            for mch in range(16):
                jobs.append(dict(
                    pre=w_tok + h2_tok,
                    mms=mm_acc([(wbuf[:, c * 2048 + mch * 128:
                                      c * 2048 + (mch + 1) * 128],
                                 h2_fm.ch(c)) for c in range(DC)]),
                    psum=lambda b: b[:, :],
                    evac=(lambda mch: lambda e, ps: e.activation(
                        mg_fm.ch(mch), ps, AF.Silu))(mch),
                ))
            sweep(ctx, jobs=jobs, banks=R["pb"][0:4], evac_eng="scalar",
                  entry=ctx.tok("evs", "evv"))
            w_tok = load_w(ctx, d["l1_wmuT"][:, fh * 2048:(fh + 1) * 2048],
                           wbuf, pre=ctx.tok("mm"))
            sg_tok = ctx.tok("evs")
            jobs = []
            for mch in range(16):
                jobs.append(dict(
                    pre=w_tok + sg_tok,
                    mms=mm_acc([(wbuf[:, c * 2048 + mch * 128:
                                      c * 2048 + (mch + 1) * 128],
                                 h2_fm.ch(c)) for c in range(DC)]),
                    psum=lambda b: b[:, :],
                    evac=(lambda mch: lambda e, ps: e.tensor_tensor(
                        mg_fm.ch(mch), ps, mg_fm.ch(mch), ALU.mult))(mch),
                ))
            sweep(ctx, jobs=jobs, banks=R["pb"][0:4], evac_eng="vector",
                  entry=ctx.tok("evs", "evv"))
            mm_tok = ctx.tok("evv")
            w_tok = load_w(ctx, d["l1_wmdT"][fh * 2048:(fh + 1) * 2048, :],
                           wbuf, pre=ctx.tok("mm"))
            jobs = []
            for mch in range(DC):
                def mk_ev(fh=fh, mch=mch):
                    if fh == 0:
                        return lambda e, ps: e.tensor_copy(acc_fm.ch(mch), ps)
                    return lambda e, ps: e.tensor_tensor(
                        acc_fm.ch(mch), acc_fm.ch(mch), ps, ALU.add)
                jobs.append(dict(
                    pre=w_tok + mm_tok,
                    mms=mm_acc([(wbuf[:, k * 1024 + mch * 128:
                                      k * 1024 + (mch + 1) * 128],
                                 mg_fm.ch(k)) for k in range(16)]),
                    psum=lambda b: b[:, :],
                    evac=mk_ev(),
                ))
            sweep(ctx, jobs=jobs, banks=R["pb"][0:4], evac_eng="vector",
                  entry=ctx.tok("evs", "evv"))
        s_tok = ctx.tok("evv")
        @blk.vector
        def _(e, t0=t0, s_tok=s_tok):
            ctx.wait(e, s_tok)
            instr = None
            for c in range(DC):
                instr = e.tensor_tensor(h2_fm.ch(c), acc_fm.ch(c), xp_fm.ch(c),
                                        ALU.add)
            ctx.inc("evv", instr)
        @blk.gpsimd
        def _(e, t0=t0):
            ctx.wait(e, ctx.tok("evv"))
            for c in range(DC):
                ctx.inc("ldg2", e.dma_start(
                    s_spill[:, c * T + t0: c * T + t0 + 512], h2_fm.ch(c)), 16)

    if dbg and "dbg_s" in dbg:
        @blk.gpsimd
        def _(e):
            ctx.wait(e, ctx.tok("ldg2"))
            ctx.inc("out", e.dma_start(
                dbg["dbg_s"].rearrange("(c p) t -> p c t", p=128),
                s_spill[:].rearrange("p (c t) -> p c t", t=T)), 16)

    sg_fm = FM(R["arena_d"], 12, 512)
    hh_fm = FM(R["arena_b"], 12, 512)
    for t in range(TC):
        t0 = t * 512
        for fh in range(2):
            f0 = fh * 1536
            w_tok = load_w(ctx, d["l1_fwgT"][:, f0:f0 + 1536], wbuf,
                           pre=ctx.tok("mm"))
            jobs = []
            for mch in range(12):
                jobs.append(dict(
                    pre=w_tok,
                    mms=mm_acc([(wbuf[:, c * 1536 + mch * 128:
                                      c * 1536 + (mch + 1) * 128],
                                 xn_fm.ch(c, t0, 512)) for c in range(DC)]),
                    psum=lambda b: b[:, :],
                    evac=(lambda mch: lambda e, ps: e.activation(
                        sg_fm.ch(mch), ps, AF.Silu, scale=1.0 / 127.0))(mch),
                ))
            sweep(ctx, jobs=jobs, banks=R["pb"][0:4], evac_eng="scalar",
                  entry=ctx.tok("evs", "evv"))
            w_tok = load_w(ctx, d["l1_fwuT"][:, f0:f0 + 1536], wbuf,
                           pre=ctx.tok("mm"))
            sg_tok = ctx.tok("evs")
            jobs = []
            for mch in range(12):
                jobs.append(dict(
                    pre=w_tok + sg_tok,
                    mms=mm_acc([(wbuf[:, c * 1536 + mch * 128:
                                      c * 1536 + (mch + 1) * 128],
                                 xn_fm.ch(c, t0, 512)) for c in range(DC)]),
                    psum=lambda b: b[:, :],
                    evac=(lambda mch: lambda e, ps: e.tensor_tensor(
                        hh_fm.ch(mch), ps, sg_fm.ch(mch), ALU.mult))(mch),
                ))
            sweep(ctx, jobs=jobs, banks=R["pb"][0:4], evac_eng="vector",
                  entry=ctx.tok("evs", "evv"))
            hh_tok = ctx.tok("evv")
            w_tok = load_w(ctx, d["l1_fwdT"][f0:f0 + 1536, :], wbuf,
                           pre=ctx.tok("mm"))
            jobs = []
            for mch in range(DC):
                def mk_ev(fh=fh, mch=mch):
                    if fh == 0:
                        return lambda e, ps: e.tensor_copy(acc_fm.ch(mch), ps)
                    return lambda e, ps: e.tensor_tensor(
                        acc_fm.ch(mch), acc_fm.ch(mch), ps, ALU.add)
                jobs.append(dict(
                    pre=w_tok + hh_tok,
                    mms=mm_acc([(wbuf[:, k * 1024 + mch * 128:
                                      k * 1024 + (mch + 1) * 128],
                                 hh_fm.ch(k)) for k in range(12)]),
                    psum=lambda b: b[:, :],
                    evac=mk_ev(),
                ))
            sweep(ctx, jobs=jobs, banks=R["pb"][0:4], evac_eng="vector",
                  entry=ctx.tok("evs", "evv"))
        y_tok = ctx.tok("evv")
        gg_fm = FM(R["xn2"], DC, 512)
        @blk.scalar
        def _(e, y_tok=y_tok):
            ctx.wait(e, y_tok + ctx.tok("ldg2"))
            instr = None
            for c in range(DC):
                instr = e.activation(gg_fm.ch(c), acc_fm.ch(c), AF.Silu,
                                     scale=1.0 / (127.0 * 127.0))
            ctx.inc("evs", instr)
        @blk.gpsimd
        def _(e, t0=t0):
            ctx.wait(e, ctx.tok("evs"))
            for c in range(DC):
                ctx.inc("ldg2", e.dma_start(
                    g_spill[:, c * T + t0: c * T + t0 + 512], gg_fm.ch(c)), 16)

    if dbg and "dbg_g" in dbg:
        @blk.gpsimd
        def _(e):
            ctx.wait(e, ctx.tok("ldg2"))
            ctx.inc("out", e.dma_start(
                dbg["dbg_g"].rearrange("(c p) t -> p c t", p=128),
                g_spill[:].rearrange("p (c t) -> p c t", t=T)), 16)

    emit_combine_spill(ctx, R, g_spill, s_spill)




def l2_prep(I, m, rank):
    import ml_dtypes
    m["l2_wkv"] = np.ascontiguousarray(I["L2_attn_W_kv"])
    m["l2_gkv"] = _col_layout(I["L2_attn_g_kv"], 2)
    m["l2_wkdec"] = np.ascontiguousarray(I["L2_attn_W_kdec"])
    m["l2_wvdec"] = np.ascontiguousarray(I["L2_attn_W_vdec"])
    m["l2_wq"] = np.ascontiguousarray(I["L2_attn_W_q"])
    m["l2_wqdec"] = np.ascontiguousarray(I["L2_attn_W_qdec"])
    m["l2_wkrope"] = np.ascontiguousarray(I["L2_attn_W_krope"])
    m["l2_wqrope"] = np.ascontiguousarray(I["L2_attn_W_qrope"])
    m["l2_wo"] = np.ascontiguousarray(I["L2_attn_W_o"])
    for nm, key in [("wf", "L2_ffn_wf"), ("uf", "L2_ffn_uf"),
                    ("wi", "L2_ffn_wi"), ("ui", "L2_ffn_ui"),
                    ("wc", "L2_ffn_wc"), ("uc", "L2_ffn_uc"),
                    ("wo2", "L2_ffn_wo"), ("uo", "L2_ffn_uo")]:
        m["l2_" + nm] = np.ascontiguousarray(np.asarray(I[key]).T)
    m["l2_bf"] = _col_layout(I["L2_ffn_bf"], DC)
    m["l2_bi"] = _col_layout(I["L2_ffn_bi"], DC)
    m["l2_bc"] = _col_layout(I["L2_ffn_bc"], DC)
    m["l2_bo"] = _col_layout(I["L2_ffn_bo"], DC)
    m["l2_inc"] = _col_layout(I["L2_ffn_in_c"], DC)
    inv_freq = 1.0 / 10000.0 ** (np.arange(0, 32, 2, dtype=np.float32) / 32)
    t = np.arange(T, dtype=np.float32)
    freqs = np.outer(t, inv_freq)
    emb = np.concatenate([freqs, freqs], -1)
    cos = np.cos(emb).T
    sin = np.sin(emb).T
    m["l2_cos"] = np.ascontiguousarray(np.tile(cos, (4, 1))).astype(ml_dtypes.bfloat16)
    m["l2_sin"] = np.ascontiguousarray(np.tile(sin, (4, 1))).astype(ml_dtypes.bfloat16)
    sign = np.where(np.arange(128) % 32 < 16, -1.0, 1.0).astype(np.float32)
    m["l2_sign"] = sign.reshape(128, 1)
    p = np.arange(128, dtype=np.int32)[:, None]
    g1 = np.zeros((128, 64), np.int32)
    g2 = np.zeros((128, 64), np.int32)
    for t8 in range(8):
        for ch in range(8):
            g1[:, t8 * 8 + ch] = ((t8 * 128 + p[:, 0]) * 64 + ch * 8 + rank)
            g2[:, t8 * 8 + ch] = ((t8 * 128 + p[:, 0]) * 64 + rank * 8 + ch)
    m["l2_gidx"] = g1
    m["l2_gidx2"] = g2


def l2_declare(nc, d):
    def din(name, shape, dt=F32):
        d[name] = nc.dram_tensor(name, shape, dt, kind="ExternalInput").ap()
    din("l2_wkv", [1024, 256]); din("l2_gkv", [128, 2])
    din("l2_wkdec", [256, 1024]); din("l2_wvdec", [256, 1024])
    din("l2_wq", [1024, 512]); din("l2_wqdec", [512, 1024])
    din("l2_wkrope", [1024, 512]); din("l2_wqrope", [512, 512])
    din("l2_wo", [1024, 1024])
    for nm in ["wf", "uf", "wi", "ui", "wc", "uc", "wo2", "uo"]:
        din("l2_" + nm, [1024, 1024])
    for nm in ["bf", "bi", "bc", "bo", "inc"]:
        din("l2_" + nm, [128, DC])
    din("l2_cos", [128, T], BF16); din("l2_sin", [128, T], BF16)
    din("l2_sign", [128, 1])
    din("l2_gidx", [128, 64], I32); din("l2_gidx2", [128, 64], I32)


def emit_l2(ctx, R, d, g_spill, s_spill, dbg=None):
    from concourse.bass import IndirectOffsetOnAxis
    blk = ctx.blk
    nc = ctx.nc
    x_fm = FM(R["x"], DC, T)
    xn_fm = FM(R["xn"], DC, T)

    smf, smb = R["smf"], R["smb"]
    gkv = smf[0:128, 0:2]
    sign_t = smf[0:128, 2:3]
    bias_t = smf[0:128, 3:35]
    inc_t = smf[0:128, 35:43]
    cos_w = smb[0:64, 0:512]
    sin_w = smb[0:64, 512:1024]
    gidx = smb.bitcast(I32)[0:128, 512:576]

    cc_in = nc.dram_tensor("l2_ccin", [128, DC * T], BF16, kind="Internal").ap()
    cc_out = nc.dram_tensor("l2_ccout", [128 * 8, DC * T], BF16, kind="Internal").ap()
    cc2_in = nc.dram_tensor("l2_cc2in", [128, DC * T], BF16, kind="Internal").ap()
    cc2_out = nc.dram_tensor("l2_cc2out", [128 * 8, DC * T], BF16, kind="Internal").ap()

    @blk.sync
    def _(e):
        ctx.wait(e, ctx.tok("comb", "evv", "evs"))
        for src, dst in [("l2_gkv", gkv), ("l2_sign", sign_t),
                         ("l2_gidx", gidx), ("l2_inc", inc_t)]:
            ctx.inc("ld", e.dma_start(dst, d[src][:]), 16)
        for i, nm in enumerate(["l2_bf", "l2_bi", "l2_bc", "l2_bo"]):
            ctx.inc("ld", e.dma_start(bias_t[0:128, i * DC:(i + 1) * DC], d[nm][:]), 16)
    small_tok = ctx.tok("ld")

    emit_ln(ctx, R, x_fm, 0, T, xn_fm, 0, rms=False, in_f32=True)
    ln_tok = ctx.tok("ln_done")
    if dbg and "dbg_xn" in dbg:
        @blk.gpsimd
        def _(e):
            ctx.wait(e, ctx.tok("ln_done"))
            ctx.inc("out", e.dma_start(
                dbg["dbg_xn"].rearrange("(c p) t -> p c t", p=128),
                R["xn"][:].rearrange("p (c t) -> p c t", t=T)), 16)

    @blk.gpsimd
    def _(e):
        ctx.wait(e, ctx.tok("ln_done"))
        ctx.inc("ldg2", e.dma_start(cc_in[:], R["xn"][:]), 16)
        ctx.wait(e, ctx.tok("ldg2"))
        ctx.inc("cc", e.collective_compute(
            "AllGather", ALU.bypass, replica_groups=[list(range(NCORES))],
            ins=[cc_in.opt()], outs=[cc_out.opt()]))
    ag1_tok = ctx.tok("cc")

    kvf = FM(R["arena_c"], 2, T)
    kvn = FM(R["xn2"], 2, T)
    wsm = R["sq"].bitcast(BF16)
    w_tok = load_w(ctx, d["l2_wkv"], wsm, col0=0, pre=ctx.tok("mm", "statmm"))
    jobs = []
    for mch in range(2):
        for tb in range(4):
            jobs.append(dict(
                pre=w_tok + ln_tok,
                mms=mm_acc([(wsm[:, c * 256 + mch * 128: c * 256 + (mch + 1) * 128],
                             xn_fm.ch(c, tb * 512, 512)) for c in range(DC)]),
                psum=lambda b: b[:, :],
                evac=(lambda mch, tb: lambda e, ps: e.tensor_copy(
                    kvf.ch(mch, tb * 512, 512), ps))(mch, tb),
            ))
    sweep(ctx, jobs=jobs, banks=R["pb"][0:2], evac_eng="vector",
          entry=ctx.tok("evs", "evv"))
    @blk.scalar
    def _(e):
        ctx.wait(e, ctx.tok("evv"))
        ctx.inc("ln_done", e.nop())
    emit_ln(ctx, R, kvf, 0, T, kvn, 0, rms=True, eps=1e-6, in_f32=True, nch=2,
            gain_col=lambda c: gkv[:, c:c + 1])
    kvn_tok = ctx.tok("ln_done")

    qc = FM(R["arena_b"], 4, T)
    wqb = R["arena_d"]
    w_tok = load_w(ctx, d["l2_wq"], wqb, col0=0, pre=ctx.tok("mm"))
    jobs = []
    for mch in range(4):
        for tb in range(4):
            jobs.append(dict(
                pre=w_tok + ln_tok,
                mms=mm_acc([(wqb[:, c * 512 + mch * 128:
                                 c * 512 + (mch + 1) * 128],
                             xn_fm.ch(c, tb * 512, 512)) for c in range(DC)]),
                psum=lambda b: b[:, :],
                evac=(lambda mch, tb: lambda e, ps: e.tensor_copy(
                    qc.ch(mch, tb * 512, 512), ps))(mch, tb),
            ))
    sweep(ctx, jobs=jobs, banks=R["pb"][0:2], evac_eng="vector",
          entry=ctx.tok("evs", "evv"))
    qc_tok = ctx.tok("evv")

    ad = R["arena_d"]
    KO, QO, VO, AO = 0, 2048, 4096, 6144
    o_all = FM(R["arena_a"], DC, T)
    attnT = R["sq"].bitcast(BF16)
    rr_t = R["sq"].bitcast(BF16)[0:64, 0:512]
    sc_row = R["arena_c"]
    acbf = R["arena_c"].bitcast(BF16)
    kr_t = acbf[0:64, 4096:6144]
    qr_t = acbf[0:64, 6144:8192]

    for hg in range(8):
        wq_tok = ctx.tok("ldg")
        load_w(ctx, d["l2_wkdec"][:, hg * 128:(hg + 1) * 128], wsm, col0=0,
               pre=ctx.tok("mm", "statmm", "out"))
        load_w(ctx, d["l2_wvdec"][:, hg * 128:(hg + 1) * 128], wsm, col0=256)
        load_w(ctx, d["l2_wqdec"][:, hg * 128:(hg + 1) * 128], wsm, col0=512)
        load_w(ctx, d["l2_wkrope"][:, hg * 64:(hg + 1) * 64], wsm, col0=1024)
        load_w(ctx, d["l2_wqrope"][:, hg * 64:(hg + 1) * 64], wsm, col0=1536)
        w_tok = ctx.tok("ldg")

        jobs = []
        for tb in range(4):
            jobs.append(dict(
                pre=w_tok + kvn_tok,
                mms=mm_acc([(wsm[:, 0 + c * 128:0 + (c + 1) * 128],
                             kvn.ch(c, tb * 512, 512)) for c in range(2)]),
                psum=lambda b: b[:, :],
                evac=(lambda tb: lambda e, ps: e.tensor_copy(
                    ad[:, KO + tb * 512: KO + (tb + 1) * 512], ps))(tb),
            ))
        for tb in range(4):
            jobs.append(dict(
                pre=w_tok + qc_tok,
                mms=mm_acc([(wsm[:, 512 + c * 128: 512 + (c + 1) * 128],
                             qc.ch(c, tb * 512, 512)) for c in range(4)]),
                psum=lambda b: b[:, :],
                evac=(lambda tb: lambda e, ps: e.tensor_copy(
                    ad[:, QO + tb * 512: QO + (tb + 1) * 512], ps))(tb),
            ))
        for ts_ in range(16):
            jobs.append(dict(
                pre=w_tok + kvn_tok,
                mms=mm_acc([(kvn.ch(c, ts_ * 128, 128),
                             wsm[:, 256 + c * 128: 256 + (c + 1) * 128])
                            for c in range(2)]),
                psum=lambda b: b[:, 0:128],
                evac=(lambda ts_: lambda e, ps: e.tensor_copy(
                    ad[:, VO + ts_ * 128: VO + (ts_ + 1) * 128], ps))(ts_),
            ))
        for tb in range(4):
            jobs.append(dict(
                pre=w_tok + ln_tok,
                mms=mm_acc([(wsm[:, 1024 + c * 64: 1024 + (c + 1) * 64],
                             xn_fm.ch(c, tb * 512, 512)) for c in range(DC)]),
                psum=lambda b: b[0:64, :],
                evac=(lambda tb: lambda e, ps: e.tensor_copy(
                    acbf[0:64, 4096 + tb * 512: 4096 + (tb + 1) * 512], ps))(tb),
            ))
        for tb in range(4):
            jobs.append(dict(
                pre=w_tok + qc_tok,
                mms=mm_acc([(wsm[:, 1536 + c * 64: 1536 + (c + 1) * 64],
                             qc.ch(c, tb * 512, 512)) for c in range(4)]),
                psum=lambda b: b[0:64, :],
                evac=(lambda tb: lambda e, ps: e.tensor_copy(
                    acbf[0:64, 6144 + tb * 512: 6144 + (tb + 1) * 512], ps))(tb),
            ))
        sweep(ctx, jobs=jobs, banks=R["pb"][0:3], evac_eng="vector",
              entry=ctx.tok("evs", "evv"))
        prj_tok = ctx.tok("evv")

        if dbg and "dbg_att" in dbg and hg == 0:
            @blk.gpsimd
            def _(e, prj_tok=prj_tok):
                ctx.wait(e, prj_tok)
                ctx.inc("out", e.dma_start(
                    dbg["dbg_att"][0:64, 7168:8192],
                    R["arena_c"].bitcast(BF16)[0:64, 6144:7168]), 16)
            @blk.vector
            def _(e):
                ctx.wait(e, ctx.tok("out"))
                ctx.inc("evv", e.nop())
        mask = [(i + 16) % 32 for i in range(32)]
        for tb in range(4):
            w0 = tb * 512
            @blk.sync
            def _(e, w0=w0):
                ctx.wait(e, ctx.tok("evv"))
                ctx.inc("ld", e.dma_start(cos_w, d["l2_cos"][0:64, w0:w0 + 512]), 16)
                ctx.inc("ld", e.dma_start(sin_w, d["l2_sin"][0:64, w0:w0 + 512]), 16)
            cs_tok = ctx.tok("ld")
            @blk.vector
            def _(e, w0=w0, cs_tok=cs_tok, prj_tok=prj_tok):
                ctx.wait(e, prj_tok + small_tok + cs_tok)
                for ro in (4096, 6144):
                    rt = acbf[0:64, ro + w0: ro + w0 + 512]
                    ctx.inc("evv", e.stream_shuffle(rr_t, rt, mask))
                    ctx.wait(e, ctx.tok("evv"))
                    ctx.inc("evv", e.tensor_tensor(rt, rt, cos_w, ALU.mult))
                    ctx.inc("evv", e.scalar_tensor_tensor(
                        rr_t, rr_t, sign_t[0:64, 0:1], sin_w[:],
                        ALU.mult, ALU.mult))
                    ctx.wait(e, ctx.tok("evv"))
                    ctx.inc("evv", e.tensor_tensor(rt, rt, rr_t, ALU.add))
                    ctx.wait(e, ctx.tok("evv"))
        rope_tok = ctx.tok("evv")
        if dbg and "dbg_att" in dbg and hg == 0:
            pass
            @blk.gpsimd
            def _(e):
                ctx.wait(e, rope_tok + prj_tok)
                ctx.inc("out", e.dma_start(dbg["dbg_att"][0:128, 0:2048], ad[:, 0:2048]), 16)
                ctx.inc("out", e.dma_start(dbg["dbg_att"][0:128, 2048:4096], ad[:, 2048:4096]), 16)
                ctx.inc("out", e.dma_start(dbg["dbg_att"][0:64, 4096:6144],
                                           R["arena_c"].bitcast(BF16)[0:64, 4096:6144]), 16)
                ctx.inc("out", e.dma_start(dbg["dbg_att"][64:128, 4096:6144],
                                           R["arena_c"].bitcast(BF16)[0:64, 6144:8192]), 16)
                ctx.inc("out", e.dma_start(dbg["dbg_att"][0:128, 6144:7168], ad[:, 4096:5120]), 16)


        for h2 in range(2):
            base = 64 * h2
            rbase = 32 * h2
            for tq in range(8):
                for qs in range(2):
                    q0 = tq * 256 + qs * 128
                    L = q0 + 128
                    sjobs = []
                    nblk = (L + 511) // 512
                    for kb in range(nblk):
                        k0 = kb * 512
                        w = min(512, L - k0)
                        has_diag = (k0 + w == L)
                        def mk_evac(k0=k0, w=w, has_diag=has_diag, L=L):
                            def f(e, ps):
                                sc = 1.0 / np.sqrt(96.0)
                                if has_diag:
                                    if w > 128:
                                        e.tensor_single_scalar(
                                            sc_row[:, k0:k0 + w - 128],
                                            ps[:, 0:w - 128], sc, ALU.mult)
                                    return e.scalar_tensor_tensor(
                                        sc_row[:, L - 128:L], ps[:, w - 128:w],
                                        sc, R["causal_t"][:], ALU.mult, ALU.add)
                                return e.tensor_single_scalar(
                                    sc_row[:, k0:k0 + w], ps[:, 0:w], sc, ALU.mult)
                            return f
                        def mk_mms(k0=k0, w=w, q0=q0, base=base, rbase=rbase):
                            def m1(e, ps):
                                return e.matmul(
                                    ps[:, 0:w],
                                    ad[base:base + 64, QO + q0:QO + q0 + 128],
                                    ad[base:base + 64, KO + k0:KO + k0 + w],
                                    start=True, stop=DBG_NOROPE)
                            def m2(e, ps):
                                if DBG_NOROPE:
                                    return None
                                return e.matmul(
                                    ps[:, 0:w],
                                    acbf[rbase:rbase + 32, 6144 + q0: 6144 + q0 + 128],
                                    acbf[rbase:rbase + 32, 4096 + k0: 4096 + k0 + w],
                                    start=DBG_ROPEONLY, stop=True)
                            return [m1, m2]
                        mmsl = mk_mms()
                        if DBG_NOROPE:
                            mmsl = mmsl[:1]
                        elif DBG_ROPEONLY:
                            mmsl = mmsl[1:]
                        sjobs.append(dict(
                            pre=rope_tok,
                            mms=mmsl,
                            psum=lambda b: b,
                            evac=mk_evac(),
                        ))
                    sweep(ctx, jobs=sjobs, banks=R["pb"][0:2], evac_eng="vector",
                          entry=ctx.tok("evs", "evv"))
                    sc_tok = ctx.tok("evv")

                    @blk.vector
                    def _(e, L=L):
                        ctx.wait(e, sc_tok)
                        instr = e.tensor_reduce(R["cols"][:, 0:1],
                                                sc_row[:, 0:L], AX.X, ALU.max,
                                                negate=True)
                        ctx.inc("evv", instr)
                    @blk.scalar
                    def _(e, L=L):
                        ctx.wait(e, ctx.tok("evv"))
                        instr = e.activation(ad[:, AO:AO + L], sc_row[:, 0:L],
                                             AF.Exp, bias=R["cols"][:, 0:1])
                        ctx.inc("evs", instr)
                    @blk.vector
                    def _(e, L=L):
                        ctx.wait(e, ctx.tok("evs"))
                        ctx.inc("evv", e.tensor_reduce(
                            R["cols"][:, 1:2], ad[:, AO:AO + L], AX.X, ALU.add))
                        ctx.wait(e, ctx.tok("evv"))
                        ctx.inc("evv", e.reciprocal(R["cols"][:, 2:3],
                                                    R["cols"][:, 1:2]))
                        ctx.wait(e, ctx.tok("evv"))
                        instr = e.tensor_scalar_mul(ad[:, AO:AO + L],
                                                    ad[:, AO:AO + L],
                                                    R["cols"][:, 2:3])
                        ctx.inc("evv", instr)
                    sm_tok = ctx.tok("evv")

                    tjobs = []
                    for tkc in range(L // 128):
                        tjobs.append(dict(
                            pre=sm_tok,
                            mms=[(lambda tkc: lambda e, ps: e.transpose(
                                ps, ad[:, AO + tkc * 128:AO + (tkc + 1) * 128],
                                R["iden_b"][:]))(tkc)],
                            psum=lambda b: b[:, 0:64].bitcast(BF16),
                            evac=(lambda tkc, qs=qs: lambda e, ps: e.tensor_copy(
                                attnT[:, tkc * 256 + qs * 128:
                                      tkc * 256 + qs * 128 + 128], ps))(tkc),
                        ))
                    sweep(ctx, jobs=tjobs, banks=R["pb"][2:4], evac_eng="vector",
                          entry=ctx.tok("evs", "evv"))

                at_tok = ctx.tok("evv")
                nk = 2 * (tq + 1)
                avmms = []
                for tkc in range(nk):
                    noff = max(0, (tkc - 2 * tq)) * 128
                    def mk(tkc=tkc, noff=noff, first=(tkc == 0),
                           last=(tkc == nk - 1), base=base):
                        def f(e, ps):
                            return e.matmul(
                                ps[0:64, noff:256],
                                ad[:, VO + tkc * 128 + base:
                                   VO + tkc * 128 + base + 64],
                                attnT[:, tkc * 256 + noff: (tkc + 1) * 256],
                                start=first, stop=last)
                        return f
                    avmms.append(mk())
                avjobs = [dict(
                    pre=at_tok,
                    mms=avmms,
                    psum=lambda b: b,
                    evac=(lambda hg=hg, h2=h2, tq=tq: lambda e, ps: e.tensor_copy(
                        o_all.ch(hg, tq * 256, 256, p0=64 * h2, pw=64),
                        ps[0:64, 0:256]))(),
                )]
                sweep(ctx, jobs=avjobs, banks=[R["pb"][4]], evac_eng="vector",
                      entry=ctx.tok("evs", "evv"))

    if dbg and "dbg_o" in dbg:
        @blk.gpsimd
        def _(e):
            ctx.wait(e, ctx.tok("evv"))
            ctx.inc("out", e.dma_start(
                dbg["dbg_o"].rearrange("(c p) t -> p c t", p=128),
                R["arena_a"][:].rearrange("p (c t) -> p c t", t=T)), 16)
    wo_t = R["arena_d"]
    wo_tok = load_w(ctx, d["l2_wo"], wo_t, pre=ctx.tok("mm"))
    s_fm = FM(R["xn2"], DC, 512)
    for t in range(TC):
        t0 = t * 512
        jobs = []
        for mch in range(DC):
            jobs.append(dict(
                pre=wo_tok + ctx.tok("evv"),
                mms=mm_acc([(wo_t[:, c * 1024 + mch * 128:
                                  c * 1024 + (mch + 1) * 128],
                             o_all.ch(c, t0, 512)) for c in range(DC)]),
                psum=lambda b: b[:, :],
                evac=(lambda mch: lambda e, ps: e.tensor_copy(
                    s_fm.ch(mch), ps))(mch),
            ))
        sweep(ctx, jobs=jobs, banks=R["pb"][5:7], evac_eng="vector",
              entry=ctx.tok("evs", "evv"))
        @blk.gpsimd
        def _(e, t0=t0):
            ctx.wait(e, ctx.tok("evv"))
            for c in range(DC):
                ctx.inc("ldg2", e.dma_start(
                    s_spill[:, c * T + t0: c * T + t0 + 512], s_fm.ch(c)), 16)
    if dbg and "dbg_s" in dbg:
        @blk.gpsimd
        def _(e):
            ctx.wait(e, ctx.tok("ldg2"))
            ctx.inc("out", e.dma_start(
                dbg["dbg_s"].rearrange("(c p) t -> p c t", p=128),
                s_spill[:].rearrange("p (c t) -> p c t", t=T)), 16)

    from concourse.bass import IndirectOffsetOnAxis as IOA
    xa = R["arena_a"]
    @blk.gpsimd
    def _(e):
        ctx.wait(e, ag1_tok + ctx.tok("mm", "evv"))
        src3 = cc_out.rearrange("q (s j) -> q s j", j=256)
        for t8 in range(8):
            for chh in range(DC):
                col = t8 * 8 + chh
                ctx.inc("ldg2", e.indirect_dma_start(
                    xa[:, col * 256:(col + 1) * 256],
                    None, src3, IOA(gidx[0:128, col:col + 1], 1)), 16)
    xa_tok = ctx.tok("ldg2")

    cst = FM(R["arena_c"], DC, 256)
    gp_f = FM(R["arena_c"], DC, 256, col0=2048)
    gp_i = FM(R["sq"], DC, 256)
    fb_fm = FM(R["xn2"], DC, 256)
    ib_fm = FM(R["xn2"], DC, 256, col0=2048)
    gb_fm = FM(R["arena_d"], DC, 256)
    tmp_fm = FM(R["arena_d"], DC, 256, col0=2048)
    cbf_fm = FM(R["arena_d"], DC, 256, col0=4096)
    wlb = R["arena_b"]

    @blk.gpsimd
    def _(e):
        ctx.wait(e, ctx.tok("evv", "mm"))
        ctx.inc("evp", e.memset(R["arena_c"][:, 0:2048], 0.0))
    @blk.vector
    def _(e):
        ctx.wait(e, small_tok + ctx.tok("evp"))
        instr = None
        for c in range(DC):
            e.tensor_scalar_add(cst.ch(c), cst.ch(c), inc_t[:, c:c + 1])
            instr = e.tensor_copy(cbf_fm.ch(c), cst.ch(c))
        ctx.inc("evs", instr)

    for t8 in range(8):
        for gi_, (wn, un, afn, dst) in enumerate([
                ("l2_wf", "l2_uf", AF.Sigmoid, fb_fm),
                ("l2_wi", "l2_ui", AF.Sigmoid, ib_fm),
                ("l2_wc", "l2_uc", AF.Tanh, gb_fm)]):
            for part, wname in [(0, wn), (1, un)]:
                w_tok = load_w(ctx, d[wname], wlb, pre=ctx.tok("mm"))
                jobs = []
                for mch in range(DC):
                    def mk_ev(part=part, mch=mch):
                        if part == 0:
                            return lambda e, ps: e.tensor_copy(gp_i.ch(mch), ps[:, 0:256])
                        return lambda e, ps: e.tensor_tensor(
                            gp_i.ch(mch), gp_i.ch(mch), ps[:, 0:256], ALU.add)
                    rhs_fm = (lambda c, t8=t8: xa[:, (t8 * 8 + c) * 256:
                                                  (t8 * 8 + c + 1) * 256])                         if part == 0 else (lambda c: cbf_fm.ch(c))
                    jobs.append(dict(
                        pre=w_tok + xa_tok + ctx.tok("evs"),
                        mms=mm_acc([(wlb[:, c * 1024 + mch * 128:
                                         c * 1024 + (mch + 1) * 128],
                                     rhs_fm(c)) for c in range(DC)]),
                        psum=lambda b: b[:, 0:256],
                        evac=mk_ev(),
                    ))
                sweep(ctx, jobs=jobs, banks=R["pb"][0:4], evac_eng="vector",
                      entry=ctx.tok("evs", "evv"))
            gp_tok = ctx.tok("evv")
            @blk.scalar
            def _(e, gp_tok=gp_tok, gi_=gi_, afn=afn, dst=dst):
                ctx.wait(e, gp_tok + small_tok)
                instr = None
                for c in range(DC):
                    instr = e.activation(dst.ch(c), gp_i.ch(c), afn,
                                         bias=bias_t[:, gi_ * DC + c:gi_ * DC + c + 1])
                ctx.inc("evs", instr)
        act_tok = ctx.tok("evs")
        @blk.vector
        def _(e, act_tok=act_tok):
            ctx.wait(e, act_tok)
            for c in range(DC):
                ctx.inc("evv", e.tensor_tensor(tmp_fm.ch(c), ib_fm.ch(c),
                                               gb_fm.ch(c), ALU.mult))
                ctx.inc("evv", e.tensor_tensor(cst.ch(c), fb_fm.ch(c),
                                               cst.ch(c), ALU.mult))
                ctx.wait(e, ctx.tok("evv"))
                ctx.inc("evv", e.tensor_tensor(cst.ch(c), cst.ch(c),
                                               tmp_fm.ch(c), ALU.add))
                ctx.wait(e, ctx.tok("evv"))
                ctx.inc("evv", e.tensor_copy(cbf_fm.ch(c), cst.ch(c)))
                ctx.wait(e, ctx.tok("evv"))
            ctx.inc("evv", e.nop())

    cfin_tok = ctx.tok("evv")
    w_tok = load_w(ctx, d["l2_uo"], wlb, pre=ctx.tok("mm"))
    jobs = []
    for mch in range(DC):
        jobs.append(dict(
            pre=w_tok + cfin_tok,
            mms=mm_acc([(wlb[:, c * 1024 + mch * 128:
                             c * 1024 + (mch + 1) * 128],
                         cbf_fm.ch(c)) for c in range(DC)]),
            psum=lambda b: b[:, 0:256],
            evac=(lambda mch: lambda e, ps: e.tensor_copy(
                gp_f.ch(mch), ps[:, 0:256]))(mch),
        ))
    sweep(ctx, jobs=jobs, banks=R["pb"][0:2], evac_eng="vector",
          entry=ctx.tok("evs", "evv"))
    cpart_tok = ctx.tok("evv")
    w_tok = load_w(ctx, d["l2_wo2"], wlb, pre=ctx.tok("mm"))
    for t8 in range(8):
        jobs = []
        for mch in range(DC):
            jobs.append(dict(
                pre=w_tok + cpart_tok,
                mms=mm_acc([(wlb[:, c * 1024 + mch * 128:
                                 c * 1024 + (mch + 1) * 128],
                             xa[:, (t8 * 8 + c) * 256:(t8 * 8 + c + 1) * 256])
                            for c in range(DC)]),
                psum=lambda b: b[:, 0:256],
                evac=(lambda mch: lambda e, ps: e.tensor_tensor(
                    gp_i.ch(mch), ps[:, 0:256], gp_f.ch(mch), ALU.add))(mch),
            ))
        sweep(ctx, jobs=jobs, banks=R["pb"][0:2], evac_eng="vector",
              entry=ctx.tok("evs", "evv"))
        hx_tok = ctx.tok("evv")
        @blk.scalar
        def _(e, hx_tok=hx_tok, t8=t8):
            ctx.wait(e, hx_tok + ctx.tok("ldg2"))
            instr = None
            for c in range(DC):
                instr = e.activation(fb_fm.ch(c), gp_i.ch(c), AF.Tanh,
                                     bias=bias_t[:, 3 * DC + c:3 * DC + c + 1])
            ctx.inc("evs", instr)
        @blk.gpsimd
        def _(e, t8=t8):
            ctx.wait(e, ctx.tok("evs"))
            for c in range(DC):
                ctx.inc("ldg2", e.dma_start(
                    cc2_in[:, (t8 * 8 + c) * 256:(t8 * 8 + c + 1) * 256],
                    fb_fm.ch(c)), 16)

    @blk.gpsimd
    def _(e):
        ctx.wait(e, ctx.tok("ldg2"))
        ctx.inc("cc", e.collective_compute(
            "AllGather", ALU.bypass, replica_groups=[list(range(NCORES))],
            ins=[cc2_in.opt()], outs=[cc2_out.opt()]))
    ag2_tok = ctx.tok("cc")

    gg_fm = FM(R["xn2"], DC, 256)
    @blk.sync
    def _(e):
        ctx.wait(e, ctx.tok("ldg2"))
        ctx.inc("ld", e.dma_start(gidx, d["l2_gidx2"][:]), 16)
    @blk.gpsimd
    def _(e):
        ctx.wait(e, ctx.tok("ld"))
        ctx.inc("evp", e.nop())
    for r in range(8):
        @blk.gpsimd
        def _(e, r=r):
            ctx.wait(e, ag2_tok + ctx.tok("evs"))
            src3 = cc2_out.rearrange("q (s j) -> q s j", j=256)
            for chh in range(DC):
                col = r * 8 + chh
                ctx.inc("ldg2", e.indirect_dma_start(
                    gg_fm.ch(chh), None, src3,
                    IOA(gidx[0:128, col:col + 1], 1)), 16)
        gh_tok = ctx.tok("ldg2")
        @blk.scalar
        def _(e, gh_tok=gh_tok, r=r):
            ctx.wait(e, gh_tok)
            instr = None
            for c in range(DC):
                instr = e.activation(gg_fm.ch(c), gg_fm.ch(c), AF.Silu)
            ctx.inc("evs", instr)
        @blk.gpsimd
        def _(e, r=r):
            ctx.wait(e, ctx.tok("evs"))
            for c in range(DC):
                ctx.inc("ldg2", e.dma_start(
                    g_spill[:, c * T + r * 256: c * T + (r + 1) * 256],
                    gg_fm.ch(c)), 16)
    if dbg and "dbg_g" in dbg:
        @blk.gpsimd
        def _(e):
            ctx.wait(e, ctx.tok("ldg2"))
            ctx.inc("out", e.dma_start(
                dbg["dbg_g"].rearrange("(c p) t -> p c t", p=128),
                g_spill[:].rearrange("p (c t) -> p c t", t=T)), 16)

    emit_combine_spill(ctx, R, g_spill, s_spill)




def l3_prep(I, m):
    m["l3_wupT"] = np.ascontiguousarray(np.asarray(I["L3_attn_w_up"]).T)
    m["l3_bup"] = _col_layout(I["L3_attn_b_up"], 32)
    cw = np.asarray(I["L3_attn_conv_w"], np.float32).reshape(2048, 4)
    m["l3_convw"] = np.ascontiguousarray(
        cw.reshape(16, 128, 4).transpose(1, 0, 2).reshape(128, 64))
    m["l3_convb"] = _col_layout(I["L3_attn_conv_b"], 16)
    m["l3_skip"] = _col_layout(I["L3_attn_skip"], 16)
    bd = np.zeros((3, 16, 128, 128), np.float32)
    for gi, key in enumerate(["L3_attn_w_f", "L3_attn_w_i", "L3_attn_w_h"]):
        w = np.asarray(I[key], np.float32)
        for c in range(16):
            for gl in range(8):
                bd[gi, c, gl * 16:(gl + 1) * 16, gl * 16:(gl + 1) * 16] = \
                    w[8 * c + gl].T
    m["l3_bd"] = np.ascontiguousarray(bd.reshape(3, 2048, 128))
    m["l3_wdownT"] = np.ascontiguousarray(np.asarray(I["L3_attn_w_down"]).T)
    m["l3_bdown"] = _col_layout(I["L3_attn_b_down"], DC)
    m["l3_gwT"] = np.ascontiguousarray(np.asarray(I["L3_ffn_gate_w"]).T)
    m["l3_rms"] = _col_layout(I["L3_ffn_rms_w"], DC)
    m["l3_w1T"] = np.ascontiguousarray(np.asarray(I["L3_ffn_W1"]).transpose(0, 2, 1))
    m["l3_w2T"] = np.ascontiguousarray(np.asarray(I["L3_ffn_W2"]).transpose(0, 2, 1))
    m["l3_w3T"] = np.ascontiguousarray(np.asarray(I["L3_ffn_W3"]).transpose(0, 2, 1))


def l3_declare(nc, d):
    def din(name, shape, dt=F32):
        d[name] = nc.dram_tensor(name, shape, dt, kind="ExternalInput").ap()
    din("l3_wupT", [1024, 4096]); din("l3_bup", [128, 32])
    din("l3_convw", [128, 64]); din("l3_convb", [128, 16]); din("l3_skip", [128, 16])
    din("l3_bd", [3, 2048, 128]); din("l3_wdownT", [2048, 1024])
    din("l3_bdown", [128, DC])
    din("l3_gwT", [1024, 8]); din("l3_rms", [128, DC])
    din("l3_w1T", [8, 1024, 2816]); din("l3_w2T", [8, 1024, 2816])
    din("l3_w3T", [8, 2816, 1024])


def emit_l3(ctx, R, d, g_spill, s_spill, dbg=None):
    blk = ctx.blk
    nc = ctx.nc
    x_fm = FM(R["x"], DC, T)
    xn_fm = FM(R["xn"], DC, T)

    smf, smb = R["smf"], R["smb"]
    bup = smf[0:128, 0:32]
    convw = smf[0:128, 32:96]
    convb = smf[0:128, 96:112]
    skip = smf[0:128, 112:128]
    bdown = smf[0:128, 128:136]
    rmsw = smf[0:128, 136:144]
    hstate = smf[0:128, 144:160]
    gw_t = smb[0:128, 0:64]
    halo = smb[0:128, 64:128]
    xme = smb[0:128, 128:648]
    sel_v = smb[0:8, 648:1672]
    bdv = R["xn2"]

    act_spill = nc.dram_tensor("l3_act_sp", [128, 16 * T], BF16, kind="Internal").ap()
    z_spill = nc.dram_tensor("l3_z_sp", [128, 16 * T], BF16, kind="Internal").ap()

    @blk.sync
    def _(e):
        ctx.wait(e, ctx.tok("comb", "evv", "evs"))
        for src, dst in [("l3_bup", bup), ("l3_convw", convw),
                         ("l3_convb", convb), ("l3_skip", skip),
                         ("l3_bdown", bdown), ("l3_rms", rmsw),
                         ("sel", sel_v)]:
            ctx.inc("ld", e.dma_start(dst, d[src][:]), 16)
    load_w(ctx, d["l3_gwT"], gw_t)
    small_tok = ctx.tok("ld", "ldg")

    emit_ln(ctx, R, x_fm, 0, T, xn_fm, 0, rms=False, in_f32=True)
    ln_tok = ctx.tok("ln_done")
    if dbg and "dbg_xn" in dbg:
        @blk.gpsimd
        def _(e):
            ctx.wait(e, ctx.tok("ln_done"))
            ctx.inc("out", e.dma_start(
                dbg["dbg_xn"].rearrange("(c p) t -> p c t", p=128),
                R["xn"][:].rearrange("p (c t) -> p c t", t=T)), 16)

    wbuf = R["arena_a"]
    xm_fm = FM(R["arena_d"], 16, 512)
    f_fm = FM(R["arena_b"], 16, 512)
    acbf = R["arena_c"].bitcast(BF16)
    sq = R["sq"]

    def ich(c):
        return acbf[0:128, c * 512:(c + 1) * 512]

    @blk.gpsimd
    def _(e):
        ctx.wait(e, ctx.tok("comb", "ldg2"))
        e.memset(hstate, 0.0)
        ctx.inc("evp", e.memset(halo, 0.0))
    init_tok = ctx.tok("evp")

    wup_tok = load_w(ctx, d["l3_wupT"][:, 0:2048], wbuf, pre=ctx.tok("mm"))

    for t in range(TC):
        t0 = t * 512
        jobs = []
        for mch in range(16):
            jobs.append(dict(
                pre=wup_tok + ln_tok + ctx.tok("evs"),
                mms=mm_acc([(wbuf[:, c * 2048 + mch * 128:
                                  c * 2048 + (mch + 1) * 128],
                             xn_fm.ch(c, t0, 512)) for c in range(DC)]),
                psum=lambda b: b[:, :],
                evac=(lambda mch: lambda e, ps: e.tensor_scalar_add(
                    xm_fm.ch(mch), ps, bup[:, mch:mch + 1]))(mch),
            ))
        sweep(ctx, jobs=jobs, banks=R["pb"][0:3], evac_eng="vector",
              entry=ctx.tok("evs", "evv"))
        xm_tok = ctx.tok("evv")

        if dbg and "dbg_act" in dbg and t == 0:
            @blk.gpsimd
            def _(e, xm_tok=xm_tok):
                ctx.wait(e, xm_tok)
                ctx.inc("out", e.dma_start(
                    dbg["dbg_act"][0:128, 32 * T + 16: 32 * T + 16 + 512],
                    xm_fm.ch(0)), 16)
            @blk.vector
            def _(e):
                ctx.wait(e, ctx.tok("out"))
                ctx.inc("evv", e.nop())
        for c in range(16):
            if dbg and "dbg_act" in dbg and t == 0 and c == 1:
                @blk.gpsimd
                def _(e):
                    ctx.wait(e, ctx.tok("evv"))
                    ctx.inc("out", e.dma_start(
                        dbg["dbg_act"][0:128, 32 * T + 528: 32 * T + 1040],
                        sq[:, 0:512]), 16)
                @blk.vector
                def _(e):
                    ctx.wait(e, ctx.tok("out"))
                    ctx.inc("evv", e.nop())
            @blk.vector
            def _(e, c=c, xm_tok=xm_tok):
                ctx.wait(e, xm_tok + ctx.tok("evs") + small_tok + init_tok)
                e.tensor_copy(xme[0:128, 0:4], halo[0:128, c * 4:(c + 1) * 4])
                ctx.inc("evv", e.tensor_copy(xme[0:128, 4:516], xm_fm.ch(c)))
                ctx.wait(e, ctx.tok("evv"))
                ctx.inc("evv", e.tensor_copy(halo[0:128, c * 4:(c + 1) * 4],
                                             xme[0:128, 512:516]))
                ctx.wait(e, ctx.tok("evv"))
                ctx.inc("evv", e.tensor_scalar_mul(sq[:, 0:512], xme[0:128, 1:513],
                                    convw[:, c * 4:c * 4 + 1]))
                for k in range(1, 4):
                    ctx.wait(e, ctx.tok("evv"))
                    ctx.inc("evv", e.scalar_tensor_tensor(
                        sq[:, 0:512], xme[0:128, 1 + k:513 + k],
                        convw[:, c * 4 + k:c * 4 + k + 1],
                        sq[:, 0:512], ALU.mult, ALU.add))
                ctx.wait(e, ctx.tok("evv"))
                ctx.inc("evv", e.nop())
            @blk.scalar
            def _(e, c=c):
                ctx.wait(e, ctx.tok("evv"))
                instr = e.activation(xm_fm.ch(c), sq[:, 0:512], AF.Silu,
                                     bias=convb[:, c:c + 1])
                ctx.inc("evs", instr)
        act_tok = ctx.tok("evs")

        @blk.gpsimd
        def _(e, t0=t0, act_tok=act_tok):
            ctx.wait(e, act_tok + ctx.tok("ldg2"))
            for c in range(16):
                ctx.inc("ldg2", e.dma_start(
                    act_spill[:, c * T + t0: c * T + t0 + 512],
                    xm_fm.ch(c)), 16)

        for gi in range(2):
            bd_tok = load_w(ctx, d["l3_bd"][gi], bdv, pre=ctx.tok("mm"))
            jobs = []
            for c in range(16):
                def mk_ev(gi=gi, c=c):
                    if gi == 0:
                        return lambda e, ps: e.tensor_copy(f_fm.ch(c), ps)
                    return lambda e, ps: e.tensor_copy(ich(c), ps)
                jobs.append(dict(
                    pre=act_tok + small_tok + bd_tok,
                    mms=mm_acc([(bdv[:, c * 128:(c + 1) * 128],
                                 xm_fm.ch(c))]),
                    psum=lambda b: b[:, :],
                    evac=mk_ev(),
                ))
            sweep(ctx, jobs=jobs, banks=R["pb"][0:3], evac_eng="vector",
                  entry=ctx.tok("evs", "evv"))
        bd_tok = load_w(ctx, d["l3_bd"][2], bdv, pre=ctx.tok("mm"))
        fi_tok = ctx.tok("evv") + bd_tok

        s0 = sq[:, 0:512]; s1 = sq[:, 512:1024]
        s2 = sq[:, 1024:1536]; s3 = sq[:, 1536:2048]
        mtile = xme[0:128, 0:512]
        for c in range(16):
            hjobs = [dict(
                pre=fi_tok + act_tok + ctx.tok("evv"),
                mms=mm_acc([(bdv[:, c * 128:(c + 1) * 128],
                             xm_fm.ch(c))]),
                psum=lambda b: b[:, :],
                evac=lambda e, ps: e.tensor_copy(s3, ps),
            )]
            sweep(ctx, jobs=hjobs, banks=[R["pb"][3]], evac_eng="vector",
                  entry=ctx.tok("evs", "evv"))
            fch = f_fm.ch(c)
            @blk.vector
            def _(e, c=c):
                ctx.wait(e, ctx.tok("evv"))
                instr = e.tensor_single_scalar(mtile, s3, 0.0, ALU.is_ge)
                ctx.inc("evv", instr)
            @blk.scalar
            def _(e, c=c, fch=fch):
                ctx.wait(e, ctx.tok("evv"))
                e.activation(s0, fch, AF.Abs)
                e.activation(s0, s0, AF.Exp, scale=-1.0)
                e.activation(s0, s0, AF.Ln, bias=1.0)
                instr = e.activation(s1, fch, AF.Relu, scale=-1.0)
                ctx.inc("evs", instr)
            @blk.vector
            def _(e, c=c):
                ctx.wait(e, ctx.tok("evs"))
                instr = e.tensor_tensor(s0, s0, s1, ALU.add)
                ctx.inc("evv", instr)
            @blk.scalar
            def _(e, c=c):
                ctx.wait(e, ctx.tok("evv"))
                e.activation(s1, ich(c), AF.Abs)
                e.activation(s1, s1, AF.Exp, scale=-1.0)
                e.activation(s1, s1, AF.Ln, bias=1.0)
                instr = e.activation(s2, ich(c), AF.Relu, scale=-1.0)
                ctx.inc("evs", instr)
            @blk.vector
            def _(e, c=c):
                ctx.wait(e, ctx.tok("evs"))
                ctx.inc("evv", e.tensor_tensor(s1, s1, s2, ALU.add))
                ctx.wait(e, ctx.tok("evv"))
                ctx.inc("evv", e.tensor_tensor(s0, s0, s1, ALU.subtract))
                ctx.wait(e, ctx.tok("evv"))
                ctx.inc("evv", e.nop())
            @blk.scalar
            def _(e, c=c):
                ctx.wait(e, ctx.tok("evv"))
                e.activation(s1, s0, AF.Abs)
                e.activation(s1, s1, AF.Exp, scale=-1.0)
                e.activation(s1, s1, AF.Ln, bias=1.0)
                instr = e.activation(s2, s0, AF.Relu)
                ctx.inc("evs", instr)
            @blk.vector
            def _(e, c=c):
                ctx.wait(e, ctx.tok("evs"))
                instr = e.scalar_tensor_tensor(s2, s2, -1.0, s1,
                                               ALU.mult, ALU.subtract)
                ctx.inc("evv", instr)
            @blk.scalar
            def _(e, c=c, fch=fch):
                ctx.wait(e, ctx.tok("evv"))
                instr = e.activation(fch, s2, AF.Exp, bias=cb(R, 1e-8))
                ctx.inc("evs", instr)
            @blk.vector
            def _(e, c=c):
                ctx.wait(e, ctx.tok("evs"))
                instr = e.scalar_tensor_tensor(s0, s0, 0.0, s1,
                                               ALU.min, ALU.subtract)
                ctx.inc("evv", instr)
            @blk.scalar
            def _(e, c=c):
                ctx.wait(e, ctx.tok("evv"))
                e.activation(s2, s3, AF.Relu)
                e.activation(s2, s2, AF.Ln, bias=cb(R, 0.5 + 1e-8))
                e.activation(s1, s3, AF.Abs)
                e.activation(s1, s1, AF.Exp, scale=-1.0)
                e.activation(s1, s1, AF.Ln, bias=1.0)
                instr = e.activation(s3, s3, AF.Relu, scale=-1.0)
                ctx.inc("evs", instr)
            @blk.vector
            def _(e, c=c):
                ctx.wait(e, ctx.tok("evs"))
                ctx.inc("evv", e.tensor_tensor(s1, s1, s3, ALU.add))
                ctx.wait(e, ctx.tok("evv"))
                ctx.inc("evv", e.tensor_scalar(s1, s1, -1.0, 1e-8,
                                               ALU.mult, ALU.add))
                ctx.wait(e, ctx.tok("evv"))
                ctx.inc("evv", e.tensor_tensor(s3, s2, s1, ALU.subtract))
                ctx.wait(e, ctx.tok("evv"))
                ctx.inc("evv", e.tensor_tensor(s3, mtile, s3, ALU.mult))
                ctx.wait(e, ctx.tok("evv"))
                ctx.inc("evv", e.tensor_tensor(s3, s1, s3, ALU.add))
                ctx.wait(e, ctx.tok("evv"))
                ctx.inc("evv", e.tensor_tensor(s0, s0, s3, ALU.add))
                ctx.wait(e, ctx.tok("evv"))
                ctx.inc("evv", e.nop())
            @blk.scalar
            def _(e, c=c):
                ctx.wait(e, ctx.tok("evv"))
                instr = e.activation(ich(c), s0, AF.Exp)
                ctx.inc("evs", instr)
            @blk.vector
            def _(e, c=c, fch=fch):
                ctx.wait(e, ctx.tok("evs") + ctx.tok("scan"))
                ctx.inc("scan", e.tensor_tensor_scan(
                    s3, fch, ich(c), hstate[0:128, c:c + 1], ALU.mult, ALU.add))
                ctx.wait(e, ctx.tok("scan"))
                ctx.inc("scan", e.tensor_copy(hstate[0:128, c:c + 1],
                                              s3[0:128, 511:512]))
                ctx.wait(e, ctx.tok("scan"))
                ctx.inc("evv", e.nop())

    wupz_tok = load_w(ctx, d["l3_wupT"][:, 2048:4096], wbuf, pre=ctx.tok("mm"))
    for t in range(TC):
        t0 = t * 512
        jobs = []
        for mch in range(16):
            jobs.append(dict(
                pre=wupz_tok + ln_tok,
                mms=mm_acc([(wbuf[:, c * 2048 + mch * 128:
                                  c * 2048 + (mch + 1) * 128],
                             xn_fm.ch(c, t0, 512)) for c in range(DC)]),
                psum=lambda b: b[:, :],
                evac=(lambda mch: lambda e, ps: e.activation(
                    xm_fm.ch(mch), ps, AF.Silu,
                    bias=bup[:, 16 + mch:16 + mch + 1]))(mch),
            ))
        sweep(ctx, jobs=jobs, banks=R["pb"][0:3], evac_eng="scalar",
              entry=ctx.tok("evs", "evv"))
        z_tok = ctx.tok("evs")
        @blk.gpsimd
        def _(e, t0=t0, z_tok=z_tok):
            ctx.wait(e, z_tok + ctx.tok("ldg2"))
            for c in range(16):
                ctx.inc("ldg2", e.dma_start(
                    z_spill[:, c * T + t0: c * T + t0 + 512],
                    xm_fm.ch(c)), 16)

    wd_tok = load_w(ctx, d["l3_wdownT"], wbuf, pre=ctx.tok("mm"))
    sz_fm = FM(R["arena_b"], 16, 512)
    s_fm = FM(R["xn2"], DC, 512)
    for t in range(TC):
        t0 = t * 512
        @blk.gpsimd
        def _(e, t0=t0):
            ctx.wait(e, ctx.tok("mm", "evv", "ldg2", "scan"))
            for c in range(16):
                ctx.inc("ldg2", e.dma_start(
                    xm_fm.ch(c), act_spill[:, c * T + t0: c * T + t0 + 512]), 16)
                ctx.inc("ldg2", e.dma_start(
                    sz_fm.ch(c), z_spill[:, c * T + t0: c * T + t0 + 512]), 16)
        ld_tok = ctx.tok("ldg2")
        @blk.vector
        def _(e, ld_tok=ld_tok):
            ctx.wait(e, ld_tok + ctx.tok("scan"))
            for c in range(16):
                ctx.inc("evv", e.tensor_scalar(xm_fm.ch(c), xm_fm.ch(c),
                                               skip[:, c:c + 1],
                                               hstate[0:128, c:c + 1],
                                               ALU.mult, ALU.add))
                ctx.wait(e, ctx.tok("evv"))
                ctx.inc("evv", e.tensor_tensor(xm_fm.ch(c), xm_fm.ch(c),
                                               sz_fm.ch(c), ALU.mult))
                ctx.wait(e, ctx.tok("evv"))
            ctx.inc("evv", e.nop())
        h_tok = ctx.tok("evv")
        jobs = []
        for mch in range(DC):
            jobs.append(dict(
                pre=wd_tok + h_tok,
                mms=mm_acc([(wbuf[:, k * 1024 + mch * 128:
                                  k * 1024 + (mch + 1) * 128],
                             xm_fm.ch(k)) for k in range(16)]),
                psum=lambda b: b[:, :],
                evac=(lambda mch: lambda e, ps: e.tensor_scalar_add(
                    s_fm.ch(mch), ps, bdown[:, mch:mch + 1]))(mch),
            ))
        sweep(ctx, jobs=jobs, banks=R["pb"][4:6], evac_eng="vector",
              entry=ctx.tok("evs", "evv"))
        @blk.gpsimd
        def _(e, t0=t0):
            ctx.wait(e, ctx.tok("evv"))
            for c in range(DC):
                ctx.inc("ldg2", e.dma_start(
                    s_spill[:, c * T + t0: c * T + t0 + 512], s_fm.ch(c)), 16)
    if dbg and "dbg_s" in dbg:
        @blk.gpsimd
        def _(e):
            ctx.wait(e, ctx.tok("ldg2"))
            ctx.inc("out", e.dma_start(
                dbg["dbg_s"].rearrange("(c p) t -> p c t", p=128),
                s_spill[:].rearrange("p (c t) -> p c t", t=T)), 16)
    if dbg and "dbg_act" in dbg:
        @blk.gpsimd
        def _(e):
            ctx.wait(e, ctx.tok("ldg2"))
            ctx.inc("out", e.dma_start(
                dbg["dbg_act"][0:128, 0:16 * T].rearrange("p (c t) -> p c t", t=T),
                act_spill[:].rearrange("p (c t) -> p c t", t=T)), 16)
            ctx.inc("out", e.dma_start(
                dbg["dbg_act"][0:128, 16 * T:32 * T].rearrange("p (c t) -> p c t", t=T),
                z_spill[:].rearrange("p (c t) -> p c t", t=T)), 16)
            ctx.inc("out", e.dma_start(dbg["dbg_act"][0:128, 32 * T:32 * T + 16],
                                       hstate[:]), 16)

    y_fm = FM(R["xn2"], DC, 512)
    su_fm = FM(R["arena_d"], 11, 512)
    hh_fm = FM(R["arena_b"], 11, 512)
    ya_fm = FM(R["arena_c"], DC, 512)
    cols = R["cols"]
    for t in range(TC):
        t0 = t * 512
        emit_ln(ctx, R, xn_fm, t0, 512, y_fm, 0, rms=True, eps=1e-6,
                in_f32=False, gain_col=lambda c: rmsw[:, c:c + 1])
        y_tok = ctx.tok("ln_done")

        rjobs = []
        for q in range(4):
            rjobs.append(dict(
                pre=small_tok + y_tok,
                mms=mm_acc([(y_fm.ch(c, q * 128, 128),
                             gw_t[:, c * 8:(c + 1) * 8]) for c in range(DC)]),
                psum=lambda b: b[:, 0:8],
                evac=(lambda q: lambda e, ps: e.tensor_copy(
                    R["ltm"][:, q * 8:(q + 1) * 8], ps))(q),
            ))
        sweep(ctx, jobs=rjobs, banks=R["pb"][0:3], evac_eng="vector",
              entry=ctx.tok("evs", "evv"))
        msk1 = sq[:, 0:8]; mskd = sq[:, 8:16]; msk2 = sq[:, 16:24]
        for q in range(4):
            lg = R["ltm"][:, q * 8:(q + 1) * 8]
            @blk.vector
            def _(e, q=q, lg=lg):
                ctx.wait(e, ctx.tok("evv", "evs"))
                ctx.inc("evv", e.tensor_reduce(cols[:, 0:1], lg, AX.X, ALU.max))
                ctx.wait(e, ctx.tok("evv"))
                ctx.inc("evv", e.tensor_scalar(msk1, lg, cols[:, 0:1], 1.0,
                                               ALU.is_ge, ALU.mult))
                ctx.wait(e, ctx.tok("evv"))
                e.scalar_tensor_tensor(mskd, msk1, -1e30, lg, ALU.mult, ALU.add)
                ctx.inc("evv", e.tensor_reduce(cols[:, 1:2], mskd, AX.X, ALU.max))
                ctx.wait(e, ctx.tok("evv"))
                instr = e.tensor_tensor(cols[:, 2:3], cols[:, 1:2], cols[:, 0:1],
                                        ALU.subtract)
                ctx.inc("evv", instr)
            @blk.scalar
            def _(e, q=q):
                ctx.wait(e, ctx.tok("evv"))
                instr = e.activation(cols[:, 3:4], cols[:, 2:3], AF.Exp)
                ctx.inc("evs", instr)
            @blk.vector
            def _(e, q=q, lg=lg):
                ctx.wait(e, ctx.tok("evs"))
                ctx.inc("evv", e.tensor_single_scalar(cols[:, 4:5], cols[:, 3:4],
                                                      1.0, ALU.add))
                ctx.wait(e, ctx.tok("evv"))
                ctx.inc("evv", e.reciprocal(cols[:, 5:6], cols[:, 4:5]))
                ctx.wait(e, ctx.tok("evv"))
                ctx.inc("evv", e.tensor_tensor(cols[:, 6:7], cols[:, 3:4],
                                               cols[:, 5:6], ALU.mult))
                ctx.wait(e, ctx.tok("evv"))
                e.tensor_scalar(msk2, mskd, cols[:, 1:2], cols[:, 6:7],
                                ALU.is_ge, ALU.mult)
                e.tensor_scalar_mul(msk1, msk1, cols[:, 5:6])
                ctx.inc("evv", e.nop())
                ctx.wait(e, ctx.tok("evv"))
                instr = e.tensor_tensor(lg, msk1, msk2, ALU.add)
                ctx.inc("evv", instr)
        @blk.vector
        def _(e):
            ctx.wait(e, ctx.tok("evv"))
            instr = e.tensor_copy(R["ltm_bf"][:], R["ltm"][:])
            ctx.inc("evv", instr)
        tjobs = []
        for q in range(4):
            tjobs.append(dict(
                pre=ctx.tok("evv"),
                mms=[(lambda q: lambda e, ps: e.transpose(
                    ps, R["ltm_bf"][:, q * 8:(q + 1) * 8], R["iden_b"][:]))(q)],
                psum=lambda b: b[0:8, 0:64].bitcast(BF16),
                evac=(lambda q: lambda e, ps: e.tensor_copy(
                    R["cfm"][:, q * 128: (q + 1) * 128], ps))(q),
            ))
        sweep(ctx, jobs=tjobs, banks=R["pb"][2:4], evac_eng="vector",
              entry=ctx.tok("evs", "evv"))
        cfm_tok = ctx.tok("evv")

        for ex in range(8):
            cjobs = [dict(
                pre=cfm_tok,
                mms=mm_acc([(sel_v[0:8, ex * 128:(ex + 1) * 128],
                             R["cfm"][0:8, 0:512])]),
                psum=lambda b: b[:, :],
                evac=lambda e, ps: e.tensor_copy(R["coeffB"], ps),
            )]
            sweep(ctx, jobs=cjobs, banks=[R["pb"][2]], evac_eng="vector",
                  entry=ctx.tok("evs", "evv"))
            coefb_tok = ctx.tok("evv")

            for fi_, (k0, nk) in enumerate([(0, 8), (8, 7), (15, 7)]):
                f0 = k0 * 128
                fw = nk * 128
                slot = fi_ % 2
                sl = wbuf[:, slot * 8192: slot * 8192 + 8192]
                w_tok = load_w(ctx, d["l3_w1T"][ex][:, f0:f0 + fw], wbuf,
                               col0=slot * 8192, pre=ctx.buf_pre(("l3w", slot)))
                jobs = []
                for mch in range(nk):
                    jobs.append(dict(
                        pre=w_tok + y_tok,
                        mms=mm_acc([(sl[:, c * fw + mch * 128:
                                        c * fw + (mch + 1) * 128],
                                     y_fm.ch(c)) for c in range(DC)]),
                        psum=lambda b: b[:, :],
                        evac=(lambda mch: lambda e, ps: e.activation(
                            su_fm.ch(mch), ps, AF.Silu))(mch),
                    ))
                sweep(ctx, jobs=jobs, banks=R["pb"][0:3], evac_eng="scalar",
                      entry=ctx.tok("evs", "evv"))
                ctx.buf_mark(("l3w", slot))
                su_tok = ctx.tok("evs")
                slot2 = (fi_ + 1) % 2
                sl2 = wbuf[:, slot2 * 8192: slot2 * 8192 + 8192]
                w_tok = load_w(ctx, d["l3_w2T"][ex][:, f0:f0 + fw], wbuf,
                               col0=slot2 * 8192, pre=ctx.buf_pre(("l3w", slot2)))
                jobs = []
                for mch in range(nk):
                    jobs.append(dict(
                        pre=w_tok + su_tok,
                        mms=mm_acc([(sl2[:, c * fw + mch * 128:
                                         c * fw + (mch + 1) * 128],
                                     y_fm.ch(c)) for c in range(DC)]),
                        psum=lambda b: b[:, :],
                        evac=(lambda mch: lambda e, ps: e.tensor_tensor(
                            hh_fm.ch(mch), ps, su_fm.ch(mch), ALU.mult))(mch),
                    ))
                sweep(ctx, jobs=jobs, banks=R["pb"][0:3], evac_eng="vector",
                      entry=ctx.tok("evs", "evv"))
                ctx.buf_mark(("l3w", slot2))
                @blk.vector
                def _(e, coefb_tok=coefb_tok, nk=nk):
                    ctx.wait(e, ctx.tok("evv") + coefb_tok)
                    instr = None
                    for mch in range(nk):
                        instr = e.tensor_tensor(hh_fm.ch(mch), hh_fm.ch(mch),
                                                R["coeffB"], ALU.mult)
                    ctx.inc("evv", instr)
                hc_tok = ctx.tok("evv")
                w_tok = load_w(ctx, d["l3_w3T"][ex][f0:f0 + fw, :], wbuf,
                               col0=slot * 8192, pre=ctx.buf_pre(("l3w", slot)),
                               group=4)
                jobs = []
                for mch in range(DC):
                    def mk_ev(ex=ex, fi_=fi_, mch=mch):
                        if ex == 0 and fi_ == 0:
                            return lambda e, ps: e.tensor_copy(ya_fm.ch(mch), ps)
                        return lambda e, ps: e.tensor_tensor(
                            ya_fm.ch(mch), ya_fm.ch(mch), ps, ALU.add)
                    jobs.append(dict(
                        pre=w_tok + hc_tok,
                        mms=mm_acc([(sl[:, k * 1024 + mch * 128:
                                        k * 1024 + (mch + 1) * 128],
                                     hh_fm.ch(k)) for k in range(nk)]),
                        psum=lambda b: b[:, :],
                        evac=mk_ev(),
                    ))
                sweep(ctx, jobs=jobs, banks=R["pb"][0:3], evac_eng="vector",
                      entry=ctx.tok("evs", "evv"))
                ctx.buf_mark(("l3w", slot))

        ya_tok = ctx.tok("evv")
        @blk.vector
        def _(e, t0=t0, ya_tok=ya_tok):
            ctx.wait(e, ya_tok)
            instr = None
            for c in range(DC):
                instr = e.tensor_tensor(ya_fm.ch(c), ya_fm.ch(c),
                                        xn_fm.ch(c, t0, 512), ALU.add)
            ctx.inc("evv", instr)
        @blk.scalar
        def _(e, t0=t0):
            ctx.wait(e, ctx.tok("evv") + ctx.tok("ldg2"))
            instr = None
            for c in range(DC):
                instr = e.activation(y_fm.ch(c), ya_fm.ch(c), AF.Silu)
            ctx.inc("evs", instr)
        @blk.gpsimd
        def _(e, t0=t0):
            ctx.wait(e, ctx.tok("evs"))
            for c in range(DC):
                ctx.inc("ldg2", e.dma_start(
                    g_spill[:, c * T + t0: c * T + t0 + 512], y_fm.ch(c)), 16)
    if dbg and "dbg_g" in dbg:
        @blk.gpsimd
        def _(e):
            ctx.wait(e, ctx.tok("ldg2"))
            ctx.inc("out", e.dma_start(
                dbg["dbg_g"].rearrange("(c p) t -> p c t", p=128),
                g_spill[:].rearrange("p (c t) -> p c t", t=T)), 16)

    emit_combine_spill(ctx, R, g_spill, s_spill)
def build_layer(layer, dbg_names=()):
    nc = bass.Bass("TRN2", target_bir_lowering=False)
    d = {}
    d["iden_bf16"] = nc.dram_tensor("iden_bf16", [128, 128], BF16,
                                    kind="ExternalInput").ap()
    d["causal"] = nc.dram_tensor("causal", [128, 128], F32,
                                 kind="ExternalInput").ap()
    d["sel"] = nc.dram_tensor("sel", [8, 8 * 128], BF16,
                              kind="ExternalInput").ap()
    xin = nc.dram_tensor("xin", [D, T], F32, kind="ExternalInput").ap()
    xout = nc.dram_tensor("xout", [D, T], F32, kind="ExternalOutput").ap()
    dbg = {}
    for nm in dbg_names:
        shape = [8 + 128, T] if nm == "dbg_coeff" else [D, T]
        dt_ = BF16 if nm in ("dbg_att",) else F32
        if nm == "dbg_att":
            shape = [128, 4 * T]
        if nm == "dbg_act":
            shape = [128, 32 * T + 16 + 1024]
            dt_ = F32
        dbg[nm] = nc.dram_tensor(nm, shape, dt_, kind="ExternalOutput").ap()
    g_spill = nc.dram_tensor("g_spill", [128, DC * T], BF16, kind="Internal").ap()
    s_spill = nc.dram_tensor("s_spill", [128, DC * T], BF16, kind="Internal").ap()
    x_spill = nc.dram_tensor("x_spill", [128, DC * T], BF16, kind="Internal").ap()

    if layer == 0:
        l0_declare(nc, d)
    elif layer == 1:
        l1_declare(nc, d)
    elif layer == 2:
        l2_declare(nc, d)
    elif layer == 3:
        l3_declare(nc, d)
    R = None
    with nc.Block() as blk:
        ctx = Ctx(nc, blk)
        for s_ in ["mm", "evs", "evv", "evp", "ldg", "ldg2", "ld", "sq",
                   "statmm", "lnv", "lns", "ln_done", "out", "comb", "cc",
                   "scan"]:
            ctx.sem(s_)
        R = alloc_resources(nc)
        emit_init(ctx, R, d, xin)
        if layer == 0:
            emit_l0(ctx, R, d, g_spill, dbg)
        elif layer == 1:
            emit_l1(ctx, R, d, g_spill, s_spill, x_spill, dbg)
        elif layer == 2:
            emit_l2(ctx, R, d, g_spill, s_spill, dbg)
        elif layer == 3:
            emit_l3(ctx, R, d, g_spill, s_spill, dbg)
        @blk.sync
        def _(e):
            ctx.wait(e, ctx.tok("comb"))
            instr = e.dma_start(xout.rearrange("(c p) t -> p c t", p=128),
                                R["x"][:].rearrange("p (c t) -> p c t", t=T))
            ctx.inc("out", instr, 16)
            e.wait_ge(ctx.sem("out"), ctx.cnt["out"])
    return nc


def build_full():
    nc = bass.Bass("TRN2", target_bir_lowering=False)
    d = {}
    d["iden_bf16"] = nc.dram_tensor("iden_bf16", [128, 128], BF16,
                                    kind="ExternalInput").ap()
    d["causal"] = nc.dram_tensor("causal", [128, 128], F32,
                                 kind="ExternalInput").ap()
    d["sel"] = nc.dram_tensor("sel", [8, 8 * 128], BF16,
                              kind="ExternalInput").ap()
    xin = nc.dram_tensor("xin", [D, T], F32, kind="ExternalInput").ap()
    xout = nc.dram_tensor("xout", [D, T], F32, kind="ExternalOutput").ap()
    g_spill = nc.dram_tensor("g_spill", [128, DC * T], BF16, kind="Internal").ap()
    s_spill = nc.dram_tensor("s_spill", [128, DC * T], BF16, kind="Internal").ap()
    x_spill = nc.dram_tensor("x_spill", [128, DC * T], BF16, kind="Internal").ap()
    l0_declare(nc, d)
    l1_declare(nc, d)
    l2_declare(nc, d)
    l3_declare(nc, d)
    with nc.Block() as blk:
        ctx = Ctx(nc, blk)
        for s_ in ["mm", "evs", "evv", "evp", "ldg", "ldg2", "ld", "sq",
                   "statmm", "lnv", "lns", "ln_done", "out", "comb", "cc",
                   "scan"]:
            ctx.sem(s_)
        R = alloc_resources(nc)
        emit_init(ctx, R, d, xin)
        emit_l0(ctx, R, d, g_spill, None)
        emit_l1(ctx, R, d, g_spill, s_spill, x_spill, None)
        emit_l2(ctx, R, d, g_spill, s_spill, None)
        emit_l3(ctx, R, d, g_spill, s_spill, None)
        @blk.sync
        def _(e):
            ctx.wait(e, ctx.tok("comb"))
            instr = e.dma_start(xout.rearrange("(c p) t -> p c t", p=128),
                                R["x"][:].rearrange("p (c t) -> p c t", t=T))
            ctx.inc("out", instr, 16)
            e.wait_ge(ctx.sem("out"), ctx.cnt["out"])
    return nc


_NC_CACHE = {}


def kernel(**inputs):
    common = _prep_common()
    base = dict(common)
    l0_prep(inputs, base)
    l1_prep(inputs, base)
    l3_prep(inputs, base)
    x = np.asarray(inputs["L0_attn_x"], np.float32)
    if "full" not in _NC_CACHE:
        _NC_CACHE["full"] = build_full()
    nc_ = _NC_CACHE["full"]
    in_maps = []
    for b in range(NCORES):
        m = dict(base)
        l2_prep(inputs, m, b)
        m["xin"] = np.ascontiguousarray(x[b].T)
        in_maps.append(m)
    res = run_bass_kernel_spmd(nc_, in_maps, core_ids=list(range(NCORES)))
    out = np.stack([np.asarray(r["xout"], np.float32).T for r in res.results], 0)
    return np.ascontiguousarray(out)


def kernel_unfused(**inputs):
    common = _prep_common()
    base = dict(common)
    l0_prep(inputs, base)
    l1_prep(inputs, base)
    l3_prep(inputs, base)
    l2_maps = []
    for b in range(NCORES):
        m2 = {}
        l2_prep(inputs, m2, b)
        l2_maps.append(m2)

    x = np.asarray(inputs["L0_attn_x"], np.float32)
    xin = [np.ascontiguousarray(x[b].T) for b in range(NCORES)]

    keysets = {
        0: [k for k in base if k.startswith("l0_")],
        1: [k for k in base if k.startswith("l1_")],
        2: None,
        3: [k for k in base if k.startswith("l3_")],
    }
    ckeys = [k for k in common]
    for layer in range(4):
        if layer not in _NC_CACHE:
            _NC_CACHE[layer] = build_layer(layer)
        nc_ = _NC_CACHE[layer]
        in_maps = []
        for b in range(NCORES):
            m = {k: base[k] for k in ckeys}
            if layer == 2:
                m.update(l2_maps[b])
            else:
                for k in keysets[layer]:
                    m[k] = base[k]
            m["xin"] = xin[b]
            in_maps.append(m)
        res = run_bass_kernel_spmd(nc_, in_maps, core_ids=list(range(NCORES)))
        xin = [np.ascontiguousarray(np.asarray(r["xout"], np.float32))
               for r in res.results]

    out = np.stack([xin[b].T for b in range(NCORES)], 0)
    return np.ascontiguousarray(out)


# version 07
# speedup vs baseline: 1.1242x; 1.1242x over previous
import sys
sys.path.insert(0, '/opt/trn_rl_repo')
import numpy as np
import concourse.bass as bass
import concourse.mybir as mybir
from concourse.bass_utils import run_bass_kernel_spmd

F32 = mybir.dt.float32
BF16 = mybir.dt.bfloat16
I32 = mybir.dt.int32
AF = mybir.ActivationFunctionType
ALU = mybir.AluOpType
AX = mybir.AxisListType

NCORES = 8
D = 1024
T = 2048
DC = 8
TC = 4
LN_EPS = 1e-5

DEBUG = False
DBG_NOROPE = False
DBG_ROPEONLY = False


class Ctx:
    def __init__(self, nc, blk):
        self.nc, self.blk = nc, blk
        self.sem_h, self.cnt = {}, {}
        self.buf_tok = {}

    def buf_pre(self, name):
        return self.buf_tok.get(name, [])

    def buf_mark(self, name):
        self.buf_tok[name] = self.tok("mm")

    def sem(self, name):
        if name not in self.sem_h:
            self.sem_h[name] = self.nc.alloc_semaphore(name)
            self.cnt[name] = 0
        return self.sem_h[name]

    def inc(self, name, instr, k=1):
        instr.then_inc(self.sem(name), k)
        self.cnt[name] += k
        return (name, self.cnt[name])

    def tok(self, *names):
        return [(n, self.cnt.get(n, 0)) for n in names]

    def wait(self, eng, toks):
        for (n, v) in toks:
            if v > 0:
                eng.wait_ge(self.sem(n), v)


class FM:
    def __init__(self, tile, nch, tlen, col0=0):
        self.t, self.nch, self.T, self.c0 = tile, nch, tlen, col0

    def ch(self, c, t0=0, tw=None, p0=0, pw=128):
        tw = self.T - t0 if tw is None else tw
        o = self.c0 + c * self.T + t0
        return self.t[p0:p0 + pw, o: o + tw]


def sweep(ctx, *, jobs, banks, evac_eng, entry=(), mm_sem="mm"):
    blk = ctx.blk
    nb = len(banks)
    ev_sem = "evs" if evac_eng == "scalar" else ("evv" if evac_eng == "vector" else "evp")
    ev_base = ctx.cnt.get(ev_sem, 0)
    ctx.sem(ev_sem)
    mm_toks = []

    @blk.tensor
    def _(e):
        for i, job in enumerate(jobs):
            ctx.wait(e, job.get("pre", ()))
            if i < nb:
                ctx.wait(e, entry)
            else:
                e.wait_ge(ctx.sem(ev_sem), ev_base + i - nb + 1)
            ps = job["psum"](banks[i % nb])
            instr = None
            for mm in job["mms"]:
                instr = mm(e, ps)
            mm_toks.append(ctx.inc(mm_sem, instr))

    @getattr(blk, evac_eng)
    def _(e):
        for i, job in enumerate(jobs):
            ctx.wait(e, [mm_toks[i]])
            instr = job["evac"](e, job["psum"](banks[i % nb]))
            ctx.inc(ev_sem, instr)

    return ctx.tok(ev_sem)


def mm_acc(pairs):
    n = len(pairs)
    out = []
    for j, item in enumerate(pairs):
        lt, rh = item[0], item[1]
        tp = item[2] if len(item) > 2 else None
        def mk(lt=lt, rh=rh, j=j, tp=tp):
            return lambda e, ps: e.matmul(ps, lt, rh, start=(j == 0),
                                          stop=(j == n - 1), tile_position=tp)
        out.append(mk())
    return out


def load_w(ctx, dram_ap, dst_tile, *, col0=0, group=8, sem="ldg", pre=()):
    blk = ctx.blk
    K, M = dram_ap.shape
    kch = K // 128

    @blk.gpsimd
    def _(e):
        ctx.wait(e, pre)
        c = 0
        while c < kch:
            g = min(group, kch - c)
            s3 = dram_ap[c * 128:(c + g) * 128, :].rearrange(
                "(c p) m -> p c m", p=128)
            d3 = dst_tile[:, col0 + c * M: col0 + (c + g) * M].rearrange(
                "p (c m) -> p c m", m=M)
            instr = e.dma_start(d3, s3)
            ctx.inc(sem, instr, 16)
            c += g
    return ctx.tok(sem)


def emit_ln(ctx, R, x_fm, xoff, tlen, out_fm, ooff, *, rms=False, eps=LN_EPS,
            in_f32=True, gain_col=None, bias_col=None, nch=DC):
    blk = ctx.blk
    inv = 1.0 / (nch * 128)
    ones_s = (R["ones_m1f"] if in_f32 else R["ones_m1b"])[:]
    psA, psB = R["pb"][6], R["pb"][7]
    sq = R["sq"]
    rA, rB, r1, r2 = R["row_a"], R["row_b"], R["row_t1"], R["row_t2"]

    for t in range(tlen // 512):
        t0 = xoff + t * 512
        o0 = ooff + t * 512
        groups = [list(range(0, min(4, nch)))]
        if nch > 4:
            groups.append(list(range(4, nch)))
        for gi, grp in enumerate(groups):
            @blk.scalar
            def _(e, grp=grp, gi=gi):
                ctx.wait(e, ctx.tok("ln_done" if gi == 0 else "statmm"))
                instr = None
                for c in grp:
                    instr = e.activation(sq[:, (c % 4) * 512:(c % 4 + 1) * 512],
                                         x_fm.ch(c, t0, 512), AF.Square)
                ctx.inc("sq", instr)
            sq_tok = ctx.tok("sq")

            @blk.tensor
            def _(e, grp=grp, gi=gi, sq_tok=sq_tok):
                ctx.wait(e, sq_tok)
                if gi == 0:
                    ctx.wait(e, ctx.tok("ln_done") + ctx.tok("evs", "evv"))
                instr = None
                if not rms:
                    for c in grp:
                        instr = e.matmul(psA[0:1, :], ones_s, x_fm.ch(c, t0, 512),
                                         start=(c == grp[0] and gi == 0),
                                         stop=(c == grp[-1] and gi == len(groups) - 1))
                for c in grp:
                    instr = e.matmul(psB[0:1, :], R["ones_m1f"][:],
                                     sq[:, (c % 4) * 512:(c % 4 + 1) * 512],
                                     start=(c == grp[0] and gi == 0),
                                     stop=(c == grp[-1] and gi == len(groups) - 1))
                ctx.inc("statmm", instr)

        @blk.vector
        def _(e):
            ctx.wait(e, ctx.tok("statmm"))
            if not rms:
                e.tensor_single_scalar(r1[:], psA[0:1, :], inv, ALU.mult)
            instr = e.tensor_single_scalar(r2[:], psB[0:1, :], inv, ALU.mult)
            ctx.inc("lnv", instr)
        @blk.scalar
        def _(e):
            ctx.wait(e, ctx.tok("lnv"))
            if not rms:
                instr = e.activation(rA[:], r1[:], AF.Square)
                ctx.inc("lns", instr)
        if not rms:
            @blk.vector
            def _(e):
                ctx.wait(e, ctx.tok("lns"))
                instr = e.tensor_tensor(r2[:], r2[:], rA[:], ALU.subtract)
                ctx.inc("lnv", instr)
        @blk.scalar
        def _(e):
            ctx.wait(e, ctx.tok("lnv"))
            e.activation(rA[:], r2[:], AF.Ln, bias=cb(R, eps, 1))
            instr = e.activation(rA[:], rA[:], AF.Exp, scale=-0.5)
            ctx.inc("lns", instr)
        @blk.vector
        def _(e):
            ctx.wait(e, ctx.tok("lns"))
            if not rms:
                instr = e.scalar_tensor_tensor(rB[:], r1[:], -1.0, rA[:],
                                               ALU.mult, ALU.mult)
                ctx.inc("lnv", instr)
        @blk.tensor
        def _(e):
            ctx.wait(e, ctx.tok("lnv", "lns"))
            e.matmul(psA[:, :], R["ones_k1f"][:], rA[:], start=True, stop=True)
            instr = e.matmul(psB[:, :], R["ones_k1f"][:],
                             rA[:] if rms else rB[:], start=True, stop=True)
            ctx.inc("statmm", instr)
        bc_tok = ctx.tok("statmm")

        @blk.vector
        def _(e):
            ctx.wait(e, bc_tok + ctx.tok("mm", "ldg2"))
            for c in range(nch):
                tmp = sq[:, (c % 4) * 512:(c % 4 + 1) * 512]
                ctx.inc("evv", e.tensor_tensor(tmp, x_fm.ch(c, t0, 512),
                                               psA[:, :], ALU.mult))
                ctx.wait(e, ctx.tok("evv"))
                if not rms:
                    ctx.inc("evv", e.tensor_tensor(tmp, tmp, psB[:, :], ALU.add))
                    ctx.wait(e, ctx.tok("evv"))
                if gain_col is not None and bias_col is not None:
                    instr = e.tensor_scalar(out_fm.ch(c, o0, 512), tmp,
                                            gain_col(c), bias_col(c),
                                            ALU.mult, ALU.add)
                elif gain_col is not None:
                    instr = e.tensor_scalar_mul(out_fm.ch(c, o0, 512), tmp,
                                                gain_col(c))
                else:
                    instr = e.tensor_copy(out_fm.ch(c, o0, 512), tmp)
                ctx.inc("evv", instr)
                ctx.wait(e, ctx.tok("evv"))
            ctx.inc("ln_done", e.nop())
    return ctx.tok("ln_done")


def _prep_common():
    import ml_dtypes
    C = {}
    C["iden_bf16"] = np.eye(128, dtype=ml_dtypes.bfloat16)
    C["causal"] = np.where(np.arange(128)[None, :] <= np.arange(128)[:, None],
                           0.0, -1e30).astype(np.float32)
    sel = np.zeros((8, 8 * 128), dtype=np.float32)
    for e in range(8):
        sel[e, e * 128:(e + 1) * 128] = 1.0
    C["sel"] = sel.astype(ml_dtypes.bfloat16)
    return C


def _col_layout(v, nch):
    return np.ascontiguousarray(np.asarray(v, np.float32).reshape(nch, 128).T)


def alloc_resources(nc):
    sb = lambda n, s, dt: nc.alloc_sbuf_tensor(n, s, dt)
    R = {}
    R["pb"] = [nc.alloc_psum_tensor(f"pb{i}", [128, 512], F32) for i in range(8)]
    R["x"] = sb("x", [128, DC * T], F32)
    R["xn"] = sb("xn", [128, DC * T], BF16)
    R["arena_a"] = sb("arena_a", [128, 16 * 1024], BF16)
    R["arena_b"] = sb("arena_b", [128, 8 * 1024], BF16)
    R["arena_c"] = sb("arena_c", [128, 8 * 512], F32)
    R["arena_d"] = sb("arena_d", [128, 16 * 512], BF16)
    R["sq"] = sb("sq", [128, 4 * 512], F32)
    R["xn2"] = sb("xn2", [128, DC * 512], BF16)
    R["sbuf_g"] = R["xn2"]
    R["sbuf_s"] = R["arena_c"].bitcast(BF16)
    R["row_a"] = sb("row_a", [1, 512], F32)
    R["row_b"] = sb("row_b", [1, 512], F32)
    R["row_t1"] = sb("row_t1", [1, 512], F32)
    R["row_t2"] = sb("row_t2", [1, 512], F32)
    R["ones_m1f"] = sb("ones_m1f", [128, 1], F32)
    R["ones_m1b"] = sb("ones_m1b", [128, 1], BF16)
    R["ones_k1f"] = sb("ones_k1f", [1, 128], F32)
    R["cols"] = sb("cols", [128, 8], F32)
    R["cbias"] = sb("cbias", [128, 8], F32)
    R["smf"] = sb("smf", [128, 192], F32)
    R["smb"] = sb("smb", [128, 2304], BF16)
    R["cbias_vals"] = {1e-5: 0, 1e-6: 1, 1e-8: 2, 1.0 + 1e-8: 3,
                       0.5 + 1e-8: 4, 1.0: 5}
    R["iden_b"] = sb("iden_b", [128, 128], BF16)
    R["causal_t"] = sb("causal_t", [128, 128], F32)
    R["cfm"] = R["sq"].bitcast(BF16)[0:8, 512:1024]
    R["ltm"] = sb("ltm", [128, 4 * 8], F32)
    R["ltm_bf"] = sb("ltm_bf", [128, 4 * 8], BF16)
    R["coeffB"] = R["sq"].bitcast(BF16)[0:128, 0:512]
    return R


def cb(R, v, pw=128):
    c = R["cbias_vals"][v]
    return R["cbias"][0:pw, c:c + 1]


def emit_init(ctx, R, d, xin):
    blk = ctx.blk

    @blk.sync
    def _(e):
        instr = e.dma_start(R["x"][:].rearrange("p (c t) -> p c t", t=T),
                            xin.rearrange("(c p) t -> p c t", p=128))
        ctx.inc("ld", instr, 16)
        for src, dst in [("iden_bf16", R["iden_b"]),
                         ("causal", R["causal_t"])]:
            if src in d:
                instr = e.dma_start(dst[:], d[src][:])
                ctx.inc("ld", instr, 16)

    @blk.gpsimd
    def _(e):
        e.memset(R["ones_m1f"][:], 1.0)
        e.memset(R["ones_m1b"][:], 1.0)
        e.memset(R["ones_k1f"][:], 1.0)
        instr = None
        for v, c in R["cbias_vals"].items():
            instr = e.memset(R["cbias"][:, c:c + 1], float(v))
        ctx.inc("evp", instr)

    @blk.scalar
    def _(e):
        ctx.wait(e, ctx.tok("ld", "evp"))
        ctx.inc("ln_done", e.nop())
    return ctx.tok("ld", "evp")


def emit_combine(ctx, R, t):
    blk = ctx.blk
    t0 = t * 512
    s_fm, g_fm = FM(R["sbuf_s"], DC, 512), FM(R["sbuf_g"], DC, 512)
    x_fm = FM(R["x"], DC, T)

    tmp_ap = R["arena_c"][:, 2048:2560]
    @blk.vector
    def _(e):
        ctx.wait(e, ctx.tok("evs", "evv", "ldg2"))
        for c in range(DC):
            ctx.inc("evv", e.tensor_tensor(tmp_ap, s_fm.ch(c), g_fm.ch(c),
                                           ALU.mult))
            ctx.wait(e, ctx.tok("evv"))
            ctx.inc("evv", e.tensor_tensor(x_fm.ch(c, t0, 512),
                                           x_fm.ch(c, t0, 512), tmp_ap, ALU.add))
            ctx.wait(e, ctx.tok("evv"))
        ctx.inc("comb", e.nop())
        ctx.inc("ln_done", e.nop())
    return ctx.tok("comb")


def emit_combine_spill(ctx, R, g_spill, s_spill):
    blk = ctx.blk
    s_fm = FM(R["sbuf_s"], DC, 512)
    g_fm = FM(R["sbuf_g"], DC, 512)
    for t in range(TC):
        t0 = t * 512
        @blk.gpsimd
        def _(e, t0=t0):
            ctx.wait(e, ctx.tok("comb") + ctx.tok("mm", "evv", "evs", "ldg2"))
            for c in range(DC):
                ctx.inc("ldg2", e.dma_start(
                    g_fm.ch(c), g_spill[:, c * T + t0: c * T + t0 + 512]), 16)
                ctx.inc("ldg2", e.dma_start(
                    s_fm.ch(c), s_spill[:, c * T + t0: c * T + t0 + 512]), 16)
        ld_tok = ctx.tok("ldg2")
        @blk.vector
        def _(e, ld_tok=ld_tok):
            ctx.wait(e, ld_tok)
            ctx.inc("evv", e.nop())
        emit_combine(ctx, R, t)


def l0_prep(I, m):
    m["l0_wq"] = np.ascontiguousarray(I["L0_attn_Wq"])
    m["l0_wk"] = np.ascontiguousarray(I["L0_attn_Wk"])
    m["l0_wv"] = np.ascontiguousarray(I["L0_attn_Wv"])
    m["l0_wo"] = np.ascontiguousarray(I["L0_attn_Wo"])
    m["l0_wr"] = np.ascontiguousarray(I["L0_ffn_Wr"])
    m["l0_br"] = np.asarray(I["L0_ffn_br"], np.float32).reshape(1, 8)
    m["l0_lng"] = _col_layout(I["L0_ffn_ln_g"], DC)
    m["l0_lnb"] = _col_layout(I["L0_ffn_ln_b"], DC)
    m["l0_w1"] = np.ascontiguousarray(I["L0_ffn_W1"])
    m["l0_b1"] = np.stack([_col_layout(I["L0_ffn_b1"][e], 32) for e in range(8)])
    m["l0_w2"] = np.ascontiguousarray(I["L0_ffn_W2"])
    m["l0_b2"] = np.asarray(I["L0_ffn_b2"], np.float32)


def l0_declare(nc, d):
    def din(name, shape, dt=F32):
        d[name] = nc.dram_tensor(name, shape, dt, kind="ExternalInput").ap()
    din("l0_wq", [1024, 1024]); din("l0_wk", [1024, 1024])
    din("l0_wv", [1024, 1024]); din("l0_wo", [1024, 1024])
    din("l0_wr", [1024, 8]); din("l0_br", [1, 8])
    din("l0_lng", [128, DC]); din("l0_lnb", [128, DC])
    din("l0_w1", [8, 1024, 4096]); din("l0_b1", [8, 128, 32])
    din("l0_w2", [8, 4096, 1024]); din("l0_b2", [8, 1024])


def emit_l0(ctx, R, d, g_spill, dbg=None):
    blk = ctx.blk
    nc = ctx.nc
    x_fm = FM(R["x"], DC, T)
    xn_fm = FM(R["xn"], DC, T)

    smf, smb = R["smf"], R["smb"]
    lng = smf[0:128, 0:8]
    lnb = smf[0:128, 8:16]
    b1_t = smf[0:128, 16:48]
    br_t = smf[0:1, 48:56]
    brB = smf[0:128, 56:64]
    wr_t = smb[0:128, 0:64]
    b2_t = smb[0:8, 64:1088]
    sel_v = smb[0:8, 1088:2112]

    @blk.sync
    def _(e):
        ctx.wait(e, ctx.tok("comb"))
        for src, dst in [("l0_lng", lng), ("l0_lnb", lnb), ("l0_br", br_t),
                         ("sel", sel_v)]:
            instr = e.dma_start(dst, d[src][:])
            ctx.inc("ld", instr, 16)
    @blk.gpsimd
    def _(e):
        instr = e.dma_start(b2_t, d["l0_b2"][:])
        ctx.inc("ldg", instr, 16)
    load_w(ctx, d["l0_wr"], wr_t)
    small_tok = ctx.tok("ld", "ldg")

    ln_tok = emit_ln(ctx, R, x_fm, 0, T, xn_fm, 0, rms=False, in_f32=True)

    @blk.tensor
    def _(e):
        ctx.wait(e, small_tok + ln_tok)
        instr = e.matmul(R["pb"][5][:, 0:8], R["ones_k1f"][:], br_t,
                         start=True, stop=True)
        ctx.inc("mm", instr)
    @blk.vector
    def _(e):
        ctx.wait(e, ctx.tok("mm"))
        instr = e.tensor_copy(brB, R["pb"][5][:, 0:8])
        ctx.inc("evv", instr)

    if dbg and "dbg_xn" in dbg:
        @blk.gpsimd
        def _(e):
            ctx.wait(e, ln_tok)
            instr = e.dma_start(dbg["dbg_xn"].rearrange("(c p) t -> p c t", p=128),
                                R["xn"][:].rearrange("p (c t) -> p c t", t=T))
            ctx.inc("out", instr, 16)

    h_fm = FM(R["arena_d"], 16, 512)
    y_fm = FM(R["arena_c"], DC, 512)
    wb1 = R["arena_b"]
    wb2 = R["arena_a"]
    xn2_fm = FM(R["xn2"], DC, 512)

    for t in range(TC):
        t0 = t * 512
        emit_ln(ctx, R, xn_fm, t0, 512, xn2_fm, 0, rms=False, in_f32=False,
                gain_col=lambda c: lng[:, c:c + 1],
                bias_col=lambda c: lnb[:, c:c + 1])
        xn2_tok = ctx.tok("ln_done")

        rjobs = []
        for q in range(4):
            rjobs.append(dict(
                pre=small_tok + ctx.tok("evv"),
                mms=mm_acc([(xn_fm.ch(c, t0 + q * 128, 128),
                             wr_t[:, c * 8:(c + 1) * 8]) for c in range(DC)]),
                psum=lambda b: b[:, 0:8],
                evac=(lambda q: lambda e, ps: e.tensor_tensor(
                    R["ltm"][:, q * 8:(q + 1) * 8], ps, brB, ALU.add))(q),
            ))
        sweep(ctx, jobs=rjobs, banks=R["pb"][0:2], evac_eng="vector",
              entry=ctx.tok("evs", "evv"))

        cols = R["cols"]
        for q in range(4):
            lg = R["ltm"][:, q * 8:(q + 1) * 8]
            @blk.vector
            def _(e, q=q, lg=lg):
                ctx.wait(e, ctx.tok("evs"))
                instr = e.tensor_reduce(cols[:, 0:1], lg, AX.X, ALU.max,
                                        negate=True)
                ctx.inc("evv", instr)
            @blk.scalar
            def _(e, q=q, lg=lg):
                ctx.wait(e, ctx.tok("evv"))
                instr = e.activation(lg, lg, AF.Exp, bias=cols[:, 0:1])
                ctx.inc("evs", instr)
            @blk.vector
            def _(e, q=q, lg=lg):
                ctx.wait(e, ctx.tok("evs"))
                ctx.inc("evv", e.tensor_reduce(cols[:, 1:2], lg, AX.X, ALU.add))
                ctx.inc("evv", e.tensor_reduce(cols[:, 3:4], lg, AX.X, ALU.max))
                ctx.wait(e, ctx.tok("evv"))
                ctx.inc("evv", e.reciprocal(cols[:, 2:3], cols[:, 1:2]))
                ctx.wait(e, ctx.tok("evv"))
                ctx.inc("evv", e.tensor_tensor(cols[:, 4:5], cols[:, 3:4],
                                               cols[:, 2:3], ALU.mult))
                ctx.wait(e, ctx.tok("evv"))
                instr = e.tensor_scalar(lg, lg, cols[:, 3:4], cols[:, 4:5],
                                        ALU.is_ge, ALU.mult)
                ctx.inc("evv", instr)
            @blk.scalar
            def _(e, q=q):
                ctx.wait(e, ctx.tok("evv"))
                ctx.inc("evs", e.nop())

        @blk.vector
        def _(e):
            ctx.wait(e, ctx.tok("evs"))
            instr = e.tensor_copy(R["ltm_bf"][:], R["ltm"][:])
            ctx.inc("evv", instr)
        tjobs = []
        for q in range(4):
            tjobs.append(dict(
                pre=ctx.tok("evv"),
                mms=[(lambda q: lambda e, ps: e.transpose(
                    ps, R["ltm_bf"][:, q * 8:(q + 1) * 8], R["iden_b"][:]))(q)],
                psum=lambda b: b[0:8, 0:64].bitcast(BF16),
                evac=(lambda q: lambda e, ps: e.tensor_copy(
                    R["cfm"][:, q * 128: (q + 1) * 128], ps))(q),
            ))
        sweep(ctx, jobs=tjobs, banks=R["pb"][0:2], evac_eng="vector",
              entry=ctx.tok("evs", "evv"))
        cfm_tok = ctx.tok("evv")
        if dbg and "dbg_coeff" in dbg:
            @blk.gpsimd
            def _(e, t=t):
                ctx.wait(e, cfm_tok)
                instr = e.dma_start(dbg["dbg_coeff"][0:8, t * 512:(t + 1) * 512],
                                    R["cfm"][0:8, 0:512])
                ctx.inc("out", instr, 16)
                instr = e.dma_start(dbg["dbg_coeff"][8:8 + 128,
                                                     t * 32:(t + 1) * 32],
                                    R["ltm"][:])
                ctx.inc("out", instr, 16)

        bjobs = []
        for mch in range(DC):
            bjobs.append(dict(
                pre=cfm_tok + small_tok,
                mms=mm_acc([(b2_t[0:8, mch * 128:(mch + 1) * 128],
                             R["cfm"][0:8, 0:512])]),
                psum=lambda b: b[:, :],
                evac=(lambda mch: lambda e, ps: e.tensor_copy(
                    y_fm.ch(mch), ps))(mch),
            ))
        sweep(ctx, jobs=bjobs, banks=R["pb"][0:2], evac_eng="vector",
              entry=ctx.tok("evs", "evv"))

        for ex in range(8):
            cjobs = [dict(
                pre=cfm_tok,
                mms=mm_acc([(sel_v[0:8, ex * 128:(ex + 1) * 128],
                             R["cfm"][0:8, 0:512])]),
                psum=lambda b: b[:, :],
                evac=lambda e, ps: e.tensor_copy(R["coeffB"], ps),
            )]
            sweep(ctx, jobs=cjobs, banks=[R["pb"][2]], evac_eng="vector",
                  entry=ctx.tok("evs", "evv"))
            coefb_tok = ctx.tok("evv")

            @blk.sync
            def _(e, ex=ex):
                ctx.wait(e, ctx.tok("evs"))
                instr = e.dma_start(b1_t, d["l0_b1"][ex])
                ctx.inc("ld", instr, 16)
            b1_tok = ctx.tok("ld")

            for fc in range(4):
                f0 = fc * 1024
                slot = fc % 2
                sl_w1 = wb2[:, slot * 8192: slot * 8192 + 8192]
                w1_tok = load_w(ctx, d["l0_w1"][ex][:, f0:f0 + 1024], wb2,
                                col0=slot * 8192,
                                pre=ctx.buf_pre(("l0w", slot)))
                hjobs = []
                for fs in range(8):
                    hjobs.append(dict(
                        pre=w1_tok + xn2_tok + b1_tok,
                        mms=mm_acc([(sl_w1[:, c * 1024 + fs * 128:
                                          c * 1024 + (fs + 1) * 128],
                                     xn2_fm.ch(c)) for c in range(DC)]),
                        psum=lambda b: b[:, :],
                        evac=(lambda fc, fs: lambda e, ps:
                              e.activation(h_fm.ch(fs), ps, AF.Relu,
                                           bias=b1_t[:, fc * 8 + fs:
                                                     fc * 8 + fs + 1]))(fc, fs),
                    ))
                sweep(ctx, jobs=hjobs, banks=R["pb"][0:4], evac_eng="scalar",
                      entry=ctx.tok("evs", "evv"))
                ctx.buf_mark(("l0w", slot))
                hc_tok = ctx.tok("evs")
                slot2 = (fc + 1) % 2
                sl_w2 = wb2[:, slot2 * 8192: slot2 * 8192 + 8192]
                w2_tok = load_w(ctx, d["l0_w2"][ex][f0:f0 + 1024, :], wb2,
                                col0=slot2 * 8192,
                                pre=ctx.buf_pre(("l0w", slot2)))
                ytmp = R["arena_b"].bitcast(F32)
                yjobs = []
                for mch in range(DC):
                    def mk_yev(mch=mch):
                        def f(e, ps):
                            tv = ytmp[:, (mch % 2) * 512:(mch % 2) * 512 + 512]
                            e.tensor_tensor(tv, ps, R["coeffB"], ALU.mult)
                            return e.tensor_tensor(y_fm.ch(mch), y_fm.ch(mch),
                                                   tv, ALU.add)
                        return f
                    yjobs.append(dict(
                        pre=w2_tok + hc_tok + coefb_tok,
                        mms=mm_acc([(sl_w2[:, k * 1024 + mch * 128:
                                          k * 1024 + (mch + 1) * 128],
                                     h_fm.ch(k)) for k in range(DC)]),
                        psum=lambda b: b[:, :],
                        evac=mk_yev(),
                    ))
                sweep(ctx, jobs=yjobs, banks=R["pb"][0:4], evac_eng="vector",
                      entry=ctx.tok("evs", "evv"))
                ctx.buf_mark(("l0w", slot2))

        y_tok = ctx.tok("evv")
        if dbg and "dbg_y" in dbg:
            @blk.gpsimd
            def _(e, t=t, y_tok=y_tok):
                ctx.wait(e, y_tok)
                for c in range(DC):
                    instr = e.dma_start(
                        dbg["dbg_y"][c * 128:(c + 1) * 128, t * 512:(t + 1) * 512],
                        y_fm.ch(c))
                    ctx.inc("out", instr, 16)
        @blk.scalar
        def _(e, t=t):
            ctx.wait(e, y_tok + ctx.tok("ldg2"))
            instr = None
            for mch in range(DC):
                instr = e.activation(FM(R["sbuf_g"], DC, 512).ch(mch),
                                     y_fm.ch(mch), AF.Silu)
            ctx.inc("evs", instr)
        @blk.gpsimd
        def _(e, t=t):
            ctx.wait(e, ctx.tok("evs"))
            for mch in range(DC):
                instr = e.dma_start(
                    g_spill[:, mch * T + t0: mch * T + t0 + 512],
                    FM(R["sbuf_g"], DC, 512).ch(mch))
                ctx.inc("ldg2", instr, 16)

    if dbg and "dbg_g" in dbg:
        @blk.gpsimd
        def _(e):
            ctx.wait(e, ctx.tok("ldg2"))
            instr = e.dma_start(dbg["dbg_g"].rearrange("(c p) t -> p c t", p=128),
                                g_spill[:].rearrange("p (c t) -> p c t", t=T))
            ctx.inc("out", instr, 16)

    o_all = FM(R["arena_a"], DC, T)
    attnT = R["arena_b"]
    ad = R["arena_d"]
    QO, KO, VO, AO = 0, 2048, 4096, 6144
    sc_row = R["arena_c"]
    wsl = R["sq"].bitcast(BF16)

    for hg in range(8):
        load_w(ctx, d["l0_wq"][:, hg * 128:(hg + 1) * 128], wsl, col0=0,
               pre=ctx.tok("mm", "statmm", "ln_done"))
        load_w(ctx, d["l0_wk"][:, hg * 128:(hg + 1) * 128], wsl, col0=1024)
        load_w(ctx, d["l0_wv"][:, hg * 128:(hg + 1) * 128], wsl, col0=2048)
        w_tok = ctx.tok("ldg")

        qk_jobs = []
        for dst_col0, wcol in [(QO, 0), (KO, 1024)]:
            for tb in range(4):
                qk_jobs.append(dict(
                    pre=w_tok,
                    mms=mm_acc([(wsl[:, wcol + c * 128: wcol + (c + 1) * 128],
                                 xn_fm.ch(c, tb * 512, 512)) for c in range(DC)]),
                    psum=lambda b: b[:, :],
                    evac=(lambda dst_col0, tb: lambda e, ps: e.copy(
                        ad[:, dst_col0 + tb * 512: dst_col0 + (tb + 1) * 512], ps))(
                            dst_col0, tb),
                ))
        for ts_ in range(16):
            qk_jobs.append(dict(
                pre=w_tok,
                mms=mm_acc([(xn_fm.ch(c, ts_ * 128, 128),
                             wsl[:, 2048 + c * 128: 2048 + (c + 1) * 128])
                            for c in range(DC)]),
                psum=lambda b: b[:, 0:128],
                evac=(lambda ts_: lambda e, ps: e.copy(
                    ad[:, VO + ts_ * 128: VO + (ts_ + 1) * 128], ps))(ts_),
            ))
        sweep(ctx, jobs=qk_jobs, banks=R["pb"][0:3], evac_eng="scalar",
              entry=ctx.tok("evs", "evv"))
        qkv_tok = ctx.tok("evs")

        for h2 in range(2):
            base = 64 * h2
            for tq in range(4):
                for qs in range(4):
                    q0 = tq * 512 + qs * 128
                    L = q0 + 128
                    sjobs = []
                    nblk = (L + 511) // 512
                    for kb in range(nblk):
                        k0 = kb * 512
                        w = min(512, L - k0)
                        has_diag = (k0 + w == L)
                        def mk_evac(k0=k0, w=w, has_diag=has_diag, L=L):
                            def f(e, ps):
                                if has_diag:
                                    if w > 128:
                                        e.tensor_single_scalar(
                                            sc_row[:, k0:k0 + w - 128],
                                            ps[:, 0:w - 128], 0.125, ALU.mult)
                                    return e.scalar_tensor_tensor(
                                        sc_row[:, L - 128:L], ps[:, w - 128:w],
                                        0.125, R["causal_t"][:], ALU.mult, ALU.add)
                                return e.tensor_single_scalar(
                                    sc_row[:, k0:k0 + w], ps[:, 0:w], 0.125,
                                    ALU.mult)
                            return f
                        sjobs.append(dict(
                            pre=qkv_tok,
                            mms=[(lambda k0=k0, w=w, q0=q0, base=base:
                                  lambda e, ps: e.matmul(
                                      ps[:, 0:w],
                                      ad[base:base + 64, QO + q0:QO + q0 + 128],
                                      ad[base:base + 64, KO + k0:KO + k0 + w],
                                      start=True, stop=True))()],
                            psum=lambda b: b,
                            evac=mk_evac(),
                        ))
                    sweep(ctx, jobs=sjobs, banks=R["pb"][0:2], evac_eng="vector",
                          entry=ctx.tok("evs", "evv"))
                    sc_tok = ctx.tok("evv")

                    @blk.vector
                    def _(e, L=L):
                        ctx.wait(e, sc_tok)
                        instr = e.tensor_reduce(R["cols"][:, 0:1],
                                                sc_row[:, 0:L], AX.X, ALU.max,
                                                negate=True)
                        ctx.inc("evv", instr)
                    @blk.scalar
                    def _(e, L=L):
                        ctx.wait(e, ctx.tok("evv"))
                        instr = e.activation(ad[:, AO:AO + L], sc_row[:, 0:L],
                                             AF.Exp, bias=R["cols"][:, 0:1])
                        ctx.inc("evs", instr)
                    @blk.vector
                    def _(e, L=L):
                        ctx.wait(e, ctx.tok("evs"))
                        ctx.inc("evv", e.tensor_reduce(
                            R["cols"][:, 1:2], ad[:, AO:AO + L], AX.X, ALU.add))
                        ctx.wait(e, ctx.tok("evv"))
                        ctx.inc("evv", e.reciprocal(R["cols"][:, 2:3],
                                                    R["cols"][:, 1:2]))
                        ctx.wait(e, ctx.tok("evv"))
                        instr = e.tensor_scalar_mul(ad[:, AO:AO + L],
                                                    ad[:, AO:AO + L],
                                                    R["cols"][:, 2:3])
                        ctx.inc("evv", instr)
                    sm_tok = ctx.tok("evv")

                    tjobs = []
                    for tkc in range(L // 128):
                        tjobs.append(dict(
                            pre=sm_tok,
                            mms=[(lambda tkc: lambda e, ps: e.transpose(
                                ps, ad[:, AO + tkc * 128:AO + (tkc + 1) * 128],
                                R["iden_b"][:]))(tkc)],
                            psum=lambda b: b[:, 0:64].bitcast(BF16),
                            evac=(lambda tkc: lambda e, ps: e.tensor_copy(
                                attnT[:, tkc * 512 + qs * 128:
                                      tkc * 512 + qs * 128 + 128], ps))(tkc),
                        ))
                    sweep(ctx, jobs=tjobs, banks=R["pb"][2:4], evac_eng="vector",
                          entry=ctx.tok("evs", "evv"))

                at_tok = ctx.tok("evv")
                nk = 4 * (tq + 1)
                avmms = []
                for tkc in range(nk):
                    noff = max(0, (tkc - 4 * tq)) * 128
                    def mk(tkc=tkc, noff=noff, first=(tkc == 0), last=(tkc == nk - 1),
                           base=base):
                        def f(e, ps):
                            return e.matmul(
                                ps[0:64, noff:512],
                                ad[:, VO + tkc * 128 + base:
                                   VO + tkc * 128 + base + 64],
                                attnT[:, tkc * 512 + noff: (tkc + 1) * 512],
                                start=first, stop=last)
                        return f
                    avmms.append(mk())
                avjobs = [dict(
                    pre=at_tok,
                    mms=avmms,
                    psum=lambda b: b,
                    evac=(lambda hg=hg, h2=h2, tq=tq: lambda e, ps: e.copy(
                        o_all.ch(hg, tq * 512, 512, p0=64 * h2, pw=64),
                        ps[0:64, :]))(),
                )]
                sweep(ctx, jobs=avjobs, banks=[R["pb"][4]], evac_eng="scalar",
                      entry=ctx.tok("evs", "evv"))

    wo_t = R["arena_d"]
    wo_tok = load_w(ctx, d["l0_wo"], wo_t, pre=ctx.tok("mm"))
    s_fm = FM(R["sbuf_s"], DC, 512)
    g_fm = FM(R["sbuf_g"], DC, 512)
    for t in range(TC):
        t0 = t * 512
        @blk.gpsimd
        def _(e, t=t):
            ctx.wait(e, ctx.tok("comb"))
            for c in range(DC):
                instr = e.dma_start(g_fm.ch(c),
                                    g_spill[:, c * T + t0: c * T + t0 + 512])
                ctx.inc("ldg2", instr, 16)
        g_tok = ctx.tok("ldg2")

        sjobs = []
        for mch in range(DC):
            sjobs.append(dict(
                pre=wo_tok + ctx.tok("evv", "evs"),
                mms=mm_acc([(wo_t[:, c * 1024 + mch * 128: c * 1024 + (mch + 1) * 128],
                             o_all.ch(c, t0, 512)) for c in range(DC)]),
                psum=lambda b: b[:, :],
                evac=(lambda mch, t0=t0: lambda e, ps: e.tensor_tensor(
                    s_fm.ch(mch), ps, xn_fm.ch(mch, t0, 512), ALU.add))(mch),
            ))
        sweep(ctx, jobs=sjobs, banks=R["pb"][5:7], evac_eng="vector",
              entry=ctx.tok("evs", "evv"))
        if dbg and "dbg_s" in dbg:
            @blk.gpsimd
            def _(e, t=t):
                ctx.wait(e, ctx.tok("evv"))
                for c in range(DC):
                    instr = e.dma_start(
                        dbg["dbg_s"][c * 128:(c + 1) * 128, t0:t0 + 512],
                        s_fm.ch(c))
                    ctx.inc("out", instr, 16)
        @blk.vector
        def _(e, t=t):
            ctx.wait(e, g_tok)
            ctx.inc("evv", e.nop())
        emit_combine(ctx, R, t)




def l1_prep(I, m):
    m["l1_ln1"] = _col_layout(I["L1_attn_ln1_w"], DC)
    m["l1_ln2"] = _col_layout(I["L1_attn_ln2_w"], DC)
    m["l1_winT"] = np.ascontiguousarray(np.asarray(I["L1_attn_w_in"]).T)
    m["l1_wgateT"] = np.ascontiguousarray(np.asarray(I["L1_attn_w_gate"]).T)
    m["l1_aparam"] = _col_layout(I["L1_attn_a_param"], 16)
    m["l1_waT"] = np.ascontiguousarray(np.asarray(I["L1_attn_w_a"]).T)
    m["l1_woutT"] = np.ascontiguousarray(np.asarray(I["L1_attn_w_out"]).T)
    m["l1_wmgT"] = np.ascontiguousarray(np.asarray(I["L1_attn_w_mlp_gate"]).T)
    m["l1_wmuT"] = np.ascontiguousarray(np.asarray(I["L1_attn_w_mlp_up"]).T)
    m["l1_wmdT"] = np.ascontiguousarray(np.asarray(I["L1_attn_w_mlp_down"]).T)
    m["l1_fwgT"] = np.ascontiguousarray(np.asarray(I["L1_ffn_w_gate"]).T)
    m["l1_fwuT"] = np.ascontiguousarray(np.asarray(I["L1_ffn_w_up"]).T)
    m["l1_fwdT"] = np.ascontiguousarray(np.asarray(I["L1_ffn_w_down"]).T)


def l1_declare(nc, d):
    def din(name, shape, dt=F32):
        d[name] = nc.dram_tensor(name, shape, dt, kind="ExternalInput").ap()
    din("l1_ln1", [128, DC]); din("l1_ln2", [128, DC])
    din("l1_winT", [1024, 2048]); din("l1_wgateT", [1024, 2048])
    din("l1_aparam", [128, 16]); din("l1_waT", [2048, 2048])
    din("l1_woutT", [2048, 1024])
    din("l1_wmgT", [1024, 4096]); din("l1_wmuT", [1024, 4096])
    din("l1_wmdT", [4096, 1024])
    din("l1_fwgT", [1024, 3072], I32); din("l1_fwuT", [1024, 3072], I32)
    din("l1_fwdT", [3072, 1024], I32)


def emit_l1(ctx, R, d, g_spill, s_spill, x_spill, dbg=None):
    blk = ctx.blk
    nc = ctx.nc
    x_fm = FM(R["x"], DC, T)
    xn_fm = FM(R["xn"], DC, T)

    smf = R["smf"]
    ln1 = smf[0:128, 0:8]
    ln2 = smf[0:128, 8:16]
    apar = smf[0:128, 16:32]
    state = smf[0:128, 32:48]

    @blk.sync
    def _(e):
        ctx.wait(e, ctx.tok("comb", "evv", "evs"))
        for src, dst in [("l1_ln1", ln1), ("l1_ln2", ln2), ("l1_aparam", apar)]:
            ctx.inc("ld", e.dma_start(dst, d[src][:]), 16)
    small_tok = ctx.tok("ld")

    emit_ln(ctx, R, x_fm, 0, T, xn_fm, 0, rms=False, in_f32=True)
    if dbg and "dbg_xn" in dbg:
        @blk.gpsimd
        def _(e):
            ctx.wait(e, ctx.tok("ln_done"))
            ctx.inc("out", e.dma_start(
                dbg["dbg_xn"].rearrange("(c p) t -> p c t", p=128),
                R["xn"][:].rearrange("p (c t) -> p c t", t=T)), 16)

    wbuf = R["arena_a"]
    xin_fm = FM(R["arena_d"], 16, 256)
    a_fm = FM(R["arena_d"], 16, 256, col0=4096)
    gt_fm = FM(R["arena_b"], 16, 256)
    y_fm = FM(R["arena_b"], 16, 256, col0=4096)
    apart_fm = FM(R["arena_c"], 16, 256)
    h1_fm = FM(R["xn2"], DC, 512)
    sq = R["sq"]

    @blk.gpsimd
    def _(e):
        ctx.wait(e, ctx.tok("comb", "ldg2"))
        ctx.inc("evp", e.memset(state, 0.0))

    for tm in range(8):
        t0 = tm * 256
        if tm % 2 == 0:
            emit_ln(ctx, R, xn_fm, t0, 512, h1_fm, 0, rms=True, eps=1e-6,
                    in_f32=False, gain_col=lambda c: ln1[:, c:c + 1])
        h1off = (tm % 2) * 256
        h1_tok = ctx.tok("ln_done")

        for wname, dst_fm, act in [("l1_winT", xin_fm, None),
                                   ("l1_wgateT", gt_fm, AF.Sigmoid)]:
            w_tok = load_w(ctx, d[wname], wbuf, pre=ctx.tok("mm"))
            jobs = []
            for mch in range(16):
                def mk_ev(dst_fm=dst_fm, act=act, mch=mch):
                    if act is None:
                        return lambda e, ps: e.tensor_copy(dst_fm.ch(mch), ps[:, 0:256])
                    return lambda e, ps: e.activation(dst_fm.ch(mch), ps[:, 0:256], act)
                jobs.append(dict(
                    pre=w_tok + h1_tok,
                    mms=mm_acc([(wbuf[:, c * 2048 + mch * 128:
                                      c * 2048 + (mch + 1) * 128],
                                 h1_fm.ch(c, h1off, 256)) for c in range(DC)]),
                    psum=lambda b: b[:, 0:256],
                    evac=mk_ev(),
                ))
            ev = "scalar" if act else "vector"
            sweep(ctx, jobs=jobs, banks=R["pb"][0:4], evac_eng=ev,
                  entry=ctx.tok("evs", "evv"))
        xin_tok = ctx.tok("evv", "evs")

        for kh in range(2):
            w_tok = load_w(ctx, d["l1_waT"][kh * 1024:(kh + 1) * 1024, :], wbuf,
                           pre=ctx.tok("mm"))
            jobs = []
            for mch in range(16):
                def mk_ev(kh=kh, mch=mch):
                    if kh == 0:
                        return lambda e, ps: e.tensor_copy(apart_fm.ch(mch), ps[:, 0:256])
                    return lambda e, ps: e.tensor_tensor(
                        apart_fm.ch(mch), apart_fm.ch(mch), ps[:, 0:256], ALU.add)
                jobs.append(dict(
                    pre=w_tok + xin_tok,
                    mms=mm_acc([(wbuf[:, c * 2048 + mch * 128:
                                      c * 2048 + (mch + 1) * 128],
                                 xin_fm.ch(c + kh * 8)) for c in range(DC)]),
                    psum=lambda b: b[:, 0:256],
                    evac=mk_ev(),
                ))
            sweep(ctx, jobs=jobs, banks=R["pb"][0:4], evac_eng="vector",
                  entry=ctx.tok("evs", "evv"))
        ap_tok = ctx.tok("evv")

        @blk.scalar
        def _(e, ap_tok=ap_tok):
            ctx.wait(e, ap_tok + small_tok)
            instr = None
            for c in range(16):
                instr = e.activation(a_fm.ch(c), apart_fm.ch(c), AF.Sigmoid,
                                     bias=apar[:, c:c + 1])
            ctx.inc("evs", instr)
        a_tok = ctx.tok("evs")

        for c in range(16):
            s1 = sq[:, (c % 4) * 512:(c % 4) * 512 + 256]
            s2 = sq[:, (c % 4) * 512 + 256:(c % 4) * 512 + 512]
            @blk.scalar
            def _(e, c=c, s1=s1, s2=s2, a_tok=a_tok):
                ctx.wait(e, a_tok + ctx.tok("scan"))
                e.activation(s1, a_fm.ch(c), AF.Square)
                instr = e.activation(s2, s1, AF.Sqrt, scale=-1.0,
                                     bias=cb(R, 1.0 + 1e-8))
                ctx.inc("evs", instr)
            @blk.vector
            def _(e, c=c, s2=s2):
                ctx.wait(e, ctx.tok("evs"))
                ctx.inc("evv", e.tensor_tensor(xin_fm.ch(c), s2, xin_fm.ch(c),
                                               ALU.mult))
                ctx.wait(e, ctx.tok("evv", "scan"))
                ctx.inc("scan", e.tensor_tensor_scan(
                    y_fm.ch(c), a_fm.ch(c), xin_fm.ch(c),
                    state[:, c:c + 1], ALU.mult, ALU.add))
                ctx.wait(e, ctx.tok("scan"))
                ctx.inc("scan", e.tensor_copy(state[:, c:c + 1],
                                              y_fm.ch(c, 255, 1)))
                ctx.inc("evv", e.tensor_tensor(y_fm.ch(c), y_fm.ch(c),
                                               gt_fm.ch(c), ALU.mult))
        scan_tok = ctx.tok("evv", "scan")

        w_tok = load_w(ctx, d["l1_woutT"], wbuf, pre=ctx.tok("mm"))
        xp_fm = FM(R["sbuf_s"], DC, 256)
        jobs = []
        for mch in range(DC):
            jobs.append(dict(
                pre=w_tok + scan_tok,
                mms=mm_acc([(wbuf[:, k * 1024 + mch * 128:
                                  k * 1024 + (mch + 1) * 128],
                             y_fm.ch(k)) for k in range(16)]),
                psum=lambda b: b[:, 0:256],
                evac=(lambda mch, t0=t0: lambda e, ps: e.tensor_tensor(
                    xp_fm.ch(mch), ps[:, 0:256], xn_fm.ch(mch, t0, 256),
                    ALU.add))(mch),
            ))
        sweep(ctx, jobs=jobs, banks=R["pb"][4:6], evac_eng="vector",
              entry=ctx.tok("evs", "evv"))
        @blk.gpsimd
        def _(e, t0=t0):
            ctx.wait(e, ctx.tok("evv"))
            for c in range(DC):
                ctx.inc("ldg2", e.dma_start(
                    x_spill[:, c * T + t0: c * T + t0 + 256], xp_fm.ch(c)), 16)

    xp_fm = FM(R["arena_d"], DC, 512)
    h2_fm = FM(R["xn2"], DC, 512)
    mg_fm = FM(R["arena_b"], 16, 512)
    acc_fm = FM(R["arena_c"], DC, 512)
    for t in range(TC):
        t0 = t * 512
        @blk.gpsimd
        def _(e, t0=t0):
            ctx.wait(e, ctx.tok("mm", "evv", "ldg2"))
            for c in range(DC):
                ctx.inc("ldg2", e.dma_start(
                    xp_fm.ch(c), x_spill[:, c * T + t0: c * T + t0 + 512]), 16)
        xp_tok = ctx.tok("ldg2")
        @blk.scalar
        def _(e, xp_tok=xp_tok):
            ctx.wait(e, xp_tok)
            ctx.inc("ln_done", e.nop())
        emit_ln(ctx, R, xp_fm, 0, 512, h2_fm, 0, rms=True, eps=1e-6,
                in_f32=False, gain_col=lambda c: ln2[:, c:c + 1])
        h2_tok = ctx.tok("ln_done")

        for fh in range(2):
            w_tok = load_w(ctx, d["l1_wmgT"][:, fh * 2048:(fh + 1) * 2048],
                           wbuf, pre=ctx.tok("mm"))
            jobs = []
            for mch in range(16):
                jobs.append(dict(
                    pre=w_tok + h2_tok,
                    mms=mm_acc([(wbuf[:, c * 2048 + mch * 128:
                                      c * 2048 + (mch + 1) * 128],
                                 h2_fm.ch(c)) for c in range(DC)]),
                    psum=lambda b: b[:, :],
                    evac=(lambda mch: lambda e, ps: e.activation(
                        mg_fm.ch(mch), ps, AF.Silu))(mch),
                ))
            sweep(ctx, jobs=jobs, banks=R["pb"][0:4], evac_eng="scalar",
                  entry=ctx.tok("evs", "evv"))
            w_tok = load_w(ctx, d["l1_wmuT"][:, fh * 2048:(fh + 1) * 2048],
                           wbuf, pre=ctx.tok("mm"))
            sg_tok = ctx.tok("evs")
            jobs = []
            for mch in range(16):
                jobs.append(dict(
                    pre=w_tok + sg_tok,
                    mms=mm_acc([(wbuf[:, c * 2048 + mch * 128:
                                      c * 2048 + (mch + 1) * 128],
                                 h2_fm.ch(c)) for c in range(DC)]),
                    psum=lambda b: b[:, :],
                    evac=(lambda mch: lambda e, ps: e.tensor_tensor(
                        mg_fm.ch(mch), ps, mg_fm.ch(mch), ALU.mult))(mch),
                ))
            sweep(ctx, jobs=jobs, banks=R["pb"][0:4], evac_eng="vector",
                  entry=ctx.tok("evs", "evv"))
            mm_tok = ctx.tok("evv")
            w_tok = load_w(ctx, d["l1_wmdT"][fh * 2048:(fh + 1) * 2048, :],
                           wbuf, pre=ctx.tok("mm"))
            jobs = []
            for mch in range(DC):
                def mk_ev(fh=fh, mch=mch):
                    if fh == 0:
                        return lambda e, ps: e.tensor_copy(acc_fm.ch(mch), ps)
                    return lambda e, ps: e.tensor_tensor(
                        acc_fm.ch(mch), acc_fm.ch(mch), ps, ALU.add)
                jobs.append(dict(
                    pre=w_tok + mm_tok,
                    mms=mm_acc([(wbuf[:, k * 1024 + mch * 128:
                                      k * 1024 + (mch + 1) * 128],
                                 mg_fm.ch(k)) for k in range(16)]),
                    psum=lambda b: b[:, :],
                    evac=mk_ev(),
                ))
            sweep(ctx, jobs=jobs, banks=R["pb"][0:4], evac_eng="vector",
                  entry=ctx.tok("evs", "evv"))
        s_tok = ctx.tok("evv")
        @blk.vector
        def _(e, t0=t0, s_tok=s_tok):
            ctx.wait(e, s_tok)
            instr = None
            for c in range(DC):
                instr = e.tensor_tensor(h2_fm.ch(c), acc_fm.ch(c), xp_fm.ch(c),
                                        ALU.add)
            ctx.inc("evv", instr)
        @blk.gpsimd
        def _(e, t0=t0):
            ctx.wait(e, ctx.tok("evv"))
            for c in range(DC):
                ctx.inc("ldg2", e.dma_start(
                    s_spill[:, c * T + t0: c * T + t0 + 512], h2_fm.ch(c)), 16)

    if dbg and "dbg_s" in dbg:
        @blk.gpsimd
        def _(e):
            ctx.wait(e, ctx.tok("ldg2"))
            ctx.inc("out", e.dma_start(
                dbg["dbg_s"].rearrange("(c p) t -> p c t", p=128),
                s_spill[:].rearrange("p (c t) -> p c t", t=T)), 16)

    sg_fm = FM(R["arena_d"], 12, 512)
    hh_fm = FM(R["arena_b"], 12, 512)
    for t in range(TC):
        t0 = t * 512
        for fh in range(2):
            f0 = fh * 1536
            w_tok = load_w(ctx, d["l1_fwgT"][:, f0:f0 + 1536], wbuf,
                           pre=ctx.tok("mm"))
            jobs = []
            for mch in range(12):
                jobs.append(dict(
                    pre=w_tok,
                    mms=mm_acc([(wbuf[:, c * 1536 + mch * 128:
                                      c * 1536 + (mch + 1) * 128],
                                 xn_fm.ch(c, t0, 512)) for c in range(DC)]),
                    psum=lambda b: b[:, :],
                    evac=(lambda mch: lambda e, ps: e.activation(
                        sg_fm.ch(mch), ps, AF.Silu, scale=1.0 / 127.0))(mch),
                ))
            sweep(ctx, jobs=jobs, banks=R["pb"][0:4], evac_eng="scalar",
                  entry=ctx.tok("evs", "evv"))
            w_tok = load_w(ctx, d["l1_fwuT"][:, f0:f0 + 1536], wbuf,
                           pre=ctx.tok("mm"))
            sg_tok = ctx.tok("evs")
            jobs = []
            for mch in range(12):
                jobs.append(dict(
                    pre=w_tok + sg_tok,
                    mms=mm_acc([(wbuf[:, c * 1536 + mch * 128:
                                      c * 1536 + (mch + 1) * 128],
                                 xn_fm.ch(c, t0, 512)) for c in range(DC)]),
                    psum=lambda b: b[:, :],
                    evac=(lambda mch: lambda e, ps: e.tensor_tensor(
                        hh_fm.ch(mch), ps, sg_fm.ch(mch), ALU.mult))(mch),
                ))
            sweep(ctx, jobs=jobs, banks=R["pb"][0:4], evac_eng="vector",
                  entry=ctx.tok("evs", "evv"))
            hh_tok = ctx.tok("evv")
            w_tok = load_w(ctx, d["l1_fwdT"][f0:f0 + 1536, :], wbuf,
                           pre=ctx.tok("mm"))
            jobs = []
            for mch in range(DC):
                def mk_ev(fh=fh, mch=mch):
                    if fh == 0:
                        return lambda e, ps: e.tensor_copy(acc_fm.ch(mch), ps)
                    return lambda e, ps: e.tensor_tensor(
                        acc_fm.ch(mch), acc_fm.ch(mch), ps, ALU.add)
                jobs.append(dict(
                    pre=w_tok + hh_tok,
                    mms=mm_acc([(wbuf[:, k * 1024 + mch * 128:
                                      k * 1024 + (mch + 1) * 128],
                                 hh_fm.ch(k)) for k in range(12)]),
                    psum=lambda b: b[:, :],
                    evac=mk_ev(),
                ))
            sweep(ctx, jobs=jobs, banks=R["pb"][0:4], evac_eng="vector",
                  entry=ctx.tok("evs", "evv"))
        y_tok = ctx.tok("evv")
        gg_fm = FM(R["xn2"], DC, 512)
        @blk.scalar
        def _(e, y_tok=y_tok):
            ctx.wait(e, y_tok + ctx.tok("ldg2"))
            instr = None
            for c in range(DC):
                instr = e.activation(gg_fm.ch(c), acc_fm.ch(c), AF.Silu,
                                     scale=1.0 / (127.0 * 127.0))
            ctx.inc("evs", instr)
        @blk.gpsimd
        def _(e, t0=t0):
            ctx.wait(e, ctx.tok("evs"))
            for c in range(DC):
                ctx.inc("ldg2", e.dma_start(
                    g_spill[:, c * T + t0: c * T + t0 + 512], gg_fm.ch(c)), 16)

    if dbg and "dbg_g" in dbg:
        @blk.gpsimd
        def _(e):
            ctx.wait(e, ctx.tok("ldg2"))
            ctx.inc("out", e.dma_start(
                dbg["dbg_g"].rearrange("(c p) t -> p c t", p=128),
                g_spill[:].rearrange("p (c t) -> p c t", t=T)), 16)

    emit_combine_spill(ctx, R, g_spill, s_spill)




def l2_prep(I, m, rank):
    import ml_dtypes
    m["l2_wkv"] = np.ascontiguousarray(I["L2_attn_W_kv"])
    m["l2_gkv"] = _col_layout(I["L2_attn_g_kv"], 2)
    m["l2_wkdec"] = np.ascontiguousarray(I["L2_attn_W_kdec"])
    m["l2_wvdec"] = np.ascontiguousarray(I["L2_attn_W_vdec"])
    m["l2_wq"] = np.ascontiguousarray(I["L2_attn_W_q"])
    m["l2_wqdec"] = np.ascontiguousarray(I["L2_attn_W_qdec"])
    m["l2_wkrope"] = np.ascontiguousarray(I["L2_attn_W_krope"])
    m["l2_wqrope"] = np.ascontiguousarray(I["L2_attn_W_qrope"])
    m["l2_wo"] = np.ascontiguousarray(I["L2_attn_W_o"])
    for nm, key in [("wf", "L2_ffn_wf"), ("uf", "L2_ffn_uf"),
                    ("wi", "L2_ffn_wi"), ("ui", "L2_ffn_ui"),
                    ("wc", "L2_ffn_wc"), ("uc", "L2_ffn_uc"),
                    ("wo2", "L2_ffn_wo"), ("uo", "L2_ffn_uo")]:
        m["l2_" + nm] = np.ascontiguousarray(np.asarray(I[key]).T)
    m["l2_bf"] = _col_layout(I["L2_ffn_bf"], DC)
    m["l2_bi"] = _col_layout(I["L2_ffn_bi"], DC)
    m["l2_bc"] = _col_layout(I["L2_ffn_bc"], DC)
    m["l2_bo"] = _col_layout(I["L2_ffn_bo"], DC)
    m["l2_inc"] = _col_layout(I["L2_ffn_in_c"], DC)
    inv_freq = 1.0 / 10000.0 ** (np.arange(0, 32, 2, dtype=np.float32) / 32)
    t = np.arange(T, dtype=np.float32)
    freqs = np.outer(t, inv_freq)
    emb = np.concatenate([freqs, freqs], -1)
    cos = np.cos(emb).T
    sin = np.sin(emb).T
    m["l2_cos"] = np.ascontiguousarray(np.tile(cos, (4, 1))).astype(ml_dtypes.bfloat16)
    m["l2_sin"] = np.ascontiguousarray(np.tile(sin, (4, 1))).astype(ml_dtypes.bfloat16)
    sign = np.where(np.arange(128) % 32 < 16, -1.0, 1.0).astype(np.float32)
    m["l2_sign"] = sign.reshape(128, 1)
    p = np.arange(128, dtype=np.int32)[:, None]
    g1 = np.zeros((128, 64), np.int32)
    g2 = np.zeros((128, 64), np.int32)
    for t8 in range(8):
        for ch in range(8):
            g1[:, t8 * 8 + ch] = ((t8 * 128 + p[:, 0]) * 64 + ch * 8 + rank)
            g2[:, t8 * 8 + ch] = ((t8 * 128 + p[:, 0]) * 64 + rank * 8 + ch)
    m["l2_gidx"] = g1
    m["l2_gidx2"] = g2


def l2_declare(nc, d):
    def din(name, shape, dt=F32):
        d[name] = nc.dram_tensor(name, shape, dt, kind="ExternalInput").ap()
    din("l2_wkv", [1024, 256]); din("l2_gkv", [128, 2])
    din("l2_wkdec", [256, 1024]); din("l2_wvdec", [256, 1024])
    din("l2_wq", [1024, 512]); din("l2_wqdec", [512, 1024])
    din("l2_wkrope", [1024, 512]); din("l2_wqrope", [512, 512])
    din("l2_wo", [1024, 1024])
    for nm in ["wf", "uf", "wi", "ui", "wc", "uc", "wo2", "uo"]:
        din("l2_" + nm, [1024, 1024])
    for nm in ["bf", "bi", "bc", "bo", "inc"]:
        din("l2_" + nm, [128, DC])
    din("l2_cos", [128, T], BF16); din("l2_sin", [128, T], BF16)
    din("l2_sign", [128, 1])
    din("l2_gidx", [128, 64], I32); din("l2_gidx2", [128, 64], I32)


def emit_l2(ctx, R, d, g_spill, s_spill, dbg=None):
    from concourse.bass import IndirectOffsetOnAxis
    blk = ctx.blk
    nc = ctx.nc
    x_fm = FM(R["x"], DC, T)
    xn_fm = FM(R["xn"], DC, T)

    smf, smb = R["smf"], R["smb"]
    gkv = smf[0:128, 0:2]
    sign_t = smf[0:128, 2:3]
    bias_t = smf[0:128, 3:35]
    inc_t = smf[0:128, 35:43]
    cos_w = smb[0:64, 0:512]
    sin_w = smb[0:64, 512:1024]
    gidx = smb.bitcast(I32)[0:128, 512:576]

    cc_in = nc.dram_tensor("l2_ccin", [128, DC * T], BF16, kind="Internal").ap()
    cc_out = nc.dram_tensor("l2_ccout", [128 * 8, DC * T], BF16, kind="Internal").ap()
    cc2_in = nc.dram_tensor("l2_cc2in", [128, DC * T], BF16, kind="Internal").ap()
    cc2_out = nc.dram_tensor("l2_cc2out", [128 * 8, DC * T], BF16, kind="Internal").ap()

    @blk.sync
    def _(e):
        ctx.wait(e, ctx.tok("comb", "evv", "evs"))
        for src, dst in [("l2_gkv", gkv), ("l2_sign", sign_t),
                         ("l2_gidx", gidx), ("l2_inc", inc_t)]:
            ctx.inc("ld", e.dma_start(dst, d[src][:]), 16)
        for i, nm in enumerate(["l2_bf", "l2_bi", "l2_bc", "l2_bo"]):
            ctx.inc("ld", e.dma_start(bias_t[0:128, i * DC:(i + 1) * DC], d[nm][:]), 16)
    small_tok = ctx.tok("ld")

    emit_ln(ctx, R, x_fm, 0, T, xn_fm, 0, rms=False, in_f32=True)
    ln_tok = ctx.tok("ln_done")
    if dbg and "dbg_xn" in dbg:
        @blk.gpsimd
        def _(e):
            ctx.wait(e, ctx.tok("ln_done"))
            ctx.inc("out", e.dma_start(
                dbg["dbg_xn"].rearrange("(c p) t -> p c t", p=128),
                R["xn"][:].rearrange("p (c t) -> p c t", t=T)), 16)

    @blk.gpsimd
    def _(e):
        ctx.wait(e, ctx.tok("ln_done"))
        ctx.inc("ldg2", e.dma_start(cc_in[:], R["xn"][:]), 16)
        ctx.wait(e, ctx.tok("ldg2"))
        ctx.inc("cc", e.collective_compute(
            "AllGather", ALU.bypass, replica_groups=[list(range(NCORES))],
            ins=[cc_in.opt()], outs=[cc_out.opt()]))
    ag1_tok = ctx.tok("cc")

    kvf = FM(R["arena_c"], 2, T)
    kvn = FM(R["xn2"], 2, T)
    wsm = R["sq"].bitcast(BF16)
    w_tok = load_w(ctx, d["l2_wkv"], wsm, col0=0, pre=ctx.tok("mm", "statmm"))
    jobs = []
    for mch in range(2):
        for tb in range(4):
            jobs.append(dict(
                pre=w_tok + ln_tok,
                mms=mm_acc([(wsm[:, c * 256 + mch * 128: c * 256 + (mch + 1) * 128],
                             xn_fm.ch(c, tb * 512, 512)) for c in range(DC)]),
                psum=lambda b: b[:, :],
                evac=(lambda mch, tb: lambda e, ps: e.copy(
                    kvf.ch(mch, tb * 512, 512), ps))(mch, tb),
            ))
    sweep(ctx, jobs=jobs, banks=R["pb"][0:2], evac_eng="scalar",
          entry=ctx.tok("evs", "evv"))
    @blk.scalar
    def _(e):
        ctx.wait(e, ctx.tok("evs"))
        ctx.inc("ln_done", e.nop())
    emit_ln(ctx, R, kvf, 0, T, kvn, 0, rms=True, eps=1e-6, in_f32=True, nch=2,
            gain_col=lambda c: gkv[:, c:c + 1])
    kvn_tok = ctx.tok("ln_done")

    qc = FM(R["arena_b"], 4, T)
    wqb = R["arena_d"]
    w_tok = load_w(ctx, d["l2_wq"], wqb, col0=0, pre=ctx.tok("mm"))
    jobs = []
    for mch in range(4):
        for tb in range(4):
            jobs.append(dict(
                pre=w_tok + ln_tok,
                mms=mm_acc([(wqb[:, c * 512 + mch * 128:
                                 c * 512 + (mch + 1) * 128],
                             xn_fm.ch(c, tb * 512, 512)) for c in range(DC)]),
                psum=lambda b: b[:, :],
                evac=(lambda mch, tb: lambda e, ps: e.copy(
                    qc.ch(mch, tb * 512, 512), ps))(mch, tb),
            ))
    sweep(ctx, jobs=jobs, banks=R["pb"][0:2], evac_eng="scalar",
          entry=ctx.tok("evs", "evv"))
    qc_tok = ctx.tok("evs")

    ad = R["arena_d"]
    KO, QO, VO, AO = 0, 2048, 4096, 6144
    o_all = FM(R["arena_a"], DC, T)
    attnT = R["sq"].bitcast(BF16)
    rr_t = R["sq"].bitcast(BF16)[0:64, 0:512]
    sc_row = R["arena_c"]
    acbf = R["arena_c"].bitcast(BF16)
    kr_t = acbf[0:64, 4096:6144]
    qr_t = acbf[0:64, 6144:8192]

    for hg in range(8):
        wq_tok = ctx.tok("ldg")
        load_w(ctx, d["l2_wkdec"][:, hg * 128:(hg + 1) * 128], wsm, col0=0,
               pre=ctx.tok("mm", "statmm", "out"))
        load_w(ctx, d["l2_wvdec"][:, hg * 128:(hg + 1) * 128], wsm, col0=256)
        load_w(ctx, d["l2_wqdec"][:, hg * 128:(hg + 1) * 128], wsm, col0=512)
        load_w(ctx, d["l2_wkrope"][:, hg * 64:(hg + 1) * 64], wsm, col0=1024)
        load_w(ctx, d["l2_wqrope"][:, hg * 64:(hg + 1) * 64], wsm, col0=1536)
        w_tok = ctx.tok("ldg")

        jobs = []
        for tb in range(4):
            jobs.append(dict(
                pre=w_tok + kvn_tok,
                mms=mm_acc([(wsm[:, 0 + c * 128:0 + (c + 1) * 128],
                             kvn.ch(c, tb * 512, 512)) for c in range(2)]),
                psum=lambda b: b[:, :],
                evac=(lambda tb: lambda e, ps: e.copy(
                    ad[:, KO + tb * 512: KO + (tb + 1) * 512], ps))(tb),
            ))
        for tb in range(4):
            jobs.append(dict(
                pre=w_tok + qc_tok,
                mms=mm_acc([(wsm[:, 512 + c * 128: 512 + (c + 1) * 128],
                             qc.ch(c, tb * 512, 512)) for c in range(4)]),
                psum=lambda b: b[:, :],
                evac=(lambda tb: lambda e, ps: e.copy(
                    ad[:, QO + tb * 512: QO + (tb + 1) * 512], ps))(tb),
            ))
        for ts_ in range(16):
            jobs.append(dict(
                pre=w_tok + kvn_tok,
                mms=mm_acc([(kvn.ch(c, ts_ * 128, 128),
                             wsm[:, 256 + c * 128: 256 + (c + 1) * 128])
                            for c in range(2)]),
                psum=lambda b: b[:, 0:128],
                evac=(lambda ts_: lambda e, ps: e.copy(
                    ad[:, VO + ts_ * 128: VO + (ts_ + 1) * 128], ps))(ts_),
            ))
        for tb in range(4):
            jobs.append(dict(
                pre=w_tok + ln_tok,
                mms=mm_acc([(wsm[:, 1024 + c * 64: 1024 + (c + 1) * 64],
                             xn_fm.ch(c, tb * 512, 512)) for c in range(DC)]),
                psum=lambda b: b[0:64, :],
                evac=(lambda tb: lambda e, ps: e.copy(
                    acbf[0:64, 4096 + tb * 512: 4096 + (tb + 1) * 512], ps))(tb),
            ))
        for tb in range(4):
            jobs.append(dict(
                pre=w_tok + qc_tok,
                mms=mm_acc([(wsm[:, 1536 + c * 64: 1536 + (c + 1) * 64],
                             qc.ch(c, tb * 512, 512)) for c in range(4)]),
                psum=lambda b: b[0:64, :],
                evac=(lambda tb: lambda e, ps: e.copy(
                    acbf[0:64, 6144 + tb * 512: 6144 + (tb + 1) * 512], ps))(tb),
            ))
        sweep(ctx, jobs=jobs, banks=R["pb"][0:3], evac_eng="scalar",
              entry=ctx.tok("evs", "evv"))
        prj_tok = ctx.tok("evs")

        if dbg and "dbg_att" in dbg and hg == 0:
            @blk.gpsimd
            def _(e, prj_tok=prj_tok):
                ctx.wait(e, prj_tok)
                ctx.inc("out", e.dma_start(
                    dbg["dbg_att"][0:64, 7168:8192],
                    R["arena_c"].bitcast(BF16)[0:64, 6144:7168]), 16)
            @blk.vector
            def _(e):
                ctx.wait(e, ctx.tok("out"))
                ctx.inc("evv", e.nop())
        mask = [(i + 16) % 32 for i in range(32)]
        for tb in range(4):
            w0 = tb * 512
            @blk.sync
            def _(e, w0=w0):
                ctx.wait(e, ctx.tok("evv"))
                ctx.inc("ld", e.dma_start(cos_w, d["l2_cos"][0:64, w0:w0 + 512]), 16)
                ctx.inc("ld", e.dma_start(sin_w, d["l2_sin"][0:64, w0:w0 + 512]), 16)
            cs_tok = ctx.tok("ld")
            @blk.vector
            def _(e, w0=w0, cs_tok=cs_tok, prj_tok=prj_tok):
                ctx.wait(e, prj_tok + small_tok + cs_tok)
                for ro in (4096, 6144):
                    rt = acbf[0:64, ro + w0: ro + w0 + 512]
                    ctx.inc("evv", e.stream_shuffle(rr_t, rt, mask))
                    ctx.wait(e, ctx.tok("evv"))
                    ctx.inc("evv", e.tensor_tensor(rt, rt, cos_w, ALU.mult))
                    ctx.inc("evv", e.scalar_tensor_tensor(
                        rr_t, rr_t, sign_t[0:64, 0:1], sin_w[:],
                        ALU.mult, ALU.mult))
                    ctx.wait(e, ctx.tok("evv"))
                    ctx.inc("evv", e.tensor_tensor(rt, rt, rr_t, ALU.add))
                    ctx.wait(e, ctx.tok("evv"))
        rope_tok = ctx.tok("evv")
        if dbg and "dbg_att" in dbg and hg == 0:
            pass
            @blk.gpsimd
            def _(e):
                ctx.wait(e, rope_tok + prj_tok)
                ctx.inc("out", e.dma_start(dbg["dbg_att"][0:128, 0:2048], ad[:, 0:2048]), 16)
                ctx.inc("out", e.dma_start(dbg["dbg_att"][0:128, 2048:4096], ad[:, 2048:4096]), 16)
                ctx.inc("out", e.dma_start(dbg["dbg_att"][0:64, 4096:6144],
                                           R["arena_c"].bitcast(BF16)[0:64, 4096:6144]), 16)
                ctx.inc("out", e.dma_start(dbg["dbg_att"][64:128, 4096:6144],
                                           R["arena_c"].bitcast(BF16)[0:64, 6144:8192]), 16)
                ctx.inc("out", e.dma_start(dbg["dbg_att"][0:128, 6144:7168], ad[:, 4096:5120]), 16)


        for h2 in range(2):
            base = 64 * h2
            rbase = 32 * h2
            for tq in range(8):
                for qs in range(2):
                    q0 = tq * 256 + qs * 128
                    L = q0 + 128
                    sjobs = []
                    nblk = (L + 511) // 512
                    for kb in range(nblk):
                        k0 = kb * 512
                        w = min(512, L - k0)
                        has_diag = (k0 + w == L)
                        def mk_evac(k0=k0, w=w, has_diag=has_diag, L=L):
                            def f(e, ps):
                                sc = 1.0 / np.sqrt(96.0)
                                if has_diag:
                                    if w > 128:
                                        e.tensor_single_scalar(
                                            sc_row[:, k0:k0 + w - 128],
                                            ps[:, 0:w - 128], sc, ALU.mult)
                                    return e.scalar_tensor_tensor(
                                        sc_row[:, L - 128:L], ps[:, w - 128:w],
                                        sc, R["causal_t"][:], ALU.mult, ALU.add)
                                return e.tensor_single_scalar(
                                    sc_row[:, k0:k0 + w], ps[:, 0:w], sc, ALU.mult)
                            return f
                        def mk_mms(k0=k0, w=w, q0=q0, base=base, rbase=rbase):
                            def m1(e, ps):
                                return e.matmul(
                                    ps[:, 0:w],
                                    ad[base:base + 64, QO + q0:QO + q0 + 128],
                                    ad[base:base + 64, KO + k0:KO + k0 + w],
                                    start=True, stop=DBG_NOROPE)
                            def m2(e, ps):
                                if DBG_NOROPE:
                                    return None
                                return e.matmul(
                                    ps[:, 0:w],
                                    acbf[rbase:rbase + 32, 6144 + q0: 6144 + q0 + 128],
                                    acbf[rbase:rbase + 32, 4096 + k0: 4096 + k0 + w],
                                    start=DBG_ROPEONLY, stop=True)
                            return [m1, m2]
                        mmsl = mk_mms()
                        if DBG_NOROPE:
                            mmsl = mmsl[:1]
                        elif DBG_ROPEONLY:
                            mmsl = mmsl[1:]
                        sjobs.append(dict(
                            pre=rope_tok,
                            mms=mmsl,
                            psum=lambda b: b,
                            evac=mk_evac(),
                        ))
                    sweep(ctx, jobs=sjobs, banks=R["pb"][0:2], evac_eng="vector",
                          entry=ctx.tok("evs", "evv"))
                    sc_tok = ctx.tok("evv")

                    @blk.vector
                    def _(e, L=L):
                        ctx.wait(e, sc_tok)
                        instr = e.tensor_reduce(R["cols"][:, 0:1],
                                                sc_row[:, 0:L], AX.X, ALU.max,
                                                negate=True)
                        ctx.inc("evv", instr)
                    @blk.scalar
                    def _(e, L=L):
                        ctx.wait(e, ctx.tok("evv"))
                        instr = e.activation(ad[:, AO:AO + L], sc_row[:, 0:L],
                                             AF.Exp, bias=R["cols"][:, 0:1])
                        ctx.inc("evs", instr)
                    @blk.vector
                    def _(e, L=L):
                        ctx.wait(e, ctx.tok("evs"))
                        ctx.inc("evv", e.tensor_reduce(
                            R["cols"][:, 1:2], ad[:, AO:AO + L], AX.X, ALU.add))
                        ctx.wait(e, ctx.tok("evv"))
                        ctx.inc("evv", e.reciprocal(R["cols"][:, 2:3],
                                                    R["cols"][:, 1:2]))
                        ctx.wait(e, ctx.tok("evv"))
                        instr = e.tensor_scalar_mul(ad[:, AO:AO + L],
                                                    ad[:, AO:AO + L],
                                                    R["cols"][:, 2:3])
                        ctx.inc("evv", instr)
                    sm_tok = ctx.tok("evv")

                    tjobs = []
                    for tkc in range(L // 128):
                        tjobs.append(dict(
                            pre=sm_tok,
                            mms=[(lambda tkc: lambda e, ps: e.transpose(
                                ps, ad[:, AO + tkc * 128:AO + (tkc + 1) * 128],
                                R["iden_b"][:]))(tkc)],
                            psum=lambda b: b[:, 0:64].bitcast(BF16),
                            evac=(lambda tkc, qs=qs: lambda e, ps: e.tensor_copy(
                                attnT[:, tkc * 256 + qs * 128:
                                      tkc * 256 + qs * 128 + 128], ps))(tkc),
                        ))
                    sweep(ctx, jobs=tjobs, banks=R["pb"][2:4], evac_eng="vector",
                          entry=ctx.tok("evs", "evv"))

                at_tok = ctx.tok("evv")
                nk = 2 * (tq + 1)
                avmms = []
                for tkc in range(nk):
                    noff = max(0, (tkc - 2 * tq)) * 128
                    def mk(tkc=tkc, noff=noff, first=(tkc == 0),
                           last=(tkc == nk - 1), base=base):
                        def f(e, ps):
                            return e.matmul(
                                ps[0:64, noff:256],
                                ad[:, VO + tkc * 128 + base:
                                   VO + tkc * 128 + base + 64],
                                attnT[:, tkc * 256 + noff: (tkc + 1) * 256],
                                start=first, stop=last)
                        return f
                    avmms.append(mk())
                avjobs = [dict(
                    pre=at_tok,
                    mms=avmms,
                    psum=lambda b: b,
                    evac=(lambda hg=hg, h2=h2, tq=tq: lambda e, ps: e.copy(
                        o_all.ch(hg, tq * 256, 256, p0=64 * h2, pw=64),
                        ps[0:64, 0:256]))(),
                )]
                sweep(ctx, jobs=avjobs, banks=[R["pb"][4]], evac_eng="scalar",
                      entry=ctx.tok("evs", "evv"))

    if dbg and "dbg_o" in dbg:
        @blk.gpsimd
        def _(e):
            ctx.wait(e, ctx.tok("evv"))
            ctx.inc("out", e.dma_start(
                dbg["dbg_o"].rearrange("(c p) t -> p c t", p=128),
                R["arena_a"][:].rearrange("p (c t) -> p c t", t=T)), 16)
    wo_t = R["arena_d"]
    wo_tok = load_w(ctx, d["l2_wo"], wo_t, pre=ctx.tok("mm"))
    s_fm = FM(R["xn2"], DC, 512)
    for t in range(TC):
        t0 = t * 512
        jobs = []
        for mch in range(DC):
            jobs.append(dict(
                pre=wo_tok + ctx.tok("evv", "evs"),
                mms=mm_acc([(wo_t[:, c * 1024 + mch * 128:
                                  c * 1024 + (mch + 1) * 128],
                             o_all.ch(c, t0, 512)) for c in range(DC)]),
                psum=lambda b: b[:, :],
                evac=(lambda mch: lambda e, ps: e.tensor_copy(
                    s_fm.ch(mch), ps))(mch),
            ))
        sweep(ctx, jobs=jobs, banks=R["pb"][5:7], evac_eng="vector",
              entry=ctx.tok("evs", "evv"))
        @blk.gpsimd
        def _(e, t0=t0):
            ctx.wait(e, ctx.tok("evv"))
            for c in range(DC):
                ctx.inc("ldg2", e.dma_start(
                    s_spill[:, c * T + t0: c * T + t0 + 512], s_fm.ch(c)), 16)
    if dbg and "dbg_s" in dbg:
        @blk.gpsimd
        def _(e):
            ctx.wait(e, ctx.tok("ldg2"))
            ctx.inc("out", e.dma_start(
                dbg["dbg_s"].rearrange("(c p) t -> p c t", p=128),
                s_spill[:].rearrange("p (c t) -> p c t", t=T)), 16)

    from concourse.bass import IndirectOffsetOnAxis as IOA
    xa = R["arena_a"]
    @blk.gpsimd
    def _(e):
        ctx.wait(e, ag1_tok + ctx.tok("mm", "evv"))
        src3 = cc_out.rearrange("q (s j) -> q s j", j=256)
        for t8 in range(8):
            for chh in range(DC):
                col = t8 * 8 + chh
                ctx.inc("ldg2", e.indirect_dma_start(
                    xa[:, col * 256:(col + 1) * 256],
                    None, src3, IOA(gidx[0:128, col:col + 1], 1)), 16)
    xa_tok = ctx.tok("ldg2")

    cst = FM(R["arena_c"], DC, 256)
    gp_f = FM(R["arena_c"], DC, 256, col0=2048)
    gp_i = FM(R["sq"], DC, 256)
    fb_fm = FM(R["xn2"], DC, 256)
    ib_fm = FM(R["xn2"], DC, 256, col0=2048)
    gb_fm = FM(R["arena_d"], DC, 256)
    tmp_fm = FM(R["arena_d"], DC, 256, col0=2048)
    cbf_fm = FM(R["arena_d"], DC, 256, col0=4096)
    wlb = R["arena_b"]

    @blk.gpsimd
    def _(e):
        ctx.wait(e, ctx.tok("evv", "mm"))
        ctx.inc("evp", e.memset(R["arena_c"][:, 0:2048], 0.0))
    @blk.vector
    def _(e):
        ctx.wait(e, small_tok + ctx.tok("evp"))
        instr = None
        for c in range(DC):
            e.tensor_scalar_add(cst.ch(c), cst.ch(c), inc_t[:, c:c + 1])
            instr = e.tensor_copy(cbf_fm.ch(c), cst.ch(c))
        ctx.inc("evs", instr)

    for t8 in range(8):
        for gi_, (wn, un, afn, dst) in enumerate([
                ("l2_wf", "l2_uf", AF.Sigmoid, fb_fm),
                ("l2_wi", "l2_ui", AF.Sigmoid, ib_fm),
                ("l2_wc", "l2_uc", AF.Tanh, gb_fm)]):
            for part, wname in [(0, wn), (1, un)]:
                w_tok = load_w(ctx, d[wname], wlb, pre=ctx.tok("mm"))
                jobs = []
                for mch in range(DC):
                    def mk_ev(part=part, mch=mch):
                        if part == 0:
                            return lambda e, ps: e.tensor_copy(gp_i.ch(mch), ps[:, 0:256])
                        return lambda e, ps: e.tensor_tensor(
                            gp_i.ch(mch), gp_i.ch(mch), ps[:, 0:256], ALU.add)
                    rhs_fm = (lambda c, t8=t8: xa[:, (t8 * 8 + c) * 256:
                                                  (t8 * 8 + c + 1) * 256])                         if part == 0 else (lambda c: cbf_fm.ch(c))
                    jobs.append(dict(
                        pre=w_tok + xa_tok + ctx.tok("evs"),
                        mms=mm_acc([(wlb[:, c * 1024 + mch * 128:
                                         c * 1024 + (mch + 1) * 128],
                                     rhs_fm(c)) for c in range(DC)]),
                        psum=lambda b: b[:, 0:256],
                        evac=mk_ev(),
                    ))
                sweep(ctx, jobs=jobs, banks=R["pb"][0:4], evac_eng="vector",
                      entry=ctx.tok("evs", "evv"))
            gp_tok = ctx.tok("evv")
            @blk.scalar
            def _(e, gp_tok=gp_tok, gi_=gi_, afn=afn, dst=dst):
                ctx.wait(e, gp_tok + small_tok)
                instr = None
                for c in range(DC):
                    instr = e.activation(dst.ch(c), gp_i.ch(c), afn,
                                         bias=bias_t[:, gi_ * DC + c:gi_ * DC + c + 1])
                ctx.inc("evs", instr)
        act_tok = ctx.tok("evs")
        @blk.vector
        def _(e, act_tok=act_tok):
            ctx.wait(e, act_tok)
            for c in range(DC):
                ctx.inc("evv", e.tensor_tensor(tmp_fm.ch(c), ib_fm.ch(c),
                                               gb_fm.ch(c), ALU.mult))
                ctx.inc("evv", e.tensor_tensor(cst.ch(c), fb_fm.ch(c),
                                               cst.ch(c), ALU.mult))
                ctx.wait(e, ctx.tok("evv"))
                ctx.inc("evv", e.tensor_tensor(cst.ch(c), cst.ch(c),
                                               tmp_fm.ch(c), ALU.add))
                ctx.wait(e, ctx.tok("evv"))
                ctx.inc("evv", e.tensor_copy(cbf_fm.ch(c), cst.ch(c)))
                ctx.wait(e, ctx.tok("evv"))
            ctx.inc("evv", e.nop())

    cfin_tok = ctx.tok("evv")
    w_tok = load_w(ctx, d["l2_uo"], wlb, pre=ctx.tok("mm"))
    jobs = []
    for mch in range(DC):
        jobs.append(dict(
            pre=w_tok + cfin_tok,
            mms=mm_acc([(wlb[:, c * 1024 + mch * 128:
                             c * 1024 + (mch + 1) * 128],
                         cbf_fm.ch(c)) for c in range(DC)]),
            psum=lambda b: b[:, 0:256],
            evac=(lambda mch: lambda e, ps: e.tensor_copy(
                gp_f.ch(mch), ps[:, 0:256]))(mch),
        ))
    sweep(ctx, jobs=jobs, banks=R["pb"][0:2], evac_eng="vector",
          entry=ctx.tok("evs", "evv"))
    cpart_tok = ctx.tok("evv")
    w_tok = load_w(ctx, d["l2_wo2"], wlb, pre=ctx.tok("mm"))
    for t8 in range(8):
        jobs = []
        for mch in range(DC):
            jobs.append(dict(
                pre=w_tok + cpart_tok,
                mms=mm_acc([(wlb[:, c * 1024 + mch * 128:
                                 c * 1024 + (mch + 1) * 128],
                             xa[:, (t8 * 8 + c) * 256:(t8 * 8 + c + 1) * 256])
                            for c in range(DC)]),
                psum=lambda b: b[:, 0:256],
                evac=(lambda mch: lambda e, ps: e.tensor_tensor(
                    gp_i.ch(mch), ps[:, 0:256], gp_f.ch(mch), ALU.add))(mch),
            ))
        sweep(ctx, jobs=jobs, banks=R["pb"][0:2], evac_eng="vector",
              entry=ctx.tok("evs", "evv"))
        hx_tok = ctx.tok("evv")
        @blk.scalar
        def _(e, hx_tok=hx_tok, t8=t8):
            ctx.wait(e, hx_tok + ctx.tok("ldg2"))
            instr = None
            for c in range(DC):
                instr = e.activation(fb_fm.ch(c), gp_i.ch(c), AF.Tanh,
                                     bias=bias_t[:, 3 * DC + c:3 * DC + c + 1])
            ctx.inc("evs", instr)
        @blk.gpsimd
        def _(e, t8=t8):
            ctx.wait(e, ctx.tok("evs"))
            for c in range(DC):
                ctx.inc("ldg2", e.dma_start(
                    cc2_in[:, (t8 * 8 + c) * 256:(t8 * 8 + c + 1) * 256],
                    fb_fm.ch(c)), 16)

    @blk.gpsimd
    def _(e):
        ctx.wait(e, ctx.tok("ldg2"))
        ctx.inc("cc", e.collective_compute(
            "AllGather", ALU.bypass, replica_groups=[list(range(NCORES))],
            ins=[cc2_in.opt()], outs=[cc2_out.opt()]))
    ag2_tok = ctx.tok("cc")

    gg_fm = FM(R["xn2"], DC, 256)
    @blk.sync
    def _(e):
        ctx.wait(e, ctx.tok("ldg2"))
        ctx.inc("ld", e.dma_start(gidx, d["l2_gidx2"][:]), 16)
    @blk.gpsimd
    def _(e):
        ctx.wait(e, ctx.tok("ld"))
        ctx.inc("evp", e.nop())
    for r in range(8):
        @blk.gpsimd
        def _(e, r=r):
            ctx.wait(e, ag2_tok + ctx.tok("evs"))
            src3 = cc2_out.rearrange("q (s j) -> q s j", j=256)
            for chh in range(DC):
                col = r * 8 + chh
                ctx.inc("ldg2", e.indirect_dma_start(
                    gg_fm.ch(chh), None, src3,
                    IOA(gidx[0:128, col:col + 1], 1)), 16)
        gh_tok = ctx.tok("ldg2")
        @blk.scalar
        def _(e, gh_tok=gh_tok, r=r):
            ctx.wait(e, gh_tok)
            instr = None
            for c in range(DC):
                instr = e.activation(gg_fm.ch(c), gg_fm.ch(c), AF.Silu)
            ctx.inc("evs", instr)
        @blk.gpsimd
        def _(e, r=r):
            ctx.wait(e, ctx.tok("evs"))
            for c in range(DC):
                ctx.inc("ldg2", e.dma_start(
                    g_spill[:, c * T + r * 256: c * T + (r + 1) * 256],
                    gg_fm.ch(c)), 16)
    if dbg and "dbg_g" in dbg:
        @blk.gpsimd
        def _(e):
            ctx.wait(e, ctx.tok("ldg2"))
            ctx.inc("out", e.dma_start(
                dbg["dbg_g"].rearrange("(c p) t -> p c t", p=128),
                g_spill[:].rearrange("p (c t) -> p c t", t=T)), 16)

    emit_combine_spill(ctx, R, g_spill, s_spill)




def l3_prep(I, m):
    m["l3_wupT"] = np.ascontiguousarray(np.asarray(I["L3_attn_w_up"]).T)
    m["l3_bup"] = _col_layout(I["L3_attn_b_up"], 32)
    cw = np.asarray(I["L3_attn_conv_w"], np.float32).reshape(2048, 4)
    m["l3_convw"] = np.ascontiguousarray(
        cw.reshape(16, 128, 4).transpose(1, 0, 2).reshape(128, 64))
    m["l3_convb"] = _col_layout(I["L3_attn_conv_b"], 16)
    m["l3_skip"] = _col_layout(I["L3_attn_skip"], 16)
    bd = np.zeros((3, 16, 128, 128), np.float32)
    for gi, key in enumerate(["L3_attn_w_f", "L3_attn_w_i", "L3_attn_w_h"]):
        w = np.asarray(I[key], np.float32)
        for c in range(16):
            for gl in range(8):
                bd[gi, c, gl * 16:(gl + 1) * 16, gl * 16:(gl + 1) * 16] = \
                    w[8 * c + gl].T
    m["l3_bd"] = np.ascontiguousarray(bd.reshape(3, 2048, 128))
    m["l3_wdownT"] = np.ascontiguousarray(np.asarray(I["L3_attn_w_down"]).T)
    m["l3_bdown"] = _col_layout(I["L3_attn_b_down"], DC)
    m["l3_gwT"] = np.ascontiguousarray(np.asarray(I["L3_ffn_gate_w"]).T)
    m["l3_rms"] = _col_layout(I["L3_ffn_rms_w"], DC)
    m["l3_w1T"] = np.ascontiguousarray(np.asarray(I["L3_ffn_W1"]).transpose(0, 2, 1))
    m["l3_w2T"] = np.ascontiguousarray(np.asarray(I["L3_ffn_W2"]).transpose(0, 2, 1))
    m["l3_w3T"] = np.ascontiguousarray(np.asarray(I["L3_ffn_W3"]).transpose(0, 2, 1))


def l3_declare(nc, d):
    def din(name, shape, dt=F32):
        d[name] = nc.dram_tensor(name, shape, dt, kind="ExternalInput").ap()
    din("l3_wupT", [1024, 4096]); din("l3_bup", [128, 32])
    din("l3_convw", [128, 64]); din("l3_convb", [128, 16]); din("l3_skip", [128, 16])
    din("l3_bd", [3, 2048, 128]); din("l3_wdownT", [2048, 1024])
    din("l3_bdown", [128, DC])
    din("l3_gwT", [1024, 8]); din("l3_rms", [128, DC])
    din("l3_w1T", [8, 1024, 2816]); din("l3_w2T", [8, 1024, 2816])
    din("l3_w3T", [8, 2816, 1024])


def emit_l3(ctx, R, d, g_spill, s_spill, dbg=None):
    blk = ctx.blk
    nc = ctx.nc
    x_fm = FM(R["x"], DC, T)
    xn_fm = FM(R["xn"], DC, T)

    smf, smb = R["smf"], R["smb"]
    bup = smf[0:128, 0:32]
    convw = smf[0:128, 32:96]
    convb = smf[0:128, 96:112]
    skip = smf[0:128, 112:128]
    bdown = smf[0:128, 128:136]
    rmsw = smf[0:128, 136:144]
    hstate = smf[0:128, 144:160]
    gw_t = smb[0:128, 0:64]
    halo = smb[0:128, 64:128]
    xme = smb[0:128, 128:648]
    sel_v = smb[0:8, 648:1672]
    bdv = R["xn2"]

    act_spill = nc.dram_tensor("l3_act_sp", [128, 16 * T], BF16, kind="Internal").ap()
    z_spill = nc.dram_tensor("l3_z_sp", [128, 16 * T], BF16, kind="Internal").ap()

    @blk.sync
    def _(e):
        ctx.wait(e, ctx.tok("comb", "evv", "evs"))
        for src, dst in [("l3_bup", bup), ("l3_convw", convw),
                         ("l3_convb", convb), ("l3_skip", skip),
                         ("l3_bdown", bdown), ("l3_rms", rmsw),
                         ("sel", sel_v)]:
            ctx.inc("ld", e.dma_start(dst, d[src][:]), 16)
    load_w(ctx, d["l3_gwT"], gw_t)
    small_tok = ctx.tok("ld", "ldg")

    emit_ln(ctx, R, x_fm, 0, T, xn_fm, 0, rms=False, in_f32=True)
    ln_tok = ctx.tok("ln_done")
    if dbg and "dbg_xn" in dbg:
        @blk.gpsimd
        def _(e):
            ctx.wait(e, ctx.tok("ln_done"))
            ctx.inc("out", e.dma_start(
                dbg["dbg_xn"].rearrange("(c p) t -> p c t", p=128),
                R["xn"][:].rearrange("p (c t) -> p c t", t=T)), 16)

    wbuf = R["arena_a"]
    xm_fm = FM(R["arena_d"], 16, 512)
    f_fm = FM(R["arena_b"], 16, 512)
    acbf = R["arena_c"].bitcast(BF16)
    sq = R["sq"]

    def ich(c):
        return acbf[0:128, c * 512:(c + 1) * 512]

    @blk.gpsimd
    def _(e):
        ctx.wait(e, ctx.tok("comb", "ldg2"))
        e.memset(hstate, 0.0)
        ctx.inc("evp", e.memset(halo, 0.0))
    init_tok = ctx.tok("evp")

    wup_tok = load_w(ctx, d["l3_wupT"][:, 0:2048], wbuf, pre=ctx.tok("mm"))

    for t in range(TC):
        t0 = t * 512
        jobs = []
        for mch in range(16):
            jobs.append(dict(
                pre=wup_tok + ln_tok + ctx.tok("evs"),
                mms=mm_acc([(wbuf[:, c * 2048 + mch * 128:
                                  c * 2048 + (mch + 1) * 128],
                             xn_fm.ch(c, t0, 512)) for c in range(DC)]),
                psum=lambda b: b[:, :],
                evac=(lambda mch: lambda e, ps: e.tensor_scalar_add(
                    xm_fm.ch(mch), ps, bup[:, mch:mch + 1]))(mch),
            ))
        sweep(ctx, jobs=jobs, banks=R["pb"][0:3], evac_eng="vector",
              entry=ctx.tok("evs", "evv"))
        xm_tok = ctx.tok("evv")

        if dbg and "dbg_act" in dbg and t == 0:
            @blk.gpsimd
            def _(e, xm_tok=xm_tok):
                ctx.wait(e, xm_tok)
                ctx.inc("out", e.dma_start(
                    dbg["dbg_act"][0:128, 32 * T + 16: 32 * T + 16 + 512],
                    xm_fm.ch(0)), 16)
            @blk.vector
            def _(e):
                ctx.wait(e, ctx.tok("out"))
                ctx.inc("evv", e.nop())
        for c in range(16):
            if dbg and "dbg_act" in dbg and t == 0 and c == 1:
                @blk.gpsimd
                def _(e):
                    ctx.wait(e, ctx.tok("evv"))
                    ctx.inc("out", e.dma_start(
                        dbg["dbg_act"][0:128, 32 * T + 528: 32 * T + 1040],
                        sq[:, 0:512]), 16)
                @blk.vector
                def _(e):
                    ctx.wait(e, ctx.tok("out"))
                    ctx.inc("evv", e.nop())
            @blk.vector
            def _(e, c=c, xm_tok=xm_tok):
                ctx.wait(e, xm_tok + ctx.tok("evs") + small_tok + init_tok)
                e.tensor_copy(xme[0:128, 0:4], halo[0:128, c * 4:(c + 1) * 4])
                ctx.inc("evv", e.tensor_copy(xme[0:128, 4:516], xm_fm.ch(c)))
                ctx.wait(e, ctx.tok("evv"))
                ctx.inc("evv", e.tensor_copy(halo[0:128, c * 4:(c + 1) * 4],
                                             xme[0:128, 512:516]))
                ctx.wait(e, ctx.tok("evv"))
                ctx.inc("evv", e.tensor_scalar_mul(sq[:, 0:512], xme[0:128, 1:513],
                                    convw[:, c * 4:c * 4 + 1]))
                for k in range(1, 4):
                    ctx.wait(e, ctx.tok("evv"))
                    ctx.inc("evv", e.scalar_tensor_tensor(
                        sq[:, 0:512], xme[0:128, 1 + k:513 + k],
                        convw[:, c * 4 + k:c * 4 + k + 1],
                        sq[:, 0:512], ALU.mult, ALU.add))
                ctx.wait(e, ctx.tok("evv"))
                ctx.inc("evv", e.nop())
            @blk.scalar
            def _(e, c=c):
                ctx.wait(e, ctx.tok("evv"))
                instr = e.activation(xm_fm.ch(c), sq[:, 0:512], AF.Silu,
                                     bias=convb[:, c:c + 1])
                ctx.inc("evs", instr)
        act_tok = ctx.tok("evs")

        @blk.gpsimd
        def _(e, t0=t0, act_tok=act_tok):
            ctx.wait(e, act_tok + ctx.tok("ldg2"))
            for c in range(16):
                ctx.inc("ldg2", e.dma_start(
                    act_spill[:, c * T + t0: c * T + t0 + 512],
                    xm_fm.ch(c)), 16)

        for gi in range(2):
            bd_tok = load_w(ctx, d["l3_bd"][gi], bdv, pre=ctx.tok("mm"))
            jobs = []
            for c in range(16):
                def mk_ev(gi=gi, c=c):
                    if gi == 0:
                        return lambda e, ps: e.tensor_copy(f_fm.ch(c), ps)
                    return lambda e, ps: e.tensor_copy(ich(c), ps)
                jobs.append(dict(
                    pre=act_tok + small_tok + bd_tok,
                    mms=mm_acc([(bdv[:, c * 128:(c + 1) * 128],
                                 xm_fm.ch(c))]),
                    psum=lambda b: b[:, :],
                    evac=mk_ev(),
                ))
            sweep(ctx, jobs=jobs, banks=R["pb"][0:3], evac_eng="vector",
                  entry=ctx.tok("evs", "evv"))
        bd_tok = load_w(ctx, d["l3_bd"][2], bdv, pre=ctx.tok("mm"))
        fi_tok = ctx.tok("evv") + bd_tok

        s0 = sq[:, 0:512]; s1 = sq[:, 512:1024]
        s2 = sq[:, 1024:1536]; s3 = sq[:, 1536:2048]
        mtile = xme[0:128, 0:512]
        for c in range(16):
            hjobs = [dict(
                pre=fi_tok + act_tok + ctx.tok("evv"),
                mms=mm_acc([(bdv[:, c * 128:(c + 1) * 128],
                             xm_fm.ch(c))]),
                psum=lambda b: b[:, :],
                evac=lambda e, ps: e.tensor_copy(s3, ps),
            )]
            sweep(ctx, jobs=hjobs, banks=[R["pb"][3]], evac_eng="vector",
                  entry=ctx.tok("evs", "evv"))
            fch = f_fm.ch(c)
            @blk.vector
            def _(e, c=c):
                ctx.wait(e, ctx.tok("evv"))
                instr = e.tensor_single_scalar(mtile, s3, 0.0, ALU.is_ge)
                ctx.inc("evv", instr)
            @blk.scalar
            def _(e, c=c, fch=fch):
                ctx.wait(e, ctx.tok("evv"))
                e.activation(s0, fch, AF.Abs)
                e.activation(s0, s0, AF.Exp, scale=-1.0)
                e.activation(s0, s0, AF.Ln, bias=1.0)
                instr = e.activation(s1, fch, AF.Relu, scale=-1.0)
                ctx.inc("evs", instr)
            @blk.vector
            def _(e, c=c):
                ctx.wait(e, ctx.tok("evs"))
                instr = e.tensor_tensor(s0, s0, s1, ALU.add)
                ctx.inc("evv", instr)
            @blk.scalar
            def _(e, c=c):
                ctx.wait(e, ctx.tok("evv"))
                e.activation(s1, ich(c), AF.Abs)
                e.activation(s1, s1, AF.Exp, scale=-1.0)
                e.activation(s1, s1, AF.Ln, bias=1.0)
                instr = e.activation(s2, ich(c), AF.Relu, scale=-1.0)
                ctx.inc("evs", instr)
            @blk.vector
            def _(e, c=c):
                ctx.wait(e, ctx.tok("evs"))
                ctx.inc("evv", e.tensor_tensor(s1, s1, s2, ALU.add))
                ctx.wait(e, ctx.tok("evv"))
                ctx.inc("evv", e.tensor_tensor(s0, s0, s1, ALU.subtract))
                ctx.wait(e, ctx.tok("evv"))
                ctx.inc("evv", e.nop())
            @blk.scalar
            def _(e, c=c):
                ctx.wait(e, ctx.tok("evv"))
                e.activation(s1, s0, AF.Abs)
                e.activation(s1, s1, AF.Exp, scale=-1.0)
                e.activation(s1, s1, AF.Ln, bias=1.0)
                instr = e.activation(s2, s0, AF.Relu)
                ctx.inc("evs", instr)
            @blk.vector
            def _(e, c=c):
                ctx.wait(e, ctx.tok("evs"))
                instr = e.scalar_tensor_tensor(s2, s2, -1.0, s1,
                                               ALU.mult, ALU.subtract)
                ctx.inc("evv", instr)
            @blk.scalar
            def _(e, c=c, fch=fch):
                ctx.wait(e, ctx.tok("evv"))
                instr = e.activation(fch, s2, AF.Exp, bias=cb(R, 1e-8))
                ctx.inc("evs", instr)
            @blk.vector
            def _(e, c=c):
                ctx.wait(e, ctx.tok("evs"))
                instr = e.scalar_tensor_tensor(s0, s0, 0.0, s1,
                                               ALU.min, ALU.subtract)
                ctx.inc("evv", instr)
            @blk.scalar
            def _(e, c=c):
                ctx.wait(e, ctx.tok("evv"))
                e.activation(s2, s3, AF.Relu)
                e.activation(s2, s2, AF.Ln, bias=cb(R, 0.5 + 1e-8))
                e.activation(s1, s3, AF.Abs)
                e.activation(s1, s1, AF.Exp, scale=-1.0)
                e.activation(s1, s1, AF.Ln, bias=1.0)
                instr = e.activation(s3, s3, AF.Relu, scale=-1.0)
                ctx.inc("evs", instr)
            @blk.vector
            def _(e, c=c):
                ctx.wait(e, ctx.tok("evs"))
                ctx.inc("evv", e.tensor_tensor(s1, s1, s3, ALU.add))
                ctx.wait(e, ctx.tok("evv"))
                ctx.inc("evv", e.tensor_scalar(s1, s1, -1.0, 1e-8,
                                               ALU.mult, ALU.add))
                ctx.wait(e, ctx.tok("evv"))
                ctx.inc("evv", e.tensor_tensor(s3, s2, s1, ALU.subtract))
                ctx.wait(e, ctx.tok("evv"))
                ctx.inc("evv", e.tensor_tensor(s3, mtile, s3, ALU.mult))
                ctx.wait(e, ctx.tok("evv"))
                ctx.inc("evv", e.tensor_tensor(s3, s1, s3, ALU.add))
                ctx.wait(e, ctx.tok("evv"))
                ctx.inc("evv", e.tensor_tensor(s0, s0, s3, ALU.add))
                ctx.wait(e, ctx.tok("evv"))
                ctx.inc("evv", e.nop())
            @blk.scalar
            def _(e, c=c):
                ctx.wait(e, ctx.tok("evv"))
                instr = e.activation(ich(c), s0, AF.Exp)
                ctx.inc("evs", instr)
            @blk.vector
            def _(e, c=c, fch=fch):
                ctx.wait(e, ctx.tok("evs") + ctx.tok("scan"))
                ctx.inc("scan", e.tensor_tensor_scan(
                    s3, fch, ich(c), hstate[0:128, c:c + 1], ALU.mult, ALU.add))
                ctx.wait(e, ctx.tok("scan"))
                ctx.inc("scan", e.tensor_copy(hstate[0:128, c:c + 1],
                                              s3[0:128, 511:512]))
                ctx.wait(e, ctx.tok("scan"))
                ctx.inc("evv", e.nop())

    wupz_tok = load_w(ctx, d["l3_wupT"][:, 2048:4096], wbuf, pre=ctx.tok("mm"))
    for t in range(TC):
        t0 = t * 512
        jobs = []
        for mch in range(16):
            jobs.append(dict(
                pre=wupz_tok + ln_tok,
                mms=mm_acc([(wbuf[:, c * 2048 + mch * 128:
                                  c * 2048 + (mch + 1) * 128],
                             xn_fm.ch(c, t0, 512)) for c in range(DC)]),
                psum=lambda b: b[:, :],
                evac=(lambda mch: lambda e, ps: e.activation(
                    xm_fm.ch(mch), ps, AF.Silu,
                    bias=bup[:, 16 + mch:16 + mch + 1]))(mch),
            ))
        sweep(ctx, jobs=jobs, banks=R["pb"][0:3], evac_eng="scalar",
              entry=ctx.tok("evs", "evv"))
        z_tok = ctx.tok("evs")
        @blk.gpsimd
        def _(e, t0=t0, z_tok=z_tok):
            ctx.wait(e, z_tok + ctx.tok("ldg2"))
            for c in range(16):
                ctx.inc("ldg2", e.dma_start(
                    z_spill[:, c * T + t0: c * T + t0 + 512],
                    xm_fm.ch(c)), 16)

    wd_tok = load_w(ctx, d["l3_wdownT"], wbuf, pre=ctx.tok("mm"))
    sz_fm = FM(R["arena_b"], 16, 512)
    s_fm = FM(R["xn2"], DC, 512)
    for t in range(TC):
        t0 = t * 512
        @blk.gpsimd
        def _(e, t0=t0):
            ctx.wait(e, ctx.tok("mm", "evv", "ldg2", "scan"))
            for c in range(16):
                ctx.inc("ldg2", e.dma_start(
                    xm_fm.ch(c), act_spill[:, c * T + t0: c * T + t0 + 512]), 16)
                ctx.inc("ldg2", e.dma_start(
                    sz_fm.ch(c), z_spill[:, c * T + t0: c * T + t0 + 512]), 16)
        ld_tok = ctx.tok("ldg2")
        @blk.vector
        def _(e, ld_tok=ld_tok):
            ctx.wait(e, ld_tok + ctx.tok("scan"))
            for c in range(16):
                ctx.inc("evv", e.tensor_scalar(xm_fm.ch(c), xm_fm.ch(c),
                                               skip[:, c:c + 1],
                                               hstate[0:128, c:c + 1],
                                               ALU.mult, ALU.add))
                ctx.wait(e, ctx.tok("evv"))
                ctx.inc("evv", e.tensor_tensor(xm_fm.ch(c), xm_fm.ch(c),
                                               sz_fm.ch(c), ALU.mult))
                ctx.wait(e, ctx.tok("evv"))
            ctx.inc("evv", e.nop())
        h_tok = ctx.tok("evv")
        jobs = []
        for mch in range(DC):
            jobs.append(dict(
                pre=wd_tok + h_tok,
                mms=mm_acc([(wbuf[:, k * 1024 + mch * 128:
                                  k * 1024 + (mch + 1) * 128],
                             xm_fm.ch(k)) for k in range(16)]),
                psum=lambda b: b[:, :],
                evac=(lambda mch: lambda e, ps: e.tensor_scalar_add(
                    s_fm.ch(mch), ps, bdown[:, mch:mch + 1]))(mch),
            ))
        sweep(ctx, jobs=jobs, banks=R["pb"][4:6], evac_eng="vector",
              entry=ctx.tok("evs", "evv"))
        @blk.gpsimd
        def _(e, t0=t0):
            ctx.wait(e, ctx.tok("evv"))
            for c in range(DC):
                ctx.inc("ldg2", e.dma_start(
                    s_spill[:, c * T + t0: c * T + t0 + 512], s_fm.ch(c)), 16)
    if dbg and "dbg_s" in dbg:
        @blk.gpsimd
        def _(e):
            ctx.wait(e, ctx.tok("ldg2"))
            ctx.inc("out", e.dma_start(
                dbg["dbg_s"].rearrange("(c p) t -> p c t", p=128),
                s_spill[:].rearrange("p (c t) -> p c t", t=T)), 16)
    if dbg and "dbg_act" in dbg:
        @blk.gpsimd
        def _(e):
            ctx.wait(e, ctx.tok("ldg2"))
            ctx.inc("out", e.dma_start(
                dbg["dbg_act"][0:128, 0:16 * T].rearrange("p (c t) -> p c t", t=T),
                act_spill[:].rearrange("p (c t) -> p c t", t=T)), 16)
            ctx.inc("out", e.dma_start(
                dbg["dbg_act"][0:128, 16 * T:32 * T].rearrange("p (c t) -> p c t", t=T),
                z_spill[:].rearrange("p (c t) -> p c t", t=T)), 16)
            ctx.inc("out", e.dma_start(dbg["dbg_act"][0:128, 32 * T:32 * T + 16],
                                       hstate[:]), 16)

    y_fm = FM(R["xn2"], DC, 512)
    su_fm = FM(R["arena_d"], 11, 512)
    hh_fm = FM(R["arena_b"], 11, 512)
    ya_fm = FM(R["arena_c"], DC, 512)
    cols = R["cols"]
    for t in range(TC):
        t0 = t * 512
        emit_ln(ctx, R, xn_fm, t0, 512, y_fm, 0, rms=True, eps=1e-6,
                in_f32=False, gain_col=lambda c: rmsw[:, c:c + 1])
        y_tok = ctx.tok("ln_done")

        rjobs = []
        for q in range(4):
            rjobs.append(dict(
                pre=small_tok + y_tok,
                mms=mm_acc([(y_fm.ch(c, q * 128, 128),
                             gw_t[:, c * 8:(c + 1) * 8]) for c in range(DC)]),
                psum=lambda b: b[:, 0:8],
                evac=(lambda q: lambda e, ps: e.tensor_copy(
                    R["ltm"][:, q * 8:(q + 1) * 8], ps))(q),
            ))
        sweep(ctx, jobs=rjobs, banks=R["pb"][0:3], evac_eng="vector",
              entry=ctx.tok("evs", "evv"))
        msk1 = sq[:, 0:8]; mskd = sq[:, 8:16]; msk2 = sq[:, 16:24]
        for q in range(4):
            lg = R["ltm"][:, q * 8:(q + 1) * 8]
            @blk.vector
            def _(e, q=q, lg=lg):
                ctx.wait(e, ctx.tok("evv", "evs"))
                ctx.inc("evv", e.tensor_reduce(cols[:, 0:1], lg, AX.X, ALU.max))
                ctx.wait(e, ctx.tok("evv"))
                ctx.inc("evv", e.tensor_scalar(msk1, lg, cols[:, 0:1], 1.0,
                                               ALU.is_ge, ALU.mult))
                ctx.wait(e, ctx.tok("evv"))
                e.scalar_tensor_tensor(mskd, msk1, -1e30, lg, ALU.mult, ALU.add)
                ctx.inc("evv", e.tensor_reduce(cols[:, 1:2], mskd, AX.X, ALU.max))
                ctx.wait(e, ctx.tok("evv"))
                instr = e.tensor_tensor(cols[:, 2:3], cols[:, 1:2], cols[:, 0:1],
                                        ALU.subtract)
                ctx.inc("evv", instr)
            @blk.scalar
            def _(e, q=q):
                ctx.wait(e, ctx.tok("evv"))
                instr = e.activation(cols[:, 3:4], cols[:, 2:3], AF.Exp)
                ctx.inc("evs", instr)
            @blk.vector
            def _(e, q=q, lg=lg):
                ctx.wait(e, ctx.tok("evs"))
                ctx.inc("evv", e.tensor_single_scalar(cols[:, 4:5], cols[:, 3:4],
                                                      1.0, ALU.add))
                ctx.wait(e, ctx.tok("evv"))
                ctx.inc("evv", e.reciprocal(cols[:, 5:6], cols[:, 4:5]))
                ctx.wait(e, ctx.tok("evv"))
                ctx.inc("evv", e.tensor_tensor(cols[:, 6:7], cols[:, 3:4],
                                               cols[:, 5:6], ALU.mult))
                ctx.wait(e, ctx.tok("evv"))
                e.tensor_scalar(msk2, mskd, cols[:, 1:2], cols[:, 6:7],
                                ALU.is_ge, ALU.mult)
                e.tensor_scalar_mul(msk1, msk1, cols[:, 5:6])
                ctx.inc("evv", e.nop())
                ctx.wait(e, ctx.tok("evv"))
                instr = e.tensor_tensor(lg, msk1, msk2, ALU.add)
                ctx.inc("evv", instr)
        @blk.vector
        def _(e):
            ctx.wait(e, ctx.tok("evv"))
            instr = e.tensor_copy(R["ltm_bf"][:], R["ltm"][:])
            ctx.inc("evv", instr)
        tjobs = []
        for q in range(4):
            tjobs.append(dict(
                pre=ctx.tok("evv"),
                mms=[(lambda q: lambda e, ps: e.transpose(
                    ps, R["ltm_bf"][:, q * 8:(q + 1) * 8], R["iden_b"][:]))(q)],
                psum=lambda b: b[0:8, 0:64].bitcast(BF16),
                evac=(lambda q: lambda e, ps: e.tensor_copy(
                    R["cfm"][:, q * 128: (q + 1) * 128], ps))(q),
            ))
        sweep(ctx, jobs=tjobs, banks=R["pb"][2:4], evac_eng="vector",
              entry=ctx.tok("evs", "evv"))
        cfm_tok = ctx.tok("evv")

        for ex in range(8):
            cjobs = [dict(
                pre=cfm_tok,
                mms=mm_acc([(sel_v[0:8, ex * 128:(ex + 1) * 128],
                             R["cfm"][0:8, 0:512])]),
                psum=lambda b: b[:, :],
                evac=lambda e, ps: e.tensor_copy(R["coeffB"], ps),
            )]
            sweep(ctx, jobs=cjobs, banks=[R["pb"][2]], evac_eng="vector",
                  entry=ctx.tok("evs", "evv"))
            coefb_tok = ctx.tok("evv")

            for fi_, (k0, nk) in enumerate([(0, 8), (8, 7), (15, 7)]):
                f0 = k0 * 128
                fw = nk * 128
                slot = fi_ % 2
                sl = wbuf[:, slot * 8192: slot * 8192 + 8192]
                w_tok = load_w(ctx, d["l3_w1T"][ex][:, f0:f0 + fw], wbuf,
                               col0=slot * 8192, pre=ctx.buf_pre(("l3w", slot)))
                jobs = []
                for mch in range(nk):
                    jobs.append(dict(
                        pre=w_tok + y_tok,
                        mms=mm_acc([(sl[:, c * fw + mch * 128:
                                        c * fw + (mch + 1) * 128],
                                     y_fm.ch(c)) for c in range(DC)]),
                        psum=lambda b: b[:, :],
                        evac=(lambda mch: lambda e, ps: e.activation(
                            su_fm.ch(mch), ps, AF.Silu))(mch),
                    ))
                sweep(ctx, jobs=jobs, banks=R["pb"][0:3], evac_eng="scalar",
                      entry=ctx.tok("evs", "evv"))
                ctx.buf_mark(("l3w", slot))
                su_tok = ctx.tok("evs")
                slot2 = (fi_ + 1) % 2
                sl2 = wbuf[:, slot2 * 8192: slot2 * 8192 + 8192]
                w_tok = load_w(ctx, d["l3_w2T"][ex][:, f0:f0 + fw], wbuf,
                               col0=slot2 * 8192, pre=ctx.buf_pre(("l3w", slot2)))
                jobs = []
                for mch in range(nk):
                    jobs.append(dict(
                        pre=w_tok + su_tok,
                        mms=mm_acc([(sl2[:, c * fw + mch * 128:
                                         c * fw + (mch + 1) * 128],
                                     y_fm.ch(c)) for c in range(DC)]),
                        psum=lambda b: b[:, :],
                        evac=(lambda mch: lambda e, ps: e.tensor_tensor(
                            hh_fm.ch(mch), ps, su_fm.ch(mch), ALU.mult))(mch),
                    ))
                sweep(ctx, jobs=jobs, banks=R["pb"][0:3], evac_eng="vector",
                      entry=ctx.tok("evs", "evv"))
                ctx.buf_mark(("l3w", slot2))
                hc_tok = ctx.tok("evv")
                w_tok = load_w(ctx, d["l3_w3T"][ex][f0:f0 + fw, :], wbuf,
                               col0=slot * 8192, pre=ctx.buf_pre(("l3w", slot)),
                               group=4)
                jobs = []
                y3tmp = R["arena_d"].bitcast(F32)
                for mch in range(DC):
                    def mk_ev(ex=ex, fi_=fi_, mch=mch):
                        def f(e, ps):
                            tv = y3tmp[:, 2816 + (mch % 2) * 512:
                                       2816 + (mch % 2) * 512 + 512]
                            e.tensor_tensor(tv, ps, R["coeffB"], ALU.mult)
                            if ex == 0 and fi_ == 0:
                                return e.tensor_copy(ya_fm.ch(mch), tv)
                            return e.tensor_tensor(ya_fm.ch(mch), ya_fm.ch(mch),
                                                   tv, ALU.add)
                        return f
                    jobs.append(dict(
                        pre=w_tok + hc_tok + coefb_tok,
                        mms=mm_acc([(sl[:, k * 1024 + mch * 128:
                                        k * 1024 + (mch + 1) * 128],
                                     hh_fm.ch(k)) for k in range(nk)]),
                        psum=lambda b: b[:, :],
                        evac=mk_ev(),
                    ))
                sweep(ctx, jobs=jobs, banks=R["pb"][0:3], evac_eng="vector",
                      entry=ctx.tok("evs", "evv"))
                ctx.buf_mark(("l3w", slot))

        ya_tok = ctx.tok("evv")
        @blk.vector
        def _(e, t0=t0, ya_tok=ya_tok):
            ctx.wait(e, ya_tok)
            instr = None
            for c in range(DC):
                instr = e.tensor_tensor(ya_fm.ch(c), ya_fm.ch(c),
                                        xn_fm.ch(c, t0, 512), ALU.add)
            ctx.inc("evv", instr)
        @blk.scalar
        def _(e, t0=t0):
            ctx.wait(e, ctx.tok("evv") + ctx.tok("ldg2"))
            instr = None
            for c in range(DC):
                instr = e.activation(y_fm.ch(c), ya_fm.ch(c), AF.Silu)
            ctx.inc("evs", instr)
        @blk.gpsimd
        def _(e, t0=t0):
            ctx.wait(e, ctx.tok("evs"))
            for c in range(DC):
                ctx.inc("ldg2", e.dma_start(
                    g_spill[:, c * T + t0: c * T + t0 + 512], y_fm.ch(c)), 16)
    if dbg and "dbg_g" in dbg:
        @blk.gpsimd
        def _(e):
            ctx.wait(e, ctx.tok("ldg2"))
            ctx.inc("out", e.dma_start(
                dbg["dbg_g"].rearrange("(c p) t -> p c t", p=128),
                g_spill[:].rearrange("p (c t) -> p c t", t=T)), 16)

    emit_combine_spill(ctx, R, g_spill, s_spill)
def build_layer(layer, dbg_names=()):
    nc = bass.Bass("TRN2", target_bir_lowering=False)
    d = {}
    d["iden_bf16"] = nc.dram_tensor("iden_bf16", [128, 128], BF16,
                                    kind="ExternalInput").ap()
    d["causal"] = nc.dram_tensor("causal", [128, 128], F32,
                                 kind="ExternalInput").ap()
    d["sel"] = nc.dram_tensor("sel", [8, 8 * 128], BF16,
                              kind="ExternalInput").ap()
    xin = nc.dram_tensor("xin", [D, T], F32, kind="ExternalInput").ap()
    xout = nc.dram_tensor("xout", [D, T], F32, kind="ExternalOutput").ap()
    dbg = {}
    for nm in dbg_names:
        shape = [8 + 128, T] if nm == "dbg_coeff" else [D, T]
        dt_ = BF16 if nm in ("dbg_att",) else F32
        if nm == "dbg_att":
            shape = [128, 4 * T]
        if nm == "dbg_act":
            shape = [128, 32 * T + 16 + 1024]
            dt_ = F32
        dbg[nm] = nc.dram_tensor(nm, shape, dt_, kind="ExternalOutput").ap()
    g_spill = nc.dram_tensor("g_spill", [128, DC * T], BF16, kind="Internal").ap()
    s_spill = nc.dram_tensor("s_spill", [128, DC * T], BF16, kind="Internal").ap()
    x_spill = nc.dram_tensor("x_spill", [128, DC * T], BF16, kind="Internal").ap()

    if layer == 0:
        l0_declare(nc, d)
    elif layer == 1:
        l1_declare(nc, d)
    elif layer == 2:
        l2_declare(nc, d)
    elif layer == 3:
        l3_declare(nc, d)
    R = None
    with nc.Block() as blk:
        ctx = Ctx(nc, blk)
        for s_ in ["mm", "evs", "evv", "evp", "ldg", "ldg2", "ld", "sq",
                   "statmm", "lnv", "lns", "ln_done", "out", "comb", "cc",
                   "scan"]:
            ctx.sem(s_)
        R = alloc_resources(nc)
        emit_init(ctx, R, d, xin)
        if layer == 0:
            emit_l0(ctx, R, d, g_spill, dbg)
        elif layer == 1:
            emit_l1(ctx, R, d, g_spill, s_spill, x_spill, dbg)
        elif layer == 2:
            emit_l2(ctx, R, d, g_spill, s_spill, dbg)
        elif layer == 3:
            emit_l3(ctx, R, d, g_spill, s_spill, dbg)
        @blk.sync
        def _(e):
            ctx.wait(e, ctx.tok("comb"))
            instr = e.dma_start(xout.rearrange("(c p) t -> p c t", p=128),
                                R["x"][:].rearrange("p (c t) -> p c t", t=T))
            ctx.inc("out", instr, 16)
            e.wait_ge(ctx.sem("out"), ctx.cnt["out"])
    return nc


def build_full():
    nc = bass.Bass("TRN2", target_bir_lowering=False)
    d = {}
    d["iden_bf16"] = nc.dram_tensor("iden_bf16", [128, 128], BF16,
                                    kind="ExternalInput").ap()
    d["causal"] = nc.dram_tensor("causal", [128, 128], F32,
                                 kind="ExternalInput").ap()
    d["sel"] = nc.dram_tensor("sel", [8, 8 * 128], BF16,
                              kind="ExternalInput").ap()
    xin = nc.dram_tensor("xin", [D, T], F32, kind="ExternalInput").ap()
    xout = nc.dram_tensor("xout", [D, T], F32, kind="ExternalOutput").ap()
    g_spill = nc.dram_tensor("g_spill", [128, DC * T], BF16, kind="Internal").ap()
    s_spill = nc.dram_tensor("s_spill", [128, DC * T], BF16, kind="Internal").ap()
    x_spill = nc.dram_tensor("x_spill", [128, DC * T], BF16, kind="Internal").ap()
    l0_declare(nc, d)
    l1_declare(nc, d)
    l2_declare(nc, d)
    l3_declare(nc, d)
    with nc.Block() as blk:
        ctx = Ctx(nc, blk)
        for s_ in ["mm", "evs", "evv", "evp", "ldg", "ldg2", "ld", "sq",
                   "statmm", "lnv", "lns", "ln_done", "out", "comb", "cc",
                   "scan"]:
            ctx.sem(s_)
        R = alloc_resources(nc)
        emit_init(ctx, R, d, xin)
        emit_l0(ctx, R, d, g_spill, None)
        emit_l1(ctx, R, d, g_spill, s_spill, x_spill, None)
        emit_l2(ctx, R, d, g_spill, s_spill, None)
        emit_l3(ctx, R, d, g_spill, s_spill, None)
        @blk.sync
        def _(e):
            ctx.wait(e, ctx.tok("comb"))
            instr = e.dma_start(xout.rearrange("(c p) t -> p c t", p=128),
                                R["x"][:].rearrange("p (c t) -> p c t", t=T))
            ctx.inc("out", instr, 16)
            e.wait_ge(ctx.sem("out"), ctx.cnt["out"])
    return nc


_NC_CACHE = {}


def kernel(**inputs):
    common = _prep_common()
    base = dict(common)
    l0_prep(inputs, base)
    l1_prep(inputs, base)
    l3_prep(inputs, base)
    x = np.asarray(inputs["L0_attn_x"], np.float32)
    if "full" not in _NC_CACHE:
        _NC_CACHE["full"] = build_full()
    nc_ = _NC_CACHE["full"]
    in_maps = []
    for b in range(NCORES):
        m = dict(base)
        l2_prep(inputs, m, b)
        m["xin"] = np.ascontiguousarray(x[b].T)
        in_maps.append(m)
    res = run_bass_kernel_spmd(nc_, in_maps, core_ids=list(range(NCORES)))
    out = np.stack([np.asarray(r["xout"], np.float32).T for r in res.results], 0)
    return np.ascontiguousarray(out)


def kernel_unfused(**inputs):
    common = _prep_common()
    base = dict(common)
    l0_prep(inputs, base)
    l1_prep(inputs, base)
    l3_prep(inputs, base)
    l2_maps = []
    for b in range(NCORES):
        m2 = {}
        l2_prep(inputs, m2, b)
        l2_maps.append(m2)

    x = np.asarray(inputs["L0_attn_x"], np.float32)
    xin = [np.ascontiguousarray(x[b].T) for b in range(NCORES)]

    keysets = {
        0: [k for k in base if k.startswith("l0_")],
        1: [k for k in base if k.startswith("l1_")],
        2: None,
        3: [k for k in base if k.startswith("l3_")],
    }
    ckeys = [k for k in common]
    for layer in range(4):
        if layer not in _NC_CACHE:
            _NC_CACHE[layer] = build_layer(layer)
        nc_ = _NC_CACHE[layer]
        in_maps = []
        for b in range(NCORES):
            m = {k: base[k] for k in ckeys}
            if layer == 2:
                m.update(l2_maps[b])
            else:
                for k in keysets[layer]:
                    m[k] = base[k]
            m["xin"] = xin[b]
            in_maps.append(m)
        res = run_bass_kernel_spmd(nc_, in_maps, core_ids=list(range(NCORES)))
        xin = [np.ascontiguousarray(np.asarray(r["xout"], np.float32))
               for r in res.results]

    out = np.stack([xin[b].T for b in range(NCORES)], 0)
    return np.ascontiguousarray(out)


# version 09
# speedup vs baseline: 1.1431x; 1.0168x over previous
import sys
sys.path.insert(0, '/opt/trn_rl_repo')
import numpy as np
import concourse.bass as bass
import concourse.mybir as mybir
from concourse.bass_utils import run_bass_kernel_spmd

F32 = mybir.dt.float32
BF16 = mybir.dt.bfloat16
I32 = mybir.dt.int32
AF = mybir.ActivationFunctionType
ALU = mybir.AluOpType
AX = mybir.AxisListType

NCORES = 8
D = 1024
T = 2048
DC = 8
TC = 4
LN_EPS = 1e-5

DEBUG = False
DBG_NOROPE = False
DBG_ROPEONLY = False


class Ctx:
    def __init__(self, nc, blk):
        self.nc, self.blk = nc, blk
        self.sem_h, self.cnt = {}, {}
        self.buf_tok = {}

    def buf_pre(self, name):
        return self.buf_tok.get(name, [])

    def buf_mark(self, name):
        self.buf_tok[name] = self.tok("mm")

    def sem(self, name):
        if name not in self.sem_h:
            self.sem_h[name] = self.nc.alloc_semaphore(name)
            self.cnt[name] = 0
        return self.sem_h[name]

    def inc(self, name, instr, k=1):
        instr.then_inc(self.sem(name), k)
        self.cnt[name] += k
        return (name, self.cnt[name])

    def tok(self, *names):
        return [(n, self.cnt.get(n, 0)) for n in names]

    def wait(self, eng, toks):
        for (n, v) in toks:
            if v > 0:
                eng.wait_ge(self.sem(n), v)


class FM:
    def __init__(self, tile, nch, tlen, col0=0):
        self.t, self.nch, self.T, self.c0 = tile, nch, tlen, col0

    def ch(self, c, t0=0, tw=None, p0=0, pw=128):
        tw = self.T - t0 if tw is None else tw
        o = self.c0 + c * self.T + t0
        return self.t[p0:p0 + pw, o: o + tw]


def sweep(ctx, *, jobs, banks, evac_eng, entry=(), mm_sem="mm"):
    blk = ctx.blk
    nb = len(banks)
    ev_sem = "evs" if evac_eng == "scalar" else ("evv" if evac_eng == "vector" else "evp")
    ev_base = ctx.cnt.get(ev_sem, 0)
    ctx.sem(ev_sem)
    mm_toks = []

    @blk.tensor
    def _(e):
        for i, job in enumerate(jobs):
            ctx.wait(e, job.get("pre", ()))
            if i < nb:
                ctx.wait(e, entry)
            else:
                e.wait_ge(ctx.sem(ev_sem), ev_base + i - nb + 1)
            ps = job["psum"](banks[i % nb])
            instr = None
            for mm in job["mms"]:
                instr = mm(e, ps)
            mm_toks.append(ctx.inc(mm_sem, instr))

    @getattr(blk, evac_eng)
    def _(e):
        for i, job in enumerate(jobs):
            ctx.wait(e, [mm_toks[i]])
            instr = job["evac"](e, job["psum"](banks[i % nb]))
            ctx.inc(ev_sem, instr)

    return ctx.tok(ev_sem)


def mm_acc(pairs):
    n = len(pairs)
    out = []
    for j, item in enumerate(pairs):
        lt, rh = item[0], item[1]
        tp = item[2] if len(item) > 2 else None
        def mk(lt=lt, rh=rh, j=j, tp=tp):
            return lambda e, ps: e.matmul(ps, lt, rh, start=(j == 0),
                                          stop=(j == n - 1), tile_position=tp)
        out.append(mk())
    return out


def load_w(ctx, dram_ap, dst_tile, *, col0=0, group=8, sem="ldg", pre=()):
    blk = ctx.blk
    K, M = dram_ap.shape
    kch = K // 128

    @blk.gpsimd
    def _(e):
        ctx.wait(e, pre)
        c = 0
        while c < kch:
            g = min(group, kch - c)
            s3 = dram_ap[c * 128:(c + g) * 128, :].rearrange(
                "(c p) m -> p c m", p=128)
            d3 = dst_tile[:, col0 + c * M: col0 + (c + g) * M].rearrange(
                "p (c m) -> p c m", m=M)
            instr = e.dma_start(d3, s3)
            ctx.inc(sem, instr, 16)
            c += g
    return ctx.tok(sem)


def emit_ln(ctx, R, x_fm, xoff, tlen, out_fm, ooff, *, rms=False, eps=LN_EPS,
            in_f32=True, gain_col=None, bias_col=None, nch=DC):
    blk = ctx.blk
    inv = 1.0 / (nch * 128)
    ones_s = (R["ones_m1f"] if in_f32 else R["ones_m1b"])[:]
    psA, psB = R["pb"][6], R["pb"][7]
    sq = R["sq"]
    rA, rB, r1, r2 = R["row_a"], R["row_b"], R["row_t1"], R["row_t2"]

    for t in range(tlen // 512):
        t0 = xoff + t * 512
        o0 = ooff + t * 512
        groups = [list(range(0, min(4, nch)))]
        if nch > 4:
            groups.append(list(range(4, nch)))
        for gi, grp in enumerate(groups):
            @blk.scalar
            def _(e, grp=grp, gi=gi):
                ctx.wait(e, ctx.tok("ln_done" if gi == 0 else "statmm"))
                instr = None
                for c in grp:
                    instr = e.activation(sq[:, (c % 4) * 512:(c % 4 + 1) * 512],
                                         x_fm.ch(c, t0, 512), AF.Square)
                ctx.inc("sq", instr)
            sq_tok = ctx.tok("sq")

            @blk.tensor
            def _(e, grp=grp, gi=gi, sq_tok=sq_tok):
                ctx.wait(e, sq_tok)
                if gi == 0:
                    ctx.wait(e, ctx.tok("ln_done") + ctx.tok("evs", "evv"))
                instr = None
                if not rms:
                    for c in grp:
                        instr = e.matmul(psA[0:1, :], ones_s, x_fm.ch(c, t0, 512),
                                         start=(c == grp[0] and gi == 0),
                                         stop=(c == grp[-1] and gi == len(groups) - 1))
                for c in grp:
                    instr = e.matmul(psB[0:1, :], R["ones_m1f"][:],
                                     sq[:, (c % 4) * 512:(c % 4 + 1) * 512],
                                     start=(c == grp[0] and gi == 0),
                                     stop=(c == grp[-1] and gi == len(groups) - 1))
                ctx.inc("statmm", instr)

        @blk.vector
        def _(e):
            ctx.wait(e, ctx.tok("statmm"))
            if not rms:
                e.tensor_single_scalar(r1[:], psA[0:1, :], inv, ALU.mult)
            instr = e.tensor_single_scalar(r2[:], psB[0:1, :], inv, ALU.mult)
            ctx.inc("lnv", instr)
        @blk.scalar
        def _(e):
            ctx.wait(e, ctx.tok("lnv"))
            if not rms:
                instr = e.activation(rA[:], r1[:], AF.Square)
                ctx.inc("lns", instr)
        if not rms:
            @blk.vector
            def _(e):
                ctx.wait(e, ctx.tok("lns"))
                instr = e.tensor_tensor(r2[:], r2[:], rA[:], ALU.subtract)
                ctx.inc("lnv", instr)
        @blk.scalar
        def _(e):
            ctx.wait(e, ctx.tok("lnv"))
            e.activation(rA[:], r2[:], AF.Ln, bias=cb(R, eps, 1))
            instr = e.activation(rA[:], rA[:], AF.Exp, scale=-0.5)
            ctx.inc("lns", instr)
        @blk.vector
        def _(e):
            ctx.wait(e, ctx.tok("lns"))
            if not rms:
                instr = e.scalar_tensor_tensor(rB[:], r1[:], -1.0, rA[:],
                                               ALU.mult, ALU.mult)
                ctx.inc("lnv", instr)
        @blk.tensor
        def _(e):
            ctx.wait(e, ctx.tok("lnv", "lns"))
            e.matmul(psA[:, :], R["ones_k1f"][:], rA[:], start=True, stop=True)
            instr = e.matmul(psB[:, :], R["ones_k1f"][:],
                             rA[:] if rms else rB[:], start=True, stop=True)
            ctx.inc("statmm", instr)
        bc_tok = ctx.tok("statmm")

        @blk.vector
        def _(e):
            ctx.wait(e, bc_tok + ctx.tok("mm", "ldg2", "lds"))
            for c in range(nch):
                tmp = sq[:, (c % 4) * 512:(c % 4 + 1) * 512]
                ctx.inc("evv", e.tensor_tensor(tmp, x_fm.ch(c, t0, 512),
                                               psA[:, :], ALU.mult))
                ctx.wait(e, ctx.tok("evv"))
                if not rms:
                    ctx.inc("evv", e.tensor_tensor(tmp, tmp, psB[:, :], ALU.add))
                    ctx.wait(e, ctx.tok("evv"))
                if gain_col is not None and bias_col is not None:
                    instr = e.tensor_scalar(out_fm.ch(c, o0, 512), tmp,
                                            gain_col(c), bias_col(c),
                                            ALU.mult, ALU.add)
                elif gain_col is not None:
                    instr = e.tensor_scalar_mul(out_fm.ch(c, o0, 512), tmp,
                                                gain_col(c))
                else:
                    instr = e.tensor_copy(out_fm.ch(c, o0, 512), tmp)
                ctx.inc("evv", instr)
                ctx.wait(e, ctx.tok("evv"))
            ctx.inc("ln_done", e.nop())
    return ctx.tok("ln_done")


def _prep_common():
    import ml_dtypes
    C = {}
    C["iden_bf16"] = np.eye(128, dtype=ml_dtypes.bfloat16)
    C["causal"] = np.where(np.arange(128)[None, :] <= np.arange(128)[:, None],
                           0.0, -1e30).astype(np.float32)
    sel = np.zeros((8, 8 * 128), dtype=np.float32)
    for e in range(8):
        sel[e, e * 128:(e + 1) * 128] = 1.0
    C["sel"] = sel.astype(ml_dtypes.bfloat16)
    return C


def _col_layout(v, nch):
    return np.ascontiguousarray(np.asarray(v, np.float32).reshape(nch, 128).T)


def alloc_resources(nc):
    sb = lambda n, s, dt: nc.alloc_sbuf_tensor(n, s, dt)
    R = {}
    R["pb"] = [nc.alloc_psum_tensor(f"pb{i}", [128, 512], F32) for i in range(8)]
    R["x"] = sb("x", [128, DC * T], F32)
    R["xn"] = sb("xn", [128, DC * T], BF16)
    R["arena_a"] = sb("arena_a", [128, 16 * 1024], BF16)
    R["arena_b"] = sb("arena_b", [128, 8 * 1024], BF16)
    R["arena_c"] = sb("arena_c", [128, 8 * 512], F32)
    R["arena_d"] = sb("arena_d", [128, 16 * 512], BF16)
    R["sq"] = sb("sq", [128, 4 * 512], F32)
    R["xn2"] = sb("xn2", [128, DC * 512], BF16)
    R["sbuf_g"] = R["xn2"]
    R["sbuf_s"] = R["arena_c"].bitcast(BF16)
    R["row_a"] = sb("row_a", [1, 512], F32)
    R["row_b"] = sb("row_b", [1, 512], F32)
    R["row_t1"] = sb("row_t1", [1, 512], F32)
    R["row_t2"] = sb("row_t2", [1, 512], F32)
    R["ones_m1f"] = sb("ones_m1f", [128, 1], F32)
    R["ones_m1b"] = sb("ones_m1b", [128, 1], BF16)
    R["ones_k1f"] = sb("ones_k1f", [1, 128], F32)
    R["cols"] = sb("cols", [128, 8], F32)
    R["cbias"] = sb("cbias", [128, 8], F32)
    R["smf"] = sb("smf", [128, 192], F32)
    R["smb"] = sb("smb", [128, 2304], BF16)
    R["cbias_vals"] = {1e-5: 0, 1e-6: 1, 1e-8: 2, 1.0 + 1e-8: 3,
                       0.5 + 1e-8: 4, 1.0: 5}
    R["iden_b"] = sb("iden_b", [128, 128], BF16)
    R["causal_t"] = sb("causal_t", [128, 128], F32)
    R["cfm"] = R["sq"].bitcast(BF16)[0:8, 512:1024]
    R["ltm"] = sb("ltm", [128, 4 * 8], F32)
    R["ltm_bf"] = sb("ltm_bf", [128, 4 * 8], BF16)
    R["coeffB"] = R["sq"].bitcast(BF16)[0:128, 0:512]
    return R


def cb(R, v, pw=128):
    c = R["cbias_vals"][v]
    return R["cbias"][0:pw, c:c + 1]


def emit_init(ctx, R, d, xin):
    blk = ctx.blk

    @blk.sync
    def _(e):
        instr = e.dma_start(R["x"][:].rearrange("p (c t) -> p c t", t=T),
                            xin.rearrange("(c p) t -> p c t", p=128))
        ctx.inc("ld", instr, 16)
        for src, dst in [("iden_bf16", R["iden_b"]),
                         ("causal", R["causal_t"])]:
            if src in d:
                instr = e.dma_start(dst[:], d[src][:])
                ctx.inc("ld", instr, 16)

    @blk.gpsimd
    def _(e):
        e.memset(R["ones_m1f"][:], 1.0)
        e.memset(R["ones_m1b"][:], 1.0)
        e.memset(R["ones_k1f"][:], 1.0)
        instr = None
        for v, c in R["cbias_vals"].items():
            instr = e.memset(R["cbias"][:, c:c + 1], float(v))
        ctx.inc("evp", instr)

    @blk.scalar
    def _(e):
        ctx.wait(e, ctx.tok("ld", "evp"))
        ctx.inc("ln_done", e.nop())
    return ctx.tok("ld", "evp")


def emit_combine(ctx, R, t):
    blk = ctx.blk
    t0 = t * 512
    s_fm, g_fm = FM(R["sbuf_s"], DC, 512), FM(R["sbuf_g"], DC, 512)
    x_fm = FM(R["x"], DC, T)

    tmp_ap = R["arena_c"][:, 2048:2560]
    @blk.vector
    def _(e):
        ctx.wait(e, ctx.tok("evs", "evv", "ldg2", "lds"))
        for c in range(DC):
            ctx.inc("evv", e.tensor_tensor(tmp_ap, s_fm.ch(c), g_fm.ch(c),
                                           ALU.mult))
            ctx.wait(e, ctx.tok("evv"))
            ctx.inc("evv", e.tensor_tensor(x_fm.ch(c, t0, 512),
                                           x_fm.ch(c, t0, 512), tmp_ap, ALU.add))
            ctx.wait(e, ctx.tok("evv"))
        ctx.inc("comb", e.nop())
        ctx.inc("ln_done", e.nop())
    return ctx.tok("comb")


def emit_combine_spill(ctx, R, g_spill, s_spill):
    blk = ctx.blk
    s_fm = FM(R["sbuf_s"], DC, 512)
    g_fm = FM(R["sbuf_g"], DC, 512)
    for t in range(TC):
        t0 = t * 512
        @blk.sync
        def _(e, t0=t0):
            ctx.wait(e, ctx.tok("comb") + ctx.tok("mm", "evv", "evs", "ldg2"))
            for c in range(DC):
                ctx.inc("lds", e.dma_start(
                    g_fm.ch(c), g_spill[:, c * T + t0: c * T + t0 + 512]), 16)
                ctx.inc("lds", e.dma_start(
                    s_fm.ch(c), s_spill[:, c * T + t0: c * T + t0 + 512]), 16)
        ld_tok = ctx.tok("lds")
        @blk.vector
        def _(e, ld_tok=ld_tok):
            ctx.wait(e, ld_tok)
            ctx.inc("evv", e.nop())
        emit_combine(ctx, R, t)


def l0_prep(I, m):
    m["l0_wq"] = np.ascontiguousarray(I["L0_attn_Wq"])
    m["l0_wk"] = np.ascontiguousarray(I["L0_attn_Wk"])
    m["l0_wv"] = np.ascontiguousarray(I["L0_attn_Wv"])
    m["l0_wo"] = np.ascontiguousarray(I["L0_attn_Wo"])
    m["l0_wr"] = np.ascontiguousarray(I["L0_ffn_Wr"])
    m["l0_br"] = np.asarray(I["L0_ffn_br"], np.float32).reshape(1, 8)
    m["l0_lng"] = _col_layout(I["L0_ffn_ln_g"], DC)
    m["l0_lnb"] = _col_layout(I["L0_ffn_ln_b"], DC)
    m["l0_w1"] = np.ascontiguousarray(I["L0_ffn_W1"])
    m["l0_b1"] = np.stack([_col_layout(I["L0_ffn_b1"][e], 32) for e in range(8)])
    m["l0_w2"] = np.ascontiguousarray(I["L0_ffn_W2"])
    m["l0_b2"] = np.asarray(I["L0_ffn_b2"], np.float32)


def l0_declare(nc, d):
    def din(name, shape, dt=F32):
        d[name] = nc.dram_tensor(name, shape, dt, kind="ExternalInput").ap()
    din("l0_wq", [1024, 1024]); din("l0_wk", [1024, 1024])
    din("l0_wv", [1024, 1024]); din("l0_wo", [1024, 1024])
    din("l0_wr", [1024, 8]); din("l0_br", [1, 8])
    din("l0_lng", [128, DC]); din("l0_lnb", [128, DC])
    din("l0_w1", [8, 1024, 4096]); din("l0_b1", [8, 128, 32])
    din("l0_w2", [8, 4096, 1024]); din("l0_b2", [8, 1024])


def emit_l0(ctx, R, d, g_spill, dbg=None):
    blk = ctx.blk
    nc = ctx.nc
    x_fm = FM(R["x"], DC, T)
    xn_fm = FM(R["xn"], DC, T)

    smf, smb = R["smf"], R["smb"]
    lng = smf[0:128, 0:8]
    lnb = smf[0:128, 8:16]
    b1_t = smf[0:128, 16:48]
    br_t = smf[0:1, 48:56]
    brB = smf[0:128, 56:64]
    wr_t = smb[0:128, 0:64]
    b2_t = smb[0:8, 64:1088]
    sel_v = smb[0:8, 1088:2112]

    @blk.sync
    def _(e):
        ctx.wait(e, ctx.tok("comb"))
        for src, dst in [("l0_lng", lng), ("l0_lnb", lnb), ("l0_br", br_t),
                         ("sel", sel_v)]:
            instr = e.dma_start(dst, d[src][:])
            ctx.inc("ld", instr, 16)
    @blk.gpsimd
    def _(e):
        instr = e.dma_start(b2_t, d["l0_b2"][:])
        ctx.inc("ldg", instr, 16)
    load_w(ctx, d["l0_wr"], wr_t)
    small_tok = ctx.tok("ld", "ldg")

    ln_tok = emit_ln(ctx, R, x_fm, 0, T, xn_fm, 0, rms=False, in_f32=True)

    @blk.tensor
    def _(e):
        ctx.wait(e, small_tok + ln_tok)
        instr = e.matmul(R["pb"][5][:, 0:8], R["ones_k1f"][:], br_t,
                         start=True, stop=True)
        ctx.inc("mm", instr)
    @blk.vector
    def _(e):
        ctx.wait(e, ctx.tok("mm"))
        instr = e.tensor_copy(brB, R["pb"][5][:, 0:8])
        ctx.inc("evv", instr)

    if dbg and "dbg_xn" in dbg:
        @blk.gpsimd
        def _(e):
            ctx.wait(e, ln_tok)
            instr = e.dma_start(dbg["dbg_xn"].rearrange("(c p) t -> p c t", p=128),
                                R["xn"][:].rearrange("p (c t) -> p c t", t=T))
            ctx.inc("out", instr, 16)

    h_fm = FM(R["arena_d"], 16, 512)
    y_fm = FM(R["arena_c"], DC, 512)
    wb1 = R["arena_b"]
    wb2 = R["arena_a"]
    xn2_fm = FM(R["xn2"], DC, 512)

    for t in range(TC):
        t0 = t * 512
        emit_ln(ctx, R, xn_fm, t0, 512, xn2_fm, 0, rms=False, in_f32=False,
                gain_col=lambda c: lng[:, c:c + 1],
                bias_col=lambda c: lnb[:, c:c + 1])
        xn2_tok = ctx.tok("ln_done")

        rjobs = []
        for q in range(4):
            rjobs.append(dict(
                pre=small_tok + ctx.tok("evv"),
                mms=mm_acc([(xn_fm.ch(c, t0 + q * 128, 128),
                             wr_t[:, c * 8:(c + 1) * 8]) for c in range(DC)]),
                psum=lambda b: b[:, 0:8],
                evac=(lambda q: lambda e, ps: e.tensor_tensor(
                    R["ltm"][:, q * 8:(q + 1) * 8], ps, brB, ALU.add))(q),
            ))
        sweep(ctx, jobs=rjobs, banks=R["pb"][0:2], evac_eng="vector",
              entry=ctx.tok("evs", "evv"))

        cols = R["cols"]
        for q in range(4):
            lg = R["ltm"][:, q * 8:(q + 1) * 8]
            @blk.vector
            def _(e, q=q, lg=lg):
                ctx.wait(e, ctx.tok("evs"))
                instr = e.tensor_reduce(cols[:, 0:1], lg, AX.X, ALU.max,
                                        negate=True)
                ctx.inc("evv", instr)
            @blk.scalar
            def _(e, q=q, lg=lg):
                ctx.wait(e, ctx.tok("evv"))
                instr = e.activation(lg, lg, AF.Exp, bias=cols[:, 0:1])
                ctx.inc("evs", instr)
            @blk.vector
            def _(e, q=q, lg=lg):
                ctx.wait(e, ctx.tok("evs"))
                ctx.inc("evv", e.tensor_reduce(cols[:, 1:2], lg, AX.X, ALU.add))
                ctx.inc("evv", e.tensor_reduce(cols[:, 3:4], lg, AX.X, ALU.max))
                ctx.wait(e, ctx.tok("evv"))
                ctx.inc("evv", e.reciprocal(cols[:, 2:3], cols[:, 1:2]))
                ctx.wait(e, ctx.tok("evv"))
                ctx.inc("evv", e.tensor_tensor(cols[:, 4:5], cols[:, 3:4],
                                               cols[:, 2:3], ALU.mult))
                ctx.wait(e, ctx.tok("evv"))
                instr = e.tensor_scalar(lg, lg, cols[:, 3:4], cols[:, 4:5],
                                        ALU.is_ge, ALU.mult)
                ctx.inc("evv", instr)
            @blk.scalar
            def _(e, q=q):
                ctx.wait(e, ctx.tok("evv"))
                ctx.inc("evs", e.nop())

        @blk.vector
        def _(e):
            ctx.wait(e, ctx.tok("evs"))
            instr = e.tensor_copy(R["ltm_bf"][:], R["ltm"][:])
            ctx.inc("evv", instr)
        tjobs = []
        for q in range(4):
            tjobs.append(dict(
                pre=ctx.tok("evv"),
                mms=[(lambda q: lambda e, ps: e.transpose(
                    ps, R["ltm_bf"][:, q * 8:(q + 1) * 8], R["iden_b"][:]))(q)],
                psum=lambda b: b[0:8, 0:64].bitcast(BF16),
                evac=(lambda q: lambda e, ps: e.tensor_copy(
                    R["cfm"][:, q * 128: (q + 1) * 128], ps))(q),
            ))
        sweep(ctx, jobs=tjobs, banks=R["pb"][0:2], evac_eng="vector",
              entry=ctx.tok("evs", "evv"))
        cfm_tok = ctx.tok("evv")
        if dbg and "dbg_coeff" in dbg:
            @blk.gpsimd
            def _(e, t=t):
                ctx.wait(e, cfm_tok)
                instr = e.dma_start(dbg["dbg_coeff"][0:8, t * 512:(t + 1) * 512],
                                    R["cfm"][0:8, 0:512])
                ctx.inc("out", instr, 16)
                instr = e.dma_start(dbg["dbg_coeff"][8:8 + 128,
                                                     t * 32:(t + 1) * 32],
                                    R["ltm"][:])
                ctx.inc("out", instr, 16)

        bjobs = []
        for mch in range(DC):
            bjobs.append(dict(
                pre=cfm_tok + small_tok,
                mms=mm_acc([(b2_t[0:8, mch * 128:(mch + 1) * 128],
                             R["cfm"][0:8, 0:512])]),
                psum=lambda b: b[:, :],
                evac=(lambda mch: lambda e, ps: e.tensor_copy(
                    y_fm.ch(mch), ps))(mch),
            ))
        sweep(ctx, jobs=bjobs, banks=R["pb"][0:2], evac_eng="vector",
              entry=ctx.tok("evs", "evv"))

        for ex in range(8):
            cjobs = [dict(
                pre=cfm_tok,
                mms=mm_acc([(sel_v[0:8, ex * 128:(ex + 1) * 128],
                             R["cfm"][0:8, 0:512])]),
                psum=lambda b: b[:, :],
                evac=lambda e, ps: e.tensor_copy(R["coeffB"], ps),
            )]
            sweep(ctx, jobs=cjobs, banks=[R["pb"][2]], evac_eng="vector",
                  entry=ctx.tok("evs", "evv"))
            coefb_tok = ctx.tok("evv")

            @blk.sync
            def _(e, ex=ex):
                ctx.wait(e, ctx.tok("evs"))
                instr = e.dma_start(b1_t, d["l0_b1"][ex])
                ctx.inc("ld", instr, 16)
            b1_tok = ctx.tok("ld")

            for fc in range(4):
                f0 = fc * 1024
                slot = fc % 2
                sl_w1 = wb2[:, slot * 8192: slot * 8192 + 8192]
                w1_tok = load_w(ctx, d["l0_w1"][ex][:, f0:f0 + 1024], wb2,
                                col0=slot * 8192,
                                pre=ctx.buf_pre(("l0w", slot)))
                hjobs = []
                for fs in range(8):
                    hjobs.append(dict(
                        pre=w1_tok + xn2_tok + b1_tok,
                        mms=mm_acc([(sl_w1[:, c * 1024 + fs * 128:
                                          c * 1024 + (fs + 1) * 128],
                                     xn2_fm.ch(c)) for c in range(DC)]),
                        psum=lambda b: b[:, :],
                        evac=(lambda fc, fs: lambda e, ps:
                              e.activation(h_fm.ch(fs), ps, AF.Relu,
                                           bias=b1_t[:, fc * 8 + fs:
                                                     fc * 8 + fs + 1]))(fc, fs),
                    ))
                sweep(ctx, jobs=hjobs, banks=R["pb"][0:4], evac_eng="scalar",
                      entry=ctx.tok("evs", "evv"))
                ctx.buf_mark(("l0w", slot))
                hc_tok = ctx.tok("evs")
                slot2 = (fc + 1) % 2
                sl_w2 = wb2[:, slot2 * 8192: slot2 * 8192 + 8192]
                w2_tok = load_w(ctx, d["l0_w2"][ex][f0:f0 + 1024, :], wb2,
                                col0=slot2 * 8192,
                                pre=ctx.buf_pre(("l0w", slot2)))
                ytmp = R["arena_b"].bitcast(F32)
                yjobs = []
                for mch in range(DC):
                    def mk_yev(mch=mch):
                        def f(e, ps):
                            tv = ytmp[:, (mch % 2) * 512:(mch % 2) * 512 + 512]
                            e.tensor_tensor(tv, ps, R["coeffB"], ALU.mult)
                            return e.tensor_tensor(y_fm.ch(mch), y_fm.ch(mch),
                                                   tv, ALU.add)
                        return f
                    yjobs.append(dict(
                        pre=w2_tok + hc_tok + coefb_tok,
                        mms=mm_acc([(sl_w2[:, k * 1024 + mch * 128:
                                          k * 1024 + (mch + 1) * 128],
                                     h_fm.ch(k)) for k in range(DC)]),
                        psum=lambda b: b[:, :],
                        evac=mk_yev(),
                    ))
                sweep(ctx, jobs=yjobs, banks=R["pb"][0:4], evac_eng="vector",
                      entry=ctx.tok("evs", "evv"))
                ctx.buf_mark(("l0w", slot2))

        y_tok = ctx.tok("evv")
        if dbg and "dbg_y" in dbg:
            @blk.gpsimd
            def _(e, t=t, y_tok=y_tok):
                ctx.wait(e, y_tok)
                for c in range(DC):
                    instr = e.dma_start(
                        dbg["dbg_y"][c * 128:(c + 1) * 128, t * 512:(t + 1) * 512],
                        y_fm.ch(c))
                    ctx.inc("out", instr, 16)
        @blk.scalar
        def _(e, t=t):
            ctx.wait(e, y_tok + ctx.tok("ldg2"))
            instr = None
            for mch in range(DC):
                instr = e.activation(FM(R["sbuf_g"], DC, 512).ch(mch),
                                     y_fm.ch(mch), AF.Silu)
            ctx.inc("evs", instr)
        @blk.gpsimd
        def _(e, t=t):
            ctx.wait(e, ctx.tok("evs"))
            for mch in range(DC):
                instr = e.dma_start(
                    g_spill[:, mch * T + t0: mch * T + t0 + 512],
                    FM(R["sbuf_g"], DC, 512).ch(mch))
                ctx.inc("ldg2", instr, 16)

    if dbg and "dbg_g" in dbg:
        @blk.gpsimd
        def _(e):
            ctx.wait(e, ctx.tok("ldg2"))
            instr = e.dma_start(dbg["dbg_g"].rearrange("(c p) t -> p c t", p=128),
                                g_spill[:].rearrange("p (c t) -> p c t", t=T))
            ctx.inc("out", instr, 16)

    o_all = FM(R["arena_a"], DC, T)
    attnT = R["arena_b"]
    ad = R["arena_d"]
    QO, KO, VO, AO = 0, 2048, 4096, 6144
    sc_row = R["arena_c"]
    wsl = R["sq"].bitcast(BF16)

    for hg in range(8):
        load_w(ctx, d["l0_wq"][:, hg * 128:(hg + 1) * 128], wsl, col0=0,
               pre=ctx.tok("mm", "statmm", "ln_done"))
        load_w(ctx, d["l0_wk"][:, hg * 128:(hg + 1) * 128], wsl, col0=1024)
        load_w(ctx, d["l0_wv"][:, hg * 128:(hg + 1) * 128], wsl, col0=2048)
        w_tok = ctx.tok("ldg")

        qk_jobs = []
        for dst_col0, wcol in [(QO, 0), (KO, 1024)]:
            for tb in range(4):
                qk_jobs.append(dict(
                    pre=w_tok,
                    mms=mm_acc([(wsl[:, wcol + c * 128: wcol + (c + 1) * 128],
                                 xn_fm.ch(c, tb * 512, 512)) for c in range(DC)]),
                    psum=lambda b: b[:, :],
                    evac=(lambda dst_col0, tb: lambda e, ps: e.tensor_copy(
                        ad[:, dst_col0 + tb * 512: dst_col0 + (tb + 1) * 512], ps))(
                            dst_col0, tb),
                ))
        for ts_ in range(16):
            qk_jobs.append(dict(
                pre=w_tok,
                mms=mm_acc([(xn_fm.ch(c, ts_ * 128, 128),
                             wsl[:, 2048 + c * 128: 2048 + (c + 1) * 128])
                            for c in range(DC)]),
                psum=lambda b: b[:, 0:128],
                evac=(lambda ts_: lambda e, ps: e.tensor_copy(
                    ad[:, VO + ts_ * 128: VO + (ts_ + 1) * 128], ps))(ts_),
            ))
        sweep(ctx, jobs=qk_jobs, banks=R["pb"][0:3], evac_eng="vector",
              entry=ctx.tok("evs", "evv"))
        qkv_tok = ctx.tok("evv")

        for h2 in range(2):
            base = 64 * h2
            for tq in range(4):
                for qs in range(4):
                    q0 = tq * 512 + qs * 128
                    L = q0 + 128
                    sjobs = []
                    nblk = (L + 511) // 512
                    for kb in range(nblk):
                        k0 = kb * 512
                        w = min(512, L - k0)
                        has_diag = (k0 + w == L)
                        def mk_evac(k0=k0, w=w, has_diag=has_diag, L=L):
                            def f(e, ps):
                                if has_diag:
                                    if w > 128:
                                        e.tensor_single_scalar(
                                            sc_row[:, k0:k0 + w - 128],
                                            ps[:, 0:w - 128], 0.125, ALU.mult)
                                    return e.scalar_tensor_tensor(
                                        sc_row[:, L - 128:L], ps[:, w - 128:w],
                                        0.125, R["causal_t"][:], ALU.mult, ALU.add)
                                return e.tensor_single_scalar(
                                    sc_row[:, k0:k0 + w], ps[:, 0:w], 0.125,
                                    ALU.mult)
                            return f
                        sjobs.append(dict(
                            pre=qkv_tok,
                            mms=[(lambda k0=k0, w=w, q0=q0, base=base:
                                  lambda e, ps: e.matmul(
                                      ps[:, 0:w],
                                      ad[base:base + 64, QO + q0:QO + q0 + 128],
                                      ad[base:base + 64, KO + k0:KO + k0 + w],
                                      start=True, stop=True))()],
                            psum=lambda b: b,
                            evac=mk_evac(),
                        ))
                    sweep(ctx, jobs=sjobs, banks=R["pb"][0:2], evac_eng="vector",
                          entry=ctx.tok("evs", "evv"))
                    sc_tok = ctx.tok("evv")

                    @blk.vector
                    def _(e, L=L):
                        ctx.wait(e, sc_tok)
                        instr = e.tensor_reduce(R["cols"][:, 0:1],
                                                sc_row[:, 0:L], AX.X, ALU.max,
                                                negate=True)
                        ctx.inc("evv", instr)
                    @blk.scalar
                    def _(e, L=L):
                        ctx.wait(e, ctx.tok("evv"))
                        instr = e.activation(ad[:, AO:AO + L], sc_row[:, 0:L],
                                             AF.Exp, bias=R["cols"][:, 0:1])
                        ctx.inc("evs", instr)
                    @blk.vector
                    def _(e, L=L):
                        ctx.wait(e, ctx.tok("evs"))
                        ctx.inc("evv", e.tensor_reduce(
                            R["cols"][:, 1:2], ad[:, AO:AO + L], AX.X, ALU.add))
                        ctx.wait(e, ctx.tok("evv"))
                        ctx.inc("evv", e.reciprocal(R["cols"][:, 2:3],
                                                    R["cols"][:, 1:2]))
                        ctx.wait(e, ctx.tok("evv"))
                        instr = e.tensor_scalar_mul(ad[:, AO:AO + L],
                                                    ad[:, AO:AO + L],
                                                    R["cols"][:, 2:3])
                        ctx.inc("evv", instr)
                    sm_tok = ctx.tok("evv")

                    tjobs = []
                    for tkc in range(L // 128):
                        tjobs.append(dict(
                            pre=sm_tok,
                            mms=[(lambda tkc: lambda e, ps: e.transpose(
                                ps, ad[:, AO + tkc * 128:AO + (tkc + 1) * 128],
                                R["iden_b"][:]))(tkc)],
                            psum=lambda b: b[:, 0:64].bitcast(BF16),
                            evac=(lambda tkc: lambda e, ps: e.tensor_copy(
                                attnT[:, tkc * 512 + qs * 128:
                                      tkc * 512 + qs * 128 + 128], ps))(tkc),
                        ))
                    sweep(ctx, jobs=tjobs, banks=R["pb"][2:4], evac_eng="vector",
                          entry=ctx.tok("evs", "evv"))

                at_tok = ctx.tok("evv")
                nk = 4 * (tq + 1)
                avmms = []
                for tkc in range(nk):
                    noff = max(0, (tkc - 4 * tq)) * 128
                    def mk(tkc=tkc, noff=noff, first=(tkc == 0), last=(tkc == nk - 1),
                           base=base):
                        def f(e, ps):
                            return e.matmul(
                                ps[0:64, noff:512],
                                ad[:, VO + tkc * 128 + base:
                                   VO + tkc * 128 + base + 64],
                                attnT[:, tkc * 512 + noff: (tkc + 1) * 512],
                                start=first, stop=last)
                        return f
                    avmms.append(mk())
                avjobs = [dict(
                    pre=at_tok,
                    mms=avmms,
                    psum=lambda b: b,
                    evac=(lambda hg=hg, h2=h2, tq=tq: lambda e, ps: e.tensor_copy(
                        o_all.ch(hg, tq * 512, 512, p0=64 * h2, pw=64),
                        ps[0:64, :]))(),
                )]
                sweep(ctx, jobs=avjobs, banks=[R["pb"][4]], evac_eng="vector",
                      entry=ctx.tok("evs", "evv"))

    wo_t = R["arena_d"]
    wo_tok = load_w(ctx, d["l0_wo"], wo_t, pre=ctx.tok("mm"))
    s_fm = FM(R["sbuf_s"], DC, 512)
    g_fm = FM(R["sbuf_g"], DC, 512)
    for t in range(TC):
        t0 = t * 512
        @blk.gpsimd
        def _(e, t=t):
            ctx.wait(e, ctx.tok("comb"))
            for c in range(DC):
                instr = e.dma_start(g_fm.ch(c),
                                    g_spill[:, c * T + t0: c * T + t0 + 512])
                ctx.inc("ldg2", instr, 16)
        g_tok = ctx.tok("ldg2")

        sjobs = []
        for mch in range(DC):
            sjobs.append(dict(
                pre=wo_tok + ctx.tok("evv"),
                mms=mm_acc([(wo_t[:, c * 1024 + mch * 128: c * 1024 + (mch + 1) * 128],
                             o_all.ch(c, t0, 512)) for c in range(DC)]),
                psum=lambda b: b[:, :],
                evac=(lambda mch, t0=t0: lambda e, ps: e.tensor_tensor(
                    s_fm.ch(mch), ps, xn_fm.ch(mch, t0, 512), ALU.add))(mch),
            ))
        sweep(ctx, jobs=sjobs, banks=R["pb"][5:7], evac_eng="vector",
              entry=ctx.tok("evs", "evv"))
        if dbg and "dbg_s" in dbg:
            @blk.gpsimd
            def _(e, t=t):
                ctx.wait(e, ctx.tok("evv"))
                for c in range(DC):
                    instr = e.dma_start(
                        dbg["dbg_s"][c * 128:(c + 1) * 128, t0:t0 + 512],
                        s_fm.ch(c))
                    ctx.inc("out", instr, 16)
        @blk.vector
        def _(e, t=t):
            ctx.wait(e, g_tok)
            ctx.inc("evv", e.nop())
        emit_combine(ctx, R, t)




def l1_prep(I, m):
    m["l1_ln1"] = _col_layout(I["L1_attn_ln1_w"], DC)
    m["l1_ln2"] = _col_layout(I["L1_attn_ln2_w"], DC)
    m["l1_winT"] = np.ascontiguousarray(np.asarray(I["L1_attn_w_in"]).T)
    m["l1_wgateT"] = np.ascontiguousarray(np.asarray(I["L1_attn_w_gate"]).T)
    m["l1_aparam"] = _col_layout(I["L1_attn_a_param"], 16)
    m["l1_waT"] = np.ascontiguousarray(np.asarray(I["L1_attn_w_a"]).T)
    m["l1_woutT"] = np.ascontiguousarray(np.asarray(I["L1_attn_w_out"]).T)
    m["l1_wmgT"] = np.ascontiguousarray(np.asarray(I["L1_attn_w_mlp_gate"]).T)
    m["l1_wmuT"] = np.ascontiguousarray(np.asarray(I["L1_attn_w_mlp_up"]).T)
    m["l1_wmdT"] = np.ascontiguousarray(np.asarray(I["L1_attn_w_mlp_down"]).T)
    m["l1_fwgT"] = np.ascontiguousarray(np.asarray(I["L1_ffn_w_gate"]).T)
    m["l1_fwuT"] = np.ascontiguousarray(np.asarray(I["L1_ffn_w_up"]).T)
    m["l1_fwdT"] = np.ascontiguousarray(np.asarray(I["L1_ffn_w_down"]).T)


def l1_declare(nc, d):
    def din(name, shape, dt=F32):
        d[name] = nc.dram_tensor(name, shape, dt, kind="ExternalInput").ap()
    din("l1_ln1", [128, DC]); din("l1_ln2", [128, DC])
    din("l1_winT", [1024, 2048]); din("l1_wgateT", [1024, 2048])
    din("l1_aparam", [128, 16]); din("l1_waT", [2048, 2048])
    din("l1_woutT", [2048, 1024])
    din("l1_wmgT", [1024, 4096]); din("l1_wmuT", [1024, 4096])
    din("l1_wmdT", [4096, 1024])
    din("l1_fwgT", [1024, 3072], I32); din("l1_fwuT", [1024, 3072], I32)
    din("l1_fwdT", [3072, 1024], I32)


def emit_l1(ctx, R, d, g_spill, s_spill, x_spill, dbg=None):
    blk = ctx.blk
    nc = ctx.nc
    x_fm = FM(R["x"], DC, T)
    xn_fm = FM(R["xn"], DC, T)

    smf = R["smf"]
    ln1 = smf[0:128, 0:8]
    ln2 = smf[0:128, 8:16]
    apar = smf[0:128, 16:32]
    state = smf[0:128, 32:48]

    @blk.sync
    def _(e):
        ctx.wait(e, ctx.tok("comb", "evv", "evs"))
        for src, dst in [("l1_ln1", ln1), ("l1_ln2", ln2), ("l1_aparam", apar)]:
            ctx.inc("ld", e.dma_start(dst, d[src][:]), 16)
    small_tok = ctx.tok("ld")

    emit_ln(ctx, R, x_fm, 0, T, xn_fm, 0, rms=False, in_f32=True)
    if dbg and "dbg_xn" in dbg:
        @blk.gpsimd
        def _(e):
            ctx.wait(e, ctx.tok("ln_done"))
            ctx.inc("out", e.dma_start(
                dbg["dbg_xn"].rearrange("(c p) t -> p c t", p=128),
                R["xn"][:].rearrange("p (c t) -> p c t", t=T)), 16)

    wbuf = R["arena_a"]
    xin_fm = FM(R["arena_d"], 16, 256)
    a_fm = FM(R["arena_d"], 16, 256, col0=4096)
    gt_fm = FM(R["arena_b"], 16, 256)
    y_fm = FM(R["arena_b"], 16, 256, col0=4096)
    apart_fm = FM(R["arena_c"], 16, 256)
    h1_fm = FM(R["xn2"], DC, 512)
    sq = R["sq"]

    @blk.gpsimd
    def _(e):
        ctx.wait(e, ctx.tok("comb", "ldg2"))
        ctx.inc("evp", e.memset(state, 0.0))

    for tm in range(8):
        t0 = tm * 256
        if tm % 2 == 0:
            emit_ln(ctx, R, xn_fm, t0, 512, h1_fm, 0, rms=True, eps=1e-6,
                    in_f32=False, gain_col=lambda c: ln1[:, c:c + 1])
        h1off = (tm % 2) * 256
        h1_tok = ctx.tok("ln_done")

        for wname, dst_fm, act in [("l1_winT", xin_fm, None),
                                   ("l1_wgateT", gt_fm, AF.Sigmoid)]:
            w_tok = load_w(ctx, d[wname], wbuf, pre=ctx.tok("mm"))
            jobs = []
            for mch in range(16):
                def mk_ev(dst_fm=dst_fm, act=act, mch=mch):
                    if act is None:
                        return lambda e, ps: e.tensor_copy(dst_fm.ch(mch), ps[:, 0:256])
                    return lambda e, ps: e.activation(dst_fm.ch(mch), ps[:, 0:256], act)
                jobs.append(dict(
                    pre=w_tok + h1_tok,
                    mms=mm_acc([(wbuf[:, c * 2048 + mch * 128:
                                      c * 2048 + (mch + 1) * 128],
                                 h1_fm.ch(c, h1off, 256)) for c in range(DC)]),
                    psum=lambda b: b[:, 0:256],
                    evac=mk_ev(),
                ))
            ev = "scalar" if act else "vector"
            sweep(ctx, jobs=jobs, banks=R["pb"][0:4], evac_eng=ev,
                  entry=ctx.tok("evs", "evv"))
        xin_tok = ctx.tok("evv", "evs")

        for kh in range(2):
            w_tok = load_w(ctx, d["l1_waT"][kh * 1024:(kh + 1) * 1024, :], wbuf,
                           pre=ctx.tok("mm"))
            jobs = []
            for mch in range(16):
                def mk_ev(kh=kh, mch=mch):
                    if kh == 0:
                        return lambda e, ps: e.tensor_copy(apart_fm.ch(mch), ps[:, 0:256])
                    return lambda e, ps: e.tensor_tensor(
                        apart_fm.ch(mch), apart_fm.ch(mch), ps[:, 0:256], ALU.add)
                jobs.append(dict(
                    pre=w_tok + xin_tok,
                    mms=mm_acc([(wbuf[:, c * 2048 + mch * 128:
                                      c * 2048 + (mch + 1) * 128],
                                 xin_fm.ch(c + kh * 8)) for c in range(DC)]),
                    psum=lambda b: b[:, 0:256],
                    evac=mk_ev(),
                ))
            sweep(ctx, jobs=jobs, banks=R["pb"][0:4], evac_eng="vector",
                  entry=ctx.tok("evs", "evv"))
        ap_tok = ctx.tok("evv")

        @blk.scalar
        def _(e, ap_tok=ap_tok):
            ctx.wait(e, ap_tok + small_tok)
            instr = None
            for c in range(16):
                instr = e.activation(a_fm.ch(c), apart_fm.ch(c), AF.Sigmoid,
                                     bias=apar[:, c:c + 1])
            ctx.inc("evs", instr)
        a_tok = ctx.tok("evs")

        for c in range(16):
            s1 = sq[:, (c % 4) * 512:(c % 4) * 512 + 256]
            s2 = sq[:, (c % 4) * 512 + 256:(c % 4) * 512 + 512]
            @blk.scalar
            def _(e, c=c, s1=s1, s2=s2, a_tok=a_tok):
                ctx.wait(e, a_tok + ctx.tok("scan"))
                e.activation(s1, a_fm.ch(c), AF.Square)
                instr = e.activation(s2, s1, AF.Sqrt, scale=-1.0,
                                     bias=cb(R, 1.0 + 1e-8))
                ctx.inc("evs", instr)
            @blk.vector
            def _(e, c=c, s2=s2):
                ctx.wait(e, ctx.tok("evs"))
                ctx.inc("evv", e.tensor_tensor(xin_fm.ch(c), s2, xin_fm.ch(c),
                                               ALU.mult))
                ctx.wait(e, ctx.tok("evv", "scan"))
                ctx.inc("scan", e.tensor_tensor_scan(
                    y_fm.ch(c), a_fm.ch(c), xin_fm.ch(c),
                    state[:, c:c + 1], ALU.mult, ALU.add))
                ctx.wait(e, ctx.tok("scan"))
                ctx.inc("scan", e.tensor_copy(state[:, c:c + 1],
                                              y_fm.ch(c, 255, 1)))
                ctx.inc("evv", e.tensor_tensor(y_fm.ch(c), y_fm.ch(c),
                                               gt_fm.ch(c), ALU.mult))
        scan_tok = ctx.tok("evv", "scan")

        w_tok = load_w(ctx, d["l1_woutT"], wbuf, pre=ctx.tok("mm"))
        xp_fm = FM(R["sbuf_s"], DC, 256)
        jobs = []
        for mch in range(DC):
            jobs.append(dict(
                pre=w_tok + scan_tok,
                mms=mm_acc([(wbuf[:, k * 1024 + mch * 128:
                                  k * 1024 + (mch + 1) * 128],
                             y_fm.ch(k)) for k in range(16)]),
                psum=lambda b: b[:, 0:256],
                evac=(lambda mch, t0=t0: lambda e, ps: e.tensor_tensor(
                    xp_fm.ch(mch), ps[:, 0:256], xn_fm.ch(mch, t0, 256),
                    ALU.add))(mch),
            ))
        sweep(ctx, jobs=jobs, banks=R["pb"][4:6], evac_eng="vector",
              entry=ctx.tok("evs", "evv"))
        @blk.gpsimd
        def _(e, t0=t0):
            ctx.wait(e, ctx.tok("evv"))
            for c in range(DC):
                ctx.inc("ldg2", e.dma_start(
                    x_spill[:, c * T + t0: c * T + t0 + 256], xp_fm.ch(c)), 16)

    xp_fm = FM(R["arena_d"], DC, 512)
    h2_fm = FM(R["xn2"], DC, 512)
    mg_fm = FM(R["arena_b"], 16, 512)
    acc_fm = FM(R["arena_c"], DC, 512)
    for t in range(TC):
        t0 = t * 512
        @blk.gpsimd
        def _(e, t0=t0):
            ctx.wait(e, ctx.tok("mm", "evv", "ldg2"))
            for c in range(DC):
                ctx.inc("ldg2", e.dma_start(
                    xp_fm.ch(c), x_spill[:, c * T + t0: c * T + t0 + 512]), 16)
        xp_tok = ctx.tok("ldg2")
        @blk.scalar
        def _(e, xp_tok=xp_tok):
            ctx.wait(e, xp_tok)
            ctx.inc("ln_done", e.nop())
        emit_ln(ctx, R, xp_fm, 0, 512, h2_fm, 0, rms=True, eps=1e-6,
                in_f32=False, gain_col=lambda c: ln2[:, c:c + 1])
        h2_tok = ctx.tok("ln_done")

        for fh in range(2):
            w_tok = load_w(ctx, d["l1_wmgT"][:, fh * 2048:(fh + 1) * 2048],
                           wbuf, pre=ctx.tok("mm"))
            jobs = []
            for mch in range(16):
                jobs.append(dict(
                    pre=w_tok + h2_tok,
                    mms=mm_acc([(wbuf[:, c * 2048 + mch * 128:
                                      c * 2048 + (mch + 1) * 128],
                                 h2_fm.ch(c)) for c in range(DC)]),
                    psum=lambda b: b[:, :],
                    evac=(lambda mch: lambda e, ps: e.activation(
                        mg_fm.ch(mch), ps, AF.Silu))(mch),
                ))
            sweep(ctx, jobs=jobs, banks=R["pb"][0:4], evac_eng="scalar",
                  entry=ctx.tok("evs", "evv"))
            w_tok = load_w(ctx, d["l1_wmuT"][:, fh * 2048:(fh + 1) * 2048],
                           wbuf, pre=ctx.tok("mm"))
            sg_tok = ctx.tok("evs")
            jobs = []
            for mch in range(16):
                jobs.append(dict(
                    pre=w_tok + sg_tok,
                    mms=mm_acc([(wbuf[:, c * 2048 + mch * 128:
                                      c * 2048 + (mch + 1) * 128],
                                 h2_fm.ch(c)) for c in range(DC)]),
                    psum=lambda b: b[:, :],
                    evac=(lambda mch: lambda e, ps: e.tensor_tensor(
                        mg_fm.ch(mch), ps, mg_fm.ch(mch), ALU.mult))(mch),
                ))
            sweep(ctx, jobs=jobs, banks=R["pb"][0:4], evac_eng="vector",
                  entry=ctx.tok("evs", "evv"))
            mm_tok = ctx.tok("evv")
            w_tok = load_w(ctx, d["l1_wmdT"][fh * 2048:(fh + 1) * 2048, :],
                           wbuf, pre=ctx.tok("mm"))
            jobs = []
            for mch in range(DC):
                def mk_ev(fh=fh, mch=mch):
                    if fh == 0:
                        return lambda e, ps: e.tensor_copy(acc_fm.ch(mch), ps)
                    return lambda e, ps: e.tensor_tensor(
                        acc_fm.ch(mch), acc_fm.ch(mch), ps, ALU.add)
                jobs.append(dict(
                    pre=w_tok + mm_tok,
                    mms=mm_acc([(wbuf[:, k * 1024 + mch * 128:
                                      k * 1024 + (mch + 1) * 128],
                                 mg_fm.ch(k)) for k in range(16)]),
                    psum=lambda b: b[:, :],
                    evac=mk_ev(),
                ))
            sweep(ctx, jobs=jobs, banks=R["pb"][0:4], evac_eng="vector",
                  entry=ctx.tok("evs", "evv"))
        s_tok = ctx.tok("evv")
        @blk.vector
        def _(e, t0=t0, s_tok=s_tok):
            ctx.wait(e, s_tok)
            instr = None
            for c in range(DC):
                instr = e.tensor_tensor(h2_fm.ch(c), acc_fm.ch(c), xp_fm.ch(c),
                                        ALU.add)
            ctx.inc("evv", instr)
        @blk.gpsimd
        def _(e, t0=t0):
            ctx.wait(e, ctx.tok("evv"))
            for c in range(DC):
                ctx.inc("ldg2", e.dma_start(
                    s_spill[:, c * T + t0: c * T + t0 + 512], h2_fm.ch(c)), 16)

    if dbg and "dbg_s" in dbg:
        @blk.gpsimd
        def _(e):
            ctx.wait(e, ctx.tok("ldg2"))
            ctx.inc("out", e.dma_start(
                dbg["dbg_s"].rearrange("(c p) t -> p c t", p=128),
                s_spill[:].rearrange("p (c t) -> p c t", t=T)), 16)

    sg_fm = FM(R["arena_d"], 12, 512)
    hh_fm = FM(R["arena_b"], 12, 512)
    for t in range(TC):
        t0 = t * 512
        for fh in range(2):
            f0 = fh * 1536
            w_tok = load_w(ctx, d["l1_fwgT"][:, f0:f0 + 1536], wbuf,
                           pre=ctx.tok("mm"))
            jobs = []
            for mch in range(12):
                jobs.append(dict(
                    pre=w_tok,
                    mms=mm_acc([(wbuf[:, c * 1536 + mch * 128:
                                      c * 1536 + (mch + 1) * 128],
                                 xn_fm.ch(c, t0, 512)) for c in range(DC)]),
                    psum=lambda b: b[:, :],
                    evac=(lambda mch: lambda e, ps: e.activation(
                        sg_fm.ch(mch), ps, AF.Silu, scale=1.0 / 127.0))(mch),
                ))
            sweep(ctx, jobs=jobs, banks=R["pb"][0:4], evac_eng="scalar",
                  entry=ctx.tok("evs", "evv"))
            w_tok = load_w(ctx, d["l1_fwuT"][:, f0:f0 + 1536], wbuf,
                           pre=ctx.tok("mm"))
            sg_tok = ctx.tok("evs")
            jobs = []
            for mch in range(12):
                jobs.append(dict(
                    pre=w_tok + sg_tok,
                    mms=mm_acc([(wbuf[:, c * 1536 + mch * 128:
                                      c * 1536 + (mch + 1) * 128],
                                 xn_fm.ch(c, t0, 512)) for c in range(DC)]),
                    psum=lambda b: b[:, :],
                    evac=(lambda mch: lambda e, ps: e.tensor_tensor(
                        hh_fm.ch(mch), ps, sg_fm.ch(mch), ALU.mult))(mch),
                ))
            sweep(ctx, jobs=jobs, banks=R["pb"][0:4], evac_eng="vector",
                  entry=ctx.tok("evs", "evv"))
            hh_tok = ctx.tok("evv")
            w_tok = load_w(ctx, d["l1_fwdT"][f0:f0 + 1536, :], wbuf,
                           pre=ctx.tok("mm"))
            jobs = []
            for mch in range(DC):
                def mk_ev(fh=fh, mch=mch):
                    if fh == 0:
                        return lambda e, ps: e.tensor_copy(acc_fm.ch(mch), ps)
                    return lambda e, ps: e.tensor_tensor(
                        acc_fm.ch(mch), acc_fm.ch(mch), ps, ALU.add)
                jobs.append(dict(
                    pre=w_tok + hh_tok,
                    mms=mm_acc([(wbuf[:, k * 1024 + mch * 128:
                                      k * 1024 + (mch + 1) * 128],
                                 hh_fm.ch(k)) for k in range(12)]),
                    psum=lambda b: b[:, :],
                    evac=mk_ev(),
                ))
            sweep(ctx, jobs=jobs, banks=R["pb"][0:4], evac_eng="vector",
                  entry=ctx.tok("evs", "evv"))
        y_tok = ctx.tok("evv")
        gg_fm = FM(R["xn2"], DC, 512)
        @blk.scalar
        def _(e, y_tok=y_tok):
            ctx.wait(e, y_tok + ctx.tok("ldg2"))
            instr = None
            for c in range(DC):
                instr = e.activation(gg_fm.ch(c), acc_fm.ch(c), AF.Silu,
                                     scale=1.0 / (127.0 * 127.0))
            ctx.inc("evs", instr)
        @blk.gpsimd
        def _(e, t0=t0):
            ctx.wait(e, ctx.tok("evs"))
            for c in range(DC):
                ctx.inc("ldg2", e.dma_start(
                    g_spill[:, c * T + t0: c * T + t0 + 512], gg_fm.ch(c)), 16)

    if dbg and "dbg_g" in dbg:
        @blk.gpsimd
        def _(e):
            ctx.wait(e, ctx.tok("ldg2"))
            ctx.inc("out", e.dma_start(
                dbg["dbg_g"].rearrange("(c p) t -> p c t", p=128),
                g_spill[:].rearrange("p (c t) -> p c t", t=T)), 16)

    emit_combine_spill(ctx, R, g_spill, s_spill)




def l2_prep(I, m, rank):
    import ml_dtypes
    m["l2_wkv"] = np.ascontiguousarray(I["L2_attn_W_kv"])
    m["l2_gkv"] = _col_layout(I["L2_attn_g_kv"], 2)
    m["l2_wkdec"] = np.ascontiguousarray(I["L2_attn_W_kdec"])
    m["l2_wvdec"] = np.ascontiguousarray(I["L2_attn_W_vdec"])
    m["l2_wq"] = np.ascontiguousarray(I["L2_attn_W_q"])
    m["l2_wqdec"] = np.ascontiguousarray(I["L2_attn_W_qdec"])
    m["l2_wkrope"] = np.ascontiguousarray(I["L2_attn_W_krope"])
    m["l2_wqrope"] = np.ascontiguousarray(I["L2_attn_W_qrope"])
    m["l2_wo"] = np.ascontiguousarray(I["L2_attn_W_o"])
    for nm, key in [("wf", "L2_ffn_wf"), ("uf", "L2_ffn_uf"),
                    ("wi", "L2_ffn_wi"), ("ui", "L2_ffn_ui"),
                    ("wc", "L2_ffn_wc"), ("uc", "L2_ffn_uc"),
                    ("wo2", "L2_ffn_wo"), ("uo", "L2_ffn_uo")]:
        m["l2_" + nm] = np.ascontiguousarray(np.asarray(I[key]).T)
    m["l2_bf"] = _col_layout(I["L2_ffn_bf"], DC)
    m["l2_bi"] = _col_layout(I["L2_ffn_bi"], DC)
    m["l2_bc"] = _col_layout(I["L2_ffn_bc"], DC)
    m["l2_bo"] = _col_layout(I["L2_ffn_bo"], DC)
    m["l2_inc"] = _col_layout(I["L2_ffn_in_c"], DC)
    inv_freq = 1.0 / 10000.0 ** (np.arange(0, 32, 2, dtype=np.float32) / 32)
    t = np.arange(T, dtype=np.float32)
    freqs = np.outer(t, inv_freq)
    emb = np.concatenate([freqs, freqs], -1)
    cos = np.cos(emb).T
    sin = np.sin(emb).T
    m["l2_cos"] = np.ascontiguousarray(np.tile(cos, (4, 1))).astype(ml_dtypes.bfloat16)
    m["l2_sin"] = np.ascontiguousarray(np.tile(sin, (4, 1))).astype(ml_dtypes.bfloat16)
    sign = np.where(np.arange(128) % 32 < 16, -1.0, 1.0).astype(np.float32)
    m["l2_sign"] = sign.reshape(128, 1)
    p = np.arange(128, dtype=np.int32)[:, None]
    g1 = np.zeros((128, 64), np.int32)
    g2 = np.zeros((128, 64), np.int32)
    for t8 in range(8):
        for ch in range(8):
            g1[:, t8 * 8 + ch] = ((t8 * 128 + p[:, 0]) * 64 + ch * 8 + rank)
            g2[:, t8 * 8 + ch] = ((t8 * 128 + p[:, 0]) * 64 + rank * 8 + ch)
    m["l2_gidx"] = g1
    m["l2_gidx2"] = g2


def l2_declare(nc, d):
    def din(name, shape, dt=F32):
        d[name] = nc.dram_tensor(name, shape, dt, kind="ExternalInput").ap()
    din("l2_wkv", [1024, 256]); din("l2_gkv", [128, 2])
    din("l2_wkdec", [256, 1024]); din("l2_wvdec", [256, 1024])
    din("l2_wq", [1024, 512]); din("l2_wqdec", [512, 1024])
    din("l2_wkrope", [1024, 512]); din("l2_wqrope", [512, 512])
    din("l2_wo", [1024, 1024])
    for nm in ["wf", "uf", "wi", "ui", "wc", "uc", "wo2", "uo"]:
        din("l2_" + nm, [1024, 1024])
    for nm in ["bf", "bi", "bc", "bo", "inc"]:
        din("l2_" + nm, [128, DC])
    din("l2_cos", [128, T], BF16); din("l2_sin", [128, T], BF16)
    din("l2_sign", [128, 1])
    din("l2_gidx", [128, 64], I32); din("l2_gidx2", [128, 64], I32)


def emit_l2(ctx, R, d, g_spill, s_spill, dbg=None):
    from concourse.bass import IndirectOffsetOnAxis
    blk = ctx.blk
    nc = ctx.nc
    x_fm = FM(R["x"], DC, T)
    xn_fm = FM(R["xn"], DC, T)

    smf, smb = R["smf"], R["smb"]
    gkv = smf[0:128, 0:2]
    sign_t = smf[0:128, 2:3]
    bias_t = smf[0:128, 3:35]
    inc_t = smf[0:128, 35:43]
    cos_w = smb[0:64, 0:512]
    sin_w = smb[0:64, 512:1024]
    gidx = smb.bitcast(I32)[0:128, 512:576]

    cc_in = nc.dram_tensor("l2_ccin", [128, DC * T], BF16, kind="Internal").ap()
    cc_out = nc.dram_tensor("l2_ccout", [128 * 8, DC * T], BF16, kind="Internal").ap()
    cc2_in = nc.dram_tensor("l2_cc2in", [128, DC * T], BF16, kind="Internal").ap()
    cc2_out = nc.dram_tensor("l2_cc2out", [128 * 8, DC * T], BF16, kind="Internal").ap()

    @blk.sync
    def _(e):
        ctx.wait(e, ctx.tok("comb", "evv", "evs"))
        for src, dst in [("l2_gkv", gkv), ("l2_sign", sign_t),
                         ("l2_gidx", gidx), ("l2_inc", inc_t)]:
            ctx.inc("ld", e.dma_start(dst, d[src][:]), 16)
        for i, nm in enumerate(["l2_bf", "l2_bi", "l2_bc", "l2_bo"]):
            ctx.inc("ld", e.dma_start(bias_t[0:128, i * DC:(i + 1) * DC], d[nm][:]), 16)
    small_tok = ctx.tok("ld")

    emit_ln(ctx, R, x_fm, 0, T, xn_fm, 0, rms=False, in_f32=True)
    ln_tok = ctx.tok("ln_done")
    if dbg and "dbg_xn" in dbg:
        @blk.gpsimd
        def _(e):
            ctx.wait(e, ctx.tok("ln_done"))
            ctx.inc("out", e.dma_start(
                dbg["dbg_xn"].rearrange("(c p) t -> p c t", p=128),
                R["xn"][:].rearrange("p (c t) -> p c t", t=T)), 16)

    @blk.gpsimd
    def _(e):
        ctx.wait(e, ctx.tok("ln_done"))
        ctx.inc("ldg2", e.dma_start(cc_in[:], R["xn"][:]), 16)
        ctx.wait(e, ctx.tok("ldg2"))
        ctx.inc("cc", e.collective_compute(
            "AllGather", ALU.bypass, replica_groups=[list(range(NCORES))],
            ins=[cc_in.opt()], outs=[cc_out.opt()]))
    ag1_tok = ctx.tok("cc")

    kvf = FM(R["arena_c"], 2, T)
    kvn = FM(R["xn2"], 2, T)
    wsm = R["sq"].bitcast(BF16)
    w_tok = load_w(ctx, d["l2_wkv"], wsm, col0=0, pre=ctx.tok("mm", "statmm"))
    jobs = []
    for mch in range(2):
        for tb in range(4):
            jobs.append(dict(
                pre=w_tok + ln_tok,
                mms=mm_acc([(wsm[:, c * 256 + mch * 128: c * 256 + (mch + 1) * 128],
                             xn_fm.ch(c, tb * 512, 512)) for c in range(DC)]),
                psum=lambda b: b[:, :],
                evac=(lambda mch, tb: lambda e, ps: e.tensor_copy(
                    kvf.ch(mch, tb * 512, 512), ps))(mch, tb),
            ))
    sweep(ctx, jobs=jobs, banks=R["pb"][0:2], evac_eng="vector",
          entry=ctx.tok("evs", "evv"))
    @blk.scalar
    def _(e):
        ctx.wait(e, ctx.tok("evv"))
        ctx.inc("ln_done", e.nop())
    emit_ln(ctx, R, kvf, 0, T, kvn, 0, rms=True, eps=1e-6, in_f32=True, nch=2,
            gain_col=lambda c: gkv[:, c:c + 1])
    kvn_tok = ctx.tok("ln_done")

    qc = FM(R["arena_b"], 4, T)
    wqb = R["arena_d"]
    w_tok = load_w(ctx, d["l2_wq"], wqb, col0=0, pre=ctx.tok("mm"))
    jobs = []
    for mch in range(4):
        for tb in range(4):
            jobs.append(dict(
                pre=w_tok + ln_tok,
                mms=mm_acc([(wqb[:, c * 512 + mch * 128:
                                 c * 512 + (mch + 1) * 128],
                             xn_fm.ch(c, tb * 512, 512)) for c in range(DC)]),
                psum=lambda b: b[:, :],
                evac=(lambda mch, tb: lambda e, ps: e.tensor_copy(
                    qc.ch(mch, tb * 512, 512), ps))(mch, tb),
            ))
    sweep(ctx, jobs=jobs, banks=R["pb"][0:2], evac_eng="vector",
          entry=ctx.tok("evs", "evv"))
    qc_tok = ctx.tok("evv")

    ad = R["arena_d"]
    KO, QO, VO, AO = 0, 2048, 4096, 6144
    o_all = FM(R["arena_a"], DC, T)
    attnT = R["sq"].bitcast(BF16)
    rr_t = R["sq"].bitcast(BF16)[0:64, 0:512]
    sc_row = R["arena_c"]
    acbf = R["arena_c"].bitcast(BF16)
    kr_t = acbf[0:64, 4096:6144]
    qr_t = acbf[0:64, 6144:8192]

    for hg in range(8):
        wq_tok = ctx.tok("ldg")
        load_w(ctx, d["l2_wkdec"][:, hg * 128:(hg + 1) * 128], wsm, col0=0,
               pre=ctx.tok("mm", "statmm", "out"))
        load_w(ctx, d["l2_wvdec"][:, hg * 128:(hg + 1) * 128], wsm, col0=256)
        load_w(ctx, d["l2_wqdec"][:, hg * 128:(hg + 1) * 128], wsm, col0=512)
        load_w(ctx, d["l2_wkrope"][:, hg * 64:(hg + 1) * 64], wsm, col0=1024)
        load_w(ctx, d["l2_wqrope"][:, hg * 64:(hg + 1) * 64], wsm, col0=1536)
        w_tok = ctx.tok("ldg")

        jobs = []
        for tb in range(4):
            jobs.append(dict(
                pre=w_tok + kvn_tok,
                mms=mm_acc([(wsm[:, 0 + c * 128:0 + (c + 1) * 128],
                             kvn.ch(c, tb * 512, 512)) for c in range(2)]),
                psum=lambda b: b[:, :],
                evac=(lambda tb: lambda e, ps: e.tensor_copy(
                    ad[:, KO + tb * 512: KO + (tb + 1) * 512], ps))(tb),
            ))
        for tb in range(4):
            jobs.append(dict(
                pre=w_tok + qc_tok,
                mms=mm_acc([(wsm[:, 512 + c * 128: 512 + (c + 1) * 128],
                             qc.ch(c, tb * 512, 512)) for c in range(4)]),
                psum=lambda b: b[:, :],
                evac=(lambda tb: lambda e, ps: e.tensor_copy(
                    ad[:, QO + tb * 512: QO + (tb + 1) * 512], ps))(tb),
            ))
        for ts_ in range(16):
            jobs.append(dict(
                pre=w_tok + kvn_tok,
                mms=mm_acc([(kvn.ch(c, ts_ * 128, 128),
                             wsm[:, 256 + c * 128: 256 + (c + 1) * 128])
                            for c in range(2)]),
                psum=lambda b: b[:, 0:128],
                evac=(lambda ts_: lambda e, ps: e.tensor_copy(
                    ad[:, VO + ts_ * 128: VO + (ts_ + 1) * 128], ps))(ts_),
            ))
        for tb in range(4):
            jobs.append(dict(
                pre=w_tok + ln_tok,
                mms=mm_acc([(wsm[:, 1024 + c * 64: 1024 + (c + 1) * 64],
                             xn_fm.ch(c, tb * 512, 512)) for c in range(DC)]),
                psum=lambda b: b[0:64, :],
                evac=(lambda tb: lambda e, ps: e.tensor_copy(
                    acbf[0:64, 4096 + tb * 512: 4096 + (tb + 1) * 512], ps))(tb),
            ))
        for tb in range(4):
            jobs.append(dict(
                pre=w_tok + qc_tok,
                mms=mm_acc([(wsm[:, 1536 + c * 64: 1536 + (c + 1) * 64],
                             qc.ch(c, tb * 512, 512)) for c in range(4)]),
                psum=lambda b: b[0:64, :],
                evac=(lambda tb: lambda e, ps: e.tensor_copy(
                    acbf[0:64, 6144 + tb * 512: 6144 + (tb + 1) * 512], ps))(tb),
            ))
        sweep(ctx, jobs=jobs, banks=R["pb"][0:3], evac_eng="vector",
              entry=ctx.tok("evs", "evv"))
        prj_tok = ctx.tok("evv")

        if dbg and "dbg_att" in dbg and hg == 0:
            @blk.gpsimd
            def _(e, prj_tok=prj_tok):
                ctx.wait(e, prj_tok)
                ctx.inc("out", e.dma_start(
                    dbg["dbg_att"][0:64, 7168:8192],
                    R["arena_c"].bitcast(BF16)[0:64, 6144:7168]), 16)
            @blk.vector
            def _(e):
                ctx.wait(e, ctx.tok("out"))
                ctx.inc("evv", e.nop())
        mask = [(i + 16) % 32 for i in range(32)]
        for tb in range(4):
            w0 = tb * 512
            @blk.sync
            def _(e, w0=w0):
                ctx.wait(e, ctx.tok("evv"))
                ctx.inc("ld", e.dma_start(cos_w, d["l2_cos"][0:64, w0:w0 + 512]), 16)
                ctx.inc("ld", e.dma_start(sin_w, d["l2_sin"][0:64, w0:w0 + 512]), 16)
            cs_tok = ctx.tok("ld")
            @blk.vector
            def _(e, w0=w0, cs_tok=cs_tok, prj_tok=prj_tok):
                ctx.wait(e, prj_tok + small_tok + cs_tok)
                for ro in (4096, 6144):
                    rt = acbf[0:64, ro + w0: ro + w0 + 512]
                    ctx.inc("evv", e.stream_shuffle(rr_t, rt, mask))
                    ctx.wait(e, ctx.tok("evv"))
                    ctx.inc("evv", e.tensor_tensor(rt, rt, cos_w, ALU.mult))
                    ctx.inc("evv", e.scalar_tensor_tensor(
                        rr_t, rr_t, sign_t[0:64, 0:1], sin_w[:],
                        ALU.mult, ALU.mult))
                    ctx.wait(e, ctx.tok("evv"))
                    ctx.inc("evv", e.tensor_tensor(rt, rt, rr_t, ALU.add))
                    ctx.wait(e, ctx.tok("evv"))
        rope_tok = ctx.tok("evv")
        if dbg and "dbg_att" in dbg and hg == 0:
            pass
            @blk.gpsimd
            def _(e):
                ctx.wait(e, rope_tok + prj_tok)
                ctx.inc("out", e.dma_start(dbg["dbg_att"][0:128, 0:2048], ad[:, 0:2048]), 16)
                ctx.inc("out", e.dma_start(dbg["dbg_att"][0:128, 2048:4096], ad[:, 2048:4096]), 16)
                ctx.inc("out", e.dma_start(dbg["dbg_att"][0:64, 4096:6144],
                                           R["arena_c"].bitcast(BF16)[0:64, 4096:6144]), 16)
                ctx.inc("out", e.dma_start(dbg["dbg_att"][64:128, 4096:6144],
                                           R["arena_c"].bitcast(BF16)[0:64, 6144:8192]), 16)
                ctx.inc("out", e.dma_start(dbg["dbg_att"][0:128, 6144:7168], ad[:, 4096:5120]), 16)


        for h2 in range(2):
            base = 64 * h2
            rbase = 32 * h2
            for tq in range(8):
                for qs in range(2):
                    q0 = tq * 256 + qs * 128
                    L = q0 + 128
                    sjobs = []
                    nblk = (L + 511) // 512
                    for kb in range(nblk):
                        k0 = kb * 512
                        w = min(512, L - k0)
                        has_diag = (k0 + w == L)
                        def mk_evac(k0=k0, w=w, has_diag=has_diag, L=L):
                            def f(e, ps):
                                sc = 1.0 / np.sqrt(96.0)
                                if has_diag:
                                    if w > 128:
                                        e.tensor_single_scalar(
                                            sc_row[:, k0:k0 + w - 128],
                                            ps[:, 0:w - 128], sc, ALU.mult)
                                    return e.scalar_tensor_tensor(
                                        sc_row[:, L - 128:L], ps[:, w - 128:w],
                                        sc, R["causal_t"][:], ALU.mult, ALU.add)
                                return e.tensor_single_scalar(
                                    sc_row[:, k0:k0 + w], ps[:, 0:w], sc, ALU.mult)
                            return f
                        def mk_mms(k0=k0, w=w, q0=q0, base=base, rbase=rbase):
                            def m1(e, ps):
                                return e.matmul(
                                    ps[:, 0:w],
                                    ad[base:base + 64, QO + q0:QO + q0 + 128],
                                    ad[base:base + 64, KO + k0:KO + k0 + w],
                                    start=True, stop=DBG_NOROPE)
                            def m2(e, ps):
                                if DBG_NOROPE:
                                    return None
                                return e.matmul(
                                    ps[:, 0:w],
                                    acbf[rbase:rbase + 32, 6144 + q0: 6144 + q0 + 128],
                                    acbf[rbase:rbase + 32, 4096 + k0: 4096 + k0 + w],
                                    start=DBG_ROPEONLY, stop=True)
                            return [m1, m2]
                        mmsl = mk_mms()
                        if DBG_NOROPE:
                            mmsl = mmsl[:1]
                        elif DBG_ROPEONLY:
                            mmsl = mmsl[1:]
                        sjobs.append(dict(
                            pre=rope_tok,
                            mms=mmsl,
                            psum=lambda b: b,
                            evac=mk_evac(),
                        ))
                    sweep(ctx, jobs=sjobs, banks=R["pb"][0:2], evac_eng="vector",
                          entry=ctx.tok("evs", "evv"))
                    sc_tok = ctx.tok("evv")

                    @blk.vector
                    def _(e, L=L):
                        ctx.wait(e, sc_tok)
                        instr = e.tensor_reduce(R["cols"][:, 0:1],
                                                sc_row[:, 0:L], AX.X, ALU.max,
                                                negate=True)
                        ctx.inc("evv", instr)
                    @blk.scalar
                    def _(e, L=L):
                        ctx.wait(e, ctx.tok("evv"))
                        instr = e.activation(ad[:, AO:AO + L], sc_row[:, 0:L],
                                             AF.Exp, bias=R["cols"][:, 0:1])
                        ctx.inc("evs", instr)
                    @blk.vector
                    def _(e, L=L):
                        ctx.wait(e, ctx.tok("evs"))
                        ctx.inc("evv", e.tensor_reduce(
                            R["cols"][:, 1:2], ad[:, AO:AO + L], AX.X, ALU.add))
                        ctx.wait(e, ctx.tok("evv"))
                        ctx.inc("evv", e.reciprocal(R["cols"][:, 2:3],
                                                    R["cols"][:, 1:2]))
                        ctx.wait(e, ctx.tok("evv"))
                        instr = e.tensor_scalar_mul(ad[:, AO:AO + L],
                                                    ad[:, AO:AO + L],
                                                    R["cols"][:, 2:3])
                        ctx.inc("evv", instr)
                    sm_tok = ctx.tok("evv")

                    tjobs = []
                    for tkc in range(L // 128):
                        tjobs.append(dict(
                            pre=sm_tok,
                            mms=[(lambda tkc: lambda e, ps: e.transpose(
                                ps, ad[:, AO + tkc * 128:AO + (tkc + 1) * 128],
                                R["iden_b"][:]))(tkc)],
                            psum=lambda b: b[:, 0:64].bitcast(BF16),
                            evac=(lambda tkc, qs=qs: lambda e, ps: e.tensor_copy(
                                attnT[:, tkc * 256 + qs * 128:
                                      tkc * 256 + qs * 128 + 128], ps))(tkc),
                        ))
                    sweep(ctx, jobs=tjobs, banks=R["pb"][2:4], evac_eng="vector",
                          entry=ctx.tok("evs", "evv"))

                at_tok = ctx.tok("evv")
                nk = 2 * (tq + 1)
                avmms = []
                for tkc in range(nk):
                    noff = max(0, (tkc - 2 * tq)) * 128
                    def mk(tkc=tkc, noff=noff, first=(tkc == 0),
                           last=(tkc == nk - 1), base=base):
                        def f(e, ps):
                            return e.matmul(
                                ps[0:64, noff:256],
                                ad[:, VO + tkc * 128 + base:
                                   VO + tkc * 128 + base + 64],
                                attnT[:, tkc * 256 + noff: (tkc + 1) * 256],
                                start=first, stop=last)
                        return f
                    avmms.append(mk())
                avjobs = [dict(
                    pre=at_tok,
                    mms=avmms,
                    psum=lambda b: b,
                    evac=(lambda hg=hg, h2=h2, tq=tq: lambda e, ps: e.tensor_copy(
                        o_all.ch(hg, tq * 256, 256, p0=64 * h2, pw=64),
                        ps[0:64, 0:256]))(),
                )]
                sweep(ctx, jobs=avjobs, banks=[R["pb"][4]], evac_eng="vector",
                      entry=ctx.tok("evs", "evv"))

    if dbg and "dbg_o" in dbg:
        @blk.gpsimd
        def _(e):
            ctx.wait(e, ctx.tok("evv"))
            ctx.inc("out", e.dma_start(
                dbg["dbg_o"].rearrange("(c p) t -> p c t", p=128),
                R["arena_a"][:].rearrange("p (c t) -> p c t", t=T)), 16)
    wo_t = R["arena_d"]
    wo_tok = load_w(ctx, d["l2_wo"], wo_t, pre=ctx.tok("mm"))
    s_fm = FM(R["xn2"], DC, 512)
    for t in range(TC):
        t0 = t * 512
        jobs = []
        for mch in range(DC):
            jobs.append(dict(
                pre=wo_tok + ctx.tok("evv"),
                mms=mm_acc([(wo_t[:, c * 1024 + mch * 128:
                                  c * 1024 + (mch + 1) * 128],
                             o_all.ch(c, t0, 512)) for c in range(DC)]),
                psum=lambda b: b[:, :],
                evac=(lambda mch: lambda e, ps: e.tensor_copy(
                    s_fm.ch(mch), ps))(mch),
            ))
        sweep(ctx, jobs=jobs, banks=R["pb"][5:7], evac_eng="vector",
              entry=ctx.tok("evs", "evv"))
        @blk.gpsimd
        def _(e, t0=t0):
            ctx.wait(e, ctx.tok("evv"))
            for c in range(DC):
                ctx.inc("ldg2", e.dma_start(
                    s_spill[:, c * T + t0: c * T + t0 + 512], s_fm.ch(c)), 16)
    if dbg and "dbg_s" in dbg:
        @blk.gpsimd
        def _(e):
            ctx.wait(e, ctx.tok("ldg2"))
            ctx.inc("out", e.dma_start(
                dbg["dbg_s"].rearrange("(c p) t -> p c t", p=128),
                s_spill[:].rearrange("p (c t) -> p c t", t=T)), 16)

    from concourse.bass import IndirectOffsetOnAxis as IOA
    xa = R["arena_a"]
    @blk.gpsimd
    def _(e):
        ctx.wait(e, ag1_tok + ctx.tok("mm", "evv"))
        src3 = cc_out.rearrange("q (s j) -> q s j", j=256)
        for t8 in range(8):
            for chh in range(DC):
                col = t8 * 8 + chh
                ctx.inc("ldg2", e.indirect_dma_start(
                    xa[:, col * 256:(col + 1) * 256],
                    None, src3, IOA(gidx[0:128, col:col + 1], 1)), 16)
    xa_tok = ctx.tok("ldg2")

    cst = FM(R["arena_c"], DC, 256)
    gp_f = FM(R["arena_c"], DC, 256, col0=2048)
    gp_i = FM(R["sq"], DC, 256)
    fb_fm = FM(R["xn2"], DC, 256)
    ib_fm = FM(R["xn2"], DC, 256, col0=2048)
    gb_fm = FM(R["arena_d"], DC, 256)
    tmp_fm = FM(R["arena_d"], DC, 256, col0=2048)
    cbf_fm = FM(R["arena_d"], DC, 256, col0=4096)
    wlb = R["arena_b"]

    @blk.gpsimd
    def _(e):
        ctx.wait(e, ctx.tok("evv", "mm"))
        ctx.inc("evp", e.memset(R["arena_c"][:, 0:2048], 0.0))
    @blk.vector
    def _(e):
        ctx.wait(e, small_tok + ctx.tok("evp"))
        instr = None
        for c in range(DC):
            e.tensor_scalar_add(cst.ch(c), cst.ch(c), inc_t[:, c:c + 1])
            instr = e.tensor_copy(cbf_fm.ch(c), cst.ch(c))
        ctx.inc("evs", instr)

    for t8 in range(8):
        for gi_, (wn, un, afn, dst) in enumerate([
                ("l2_wf", "l2_uf", AF.Sigmoid, fb_fm),
                ("l2_wi", "l2_ui", AF.Sigmoid, ib_fm),
                ("l2_wc", "l2_uc", AF.Tanh, gb_fm)]):
            for part, wname in [(0, wn), (1, un)]:
                w_tok = load_w(ctx, d[wname], wlb, pre=ctx.tok("mm"))
                jobs = []
                for mch in range(DC):
                    def mk_ev(part=part, mch=mch):
                        if part == 0:
                            return lambda e, ps: e.tensor_copy(gp_i.ch(mch), ps[:, 0:256])
                        return lambda e, ps: e.tensor_tensor(
                            gp_i.ch(mch), gp_i.ch(mch), ps[:, 0:256], ALU.add)
                    rhs_fm = (lambda c, t8=t8: xa[:, (t8 * 8 + c) * 256:
                                                  (t8 * 8 + c + 1) * 256])                         if part == 0 else (lambda c: cbf_fm.ch(c))
                    jobs.append(dict(
                        pre=w_tok + xa_tok + ctx.tok("evs"),
                        mms=mm_acc([(wlb[:, c * 1024 + mch * 128:
                                         c * 1024 + (mch + 1) * 128],
                                     rhs_fm(c)) for c in range(DC)]),
                        psum=lambda b: b[:, 0:256],
                        evac=mk_ev(),
                    ))
                sweep(ctx, jobs=jobs, banks=R["pb"][0:4], evac_eng="vector",
                      entry=ctx.tok("evs", "evv"))
            gp_tok = ctx.tok("evv")
            @blk.scalar
            def _(e, gp_tok=gp_tok, gi_=gi_, afn=afn, dst=dst):
                ctx.wait(e, gp_tok + small_tok)
                instr = None
                for c in range(DC):
                    instr = e.activation(dst.ch(c), gp_i.ch(c), afn,
                                         bias=bias_t[:, gi_ * DC + c:gi_ * DC + c + 1])
                ctx.inc("evs", instr)
        act_tok = ctx.tok("evs")
        @blk.vector
        def _(e, act_tok=act_tok):
            ctx.wait(e, act_tok)
            for c in range(DC):
                ctx.inc("evv", e.tensor_tensor(tmp_fm.ch(c), ib_fm.ch(c),
                                               gb_fm.ch(c), ALU.mult))
                ctx.inc("evv", e.tensor_tensor(cst.ch(c), fb_fm.ch(c),
                                               cst.ch(c), ALU.mult))
                ctx.wait(e, ctx.tok("evv"))
                ctx.inc("evv", e.tensor_tensor(cst.ch(c), cst.ch(c),
                                               tmp_fm.ch(c), ALU.add))
                ctx.wait(e, ctx.tok("evv"))
                ctx.inc("evv", e.tensor_copy(cbf_fm.ch(c), cst.ch(c)))
                ctx.wait(e, ctx.tok("evv"))
            ctx.inc("evv", e.nop())

    cfin_tok = ctx.tok("evv")
    w_tok = load_w(ctx, d["l2_uo"], wlb, pre=ctx.tok("mm"))
    jobs = []
    for mch in range(DC):
        jobs.append(dict(
            pre=w_tok + cfin_tok,
            mms=mm_acc([(wlb[:, c * 1024 + mch * 128:
                             c * 1024 + (mch + 1) * 128],
                         cbf_fm.ch(c)) for c in range(DC)]),
            psum=lambda b: b[:, 0:256],
            evac=(lambda mch: lambda e, ps: e.tensor_copy(
                gp_f.ch(mch), ps[:, 0:256]))(mch),
        ))
    sweep(ctx, jobs=jobs, banks=R["pb"][0:2], evac_eng="vector",
          entry=ctx.tok("evs", "evv"))
    cpart_tok = ctx.tok("evv")
    w_tok = load_w(ctx, d["l2_wo2"], wlb, pre=ctx.tok("mm"))
    for t8 in range(8):
        jobs = []
        for mch in range(DC):
            jobs.append(dict(
                pre=w_tok + cpart_tok,
                mms=mm_acc([(wlb[:, c * 1024 + mch * 128:
                                 c * 1024 + (mch + 1) * 128],
                             xa[:, (t8 * 8 + c) * 256:(t8 * 8 + c + 1) * 256])
                            for c in range(DC)]),
                psum=lambda b: b[:, 0:256],
                evac=(lambda mch: lambda e, ps: e.tensor_tensor(
                    gp_i.ch(mch), ps[:, 0:256], gp_f.ch(mch), ALU.add))(mch),
            ))
        sweep(ctx, jobs=jobs, banks=R["pb"][0:2], evac_eng="vector",
              entry=ctx.tok("evs", "evv"))
        hx_tok = ctx.tok("evv")
        @blk.scalar
        def _(e, hx_tok=hx_tok, t8=t8):
            ctx.wait(e, hx_tok + ctx.tok("ldg2"))
            instr = None
            for c in range(DC):
                instr = e.activation(fb_fm.ch(c), gp_i.ch(c), AF.Tanh,
                                     bias=bias_t[:, 3 * DC + c:3 * DC + c + 1])
            ctx.inc("evs", instr)
        @blk.gpsimd
        def _(e, t8=t8):
            ctx.wait(e, ctx.tok("evs"))
            for c in range(DC):
                ctx.inc("ldg2", e.dma_start(
                    cc2_in[:, (t8 * 8 + c) * 256:(t8 * 8 + c + 1) * 256],
                    fb_fm.ch(c)), 16)

    @blk.gpsimd
    def _(e):
        ctx.wait(e, ctx.tok("ldg2"))
        ctx.inc("cc", e.collective_compute(
            "AllGather", ALU.bypass, replica_groups=[list(range(NCORES))],
            ins=[cc2_in.opt()], outs=[cc2_out.opt()]))
    ag2_tok = ctx.tok("cc")

    gg_fm = FM(R["xn2"], DC, 256)
    @blk.sync
    def _(e):
        ctx.wait(e, ctx.tok("ldg2"))
        ctx.inc("ld", e.dma_start(gidx, d["l2_gidx2"][:]), 16)
    @blk.gpsimd
    def _(e):
        ctx.wait(e, ctx.tok("ld"))
        ctx.inc("evp", e.nop())
    for r in range(8):
        @blk.gpsimd
        def _(e, r=r):
            ctx.wait(e, ag2_tok + ctx.tok("evs"))
            src3 = cc2_out.rearrange("q (s j) -> q s j", j=256)
            for chh in range(DC):
                col = r * 8 + chh
                ctx.inc("ldg2", e.indirect_dma_start(
                    gg_fm.ch(chh), None, src3,
                    IOA(gidx[0:128, col:col + 1], 1)), 16)
        gh_tok = ctx.tok("ldg2")
        @blk.scalar
        def _(e, gh_tok=gh_tok, r=r):
            ctx.wait(e, gh_tok)
            instr = None
            for c in range(DC):
                instr = e.activation(gg_fm.ch(c), gg_fm.ch(c), AF.Silu)
            ctx.inc("evs", instr)
        @blk.gpsimd
        def _(e, r=r):
            ctx.wait(e, ctx.tok("evs"))
            for c in range(DC):
                ctx.inc("ldg2", e.dma_start(
                    g_spill[:, c * T + r * 256: c * T + (r + 1) * 256],
                    gg_fm.ch(c)), 16)
    if dbg and "dbg_g" in dbg:
        @blk.gpsimd
        def _(e):
            ctx.wait(e, ctx.tok("ldg2"))
            ctx.inc("out", e.dma_start(
                dbg["dbg_g"].rearrange("(c p) t -> p c t", p=128),
                g_spill[:].rearrange("p (c t) -> p c t", t=T)), 16)

    emit_combine_spill(ctx, R, g_spill, s_spill)




def l3_prep(I, m):
    m["l3_wupT"] = np.ascontiguousarray(np.asarray(I["L3_attn_w_up"]).T)
    m["l3_bup"] = _col_layout(I["L3_attn_b_up"], 32)
    cw = np.asarray(I["L3_attn_conv_w"], np.float32).reshape(2048, 4)
    m["l3_convw"] = np.ascontiguousarray(
        cw.reshape(16, 128, 4).transpose(1, 0, 2).reshape(128, 64))
    m["l3_convb"] = _col_layout(I["L3_attn_conv_b"], 16)
    m["l3_skip"] = _col_layout(I["L3_attn_skip"], 16)
    bd = np.zeros((3, 16, 128, 128), np.float32)
    for gi, key in enumerate(["L3_attn_w_f", "L3_attn_w_i", "L3_attn_w_h"]):
        w = np.asarray(I[key], np.float32)
        for c in range(16):
            for gl in range(8):
                bd[gi, c, gl * 16:(gl + 1) * 16, gl * 16:(gl + 1) * 16] = \
                    w[8 * c + gl].T
    m["l3_bd"] = np.ascontiguousarray(bd.reshape(3, 2048, 128))
    m["l3_wdownT"] = np.ascontiguousarray(np.asarray(I["L3_attn_w_down"]).T)
    m["l3_bdown"] = _col_layout(I["L3_attn_b_down"], DC)
    m["l3_gwT"] = np.ascontiguousarray(np.asarray(I["L3_ffn_gate_w"]).T)
    m["l3_rms"] = _col_layout(I["L3_ffn_rms_w"], DC)
    m["l3_w1T"] = np.ascontiguousarray(np.asarray(I["L3_ffn_W1"]).transpose(0, 2, 1))
    m["l3_w2T"] = np.ascontiguousarray(np.asarray(I["L3_ffn_W2"]).transpose(0, 2, 1))
    m["l3_w3T"] = np.ascontiguousarray(np.asarray(I["L3_ffn_W3"]).transpose(0, 2, 1))


def l3_declare(nc, d):
    def din(name, shape, dt=F32):
        d[name] = nc.dram_tensor(name, shape, dt, kind="ExternalInput").ap()
    din("l3_wupT", [1024, 4096]); din("l3_bup", [128, 32])
    din("l3_convw", [128, 64]); din("l3_convb", [128, 16]); din("l3_skip", [128, 16])
    din("l3_bd", [3, 2048, 128]); din("l3_wdownT", [2048, 1024])
    din("l3_bdown", [128, DC])
    din("l3_gwT", [1024, 8]); din("l3_rms", [128, DC])
    din("l3_w1T", [8, 1024, 2816]); din("l3_w2T", [8, 1024, 2816])
    din("l3_w3T", [8, 2816, 1024])


def emit_l3(ctx, R, d, g_spill, s_spill, dbg=None):
    blk = ctx.blk
    nc = ctx.nc
    x_fm = FM(R["x"], DC, T)
    xn_fm = FM(R["xn"], DC, T)

    smf, smb = R["smf"], R["smb"]
    bup = smf[0:128, 0:32]
    convw = smf[0:128, 32:96]
    convb = smf[0:128, 96:112]
    skip = smf[0:128, 112:128]
    bdown = smf[0:128, 128:136]
    rmsw = smf[0:128, 136:144]
    hstate = smf[0:128, 144:160]
    gw_t = smb[0:128, 0:64]
    halo = smb[0:128, 64:128]
    xme = smb[0:128, 128:648]
    sel_v = smb[0:8, 648:1672]
    bdv = R["xn2"]

    act_spill = nc.dram_tensor("l3_act_sp", [128, 16 * T], BF16, kind="Internal").ap()
    z_spill = nc.dram_tensor("l3_z_sp", [128, 16 * T], BF16, kind="Internal").ap()

    @blk.sync
    def _(e):
        ctx.wait(e, ctx.tok("comb", "evv", "evs"))
        for src, dst in [("l3_bup", bup), ("l3_convw", convw),
                         ("l3_convb", convb), ("l3_skip", skip),
                         ("l3_bdown", bdown), ("l3_rms", rmsw),
                         ("sel", sel_v)]:
            ctx.inc("ld", e.dma_start(dst, d[src][:]), 16)
    load_w(ctx, d["l3_gwT"], gw_t)
    small_tok = ctx.tok("ld", "ldg")

    emit_ln(ctx, R, x_fm, 0, T, xn_fm, 0, rms=False, in_f32=True)
    ln_tok = ctx.tok("ln_done")
    if dbg and "dbg_xn" in dbg:
        @blk.gpsimd
        def _(e):
            ctx.wait(e, ctx.tok("ln_done"))
            ctx.inc("out", e.dma_start(
                dbg["dbg_xn"].rearrange("(c p) t -> p c t", p=128),
                R["xn"][:].rearrange("p (c t) -> p c t", t=T)), 16)

    wbuf = R["arena_a"]
    xm_fm = FM(R["arena_d"], 16, 512)
    f_fm = FM(R["arena_b"], 16, 512)
    acbf = R["arena_c"].bitcast(BF16)
    sq = R["sq"]

    def ich(c):
        return acbf[0:128, c * 512:(c + 1) * 512]

    @blk.gpsimd
    def _(e):
        ctx.wait(e, ctx.tok("comb", "ldg2"))
        e.memset(hstate, 0.0)
        ctx.inc("evp", e.memset(halo, 0.0))
    init_tok = ctx.tok("evp")

    wup_tok = load_w(ctx, d["l3_wupT"][:, 0:2048], wbuf, pre=ctx.tok("mm"))

    for t in range(TC):
        t0 = t * 512
        jobs = []
        for mch in range(16):
            jobs.append(dict(
                pre=wup_tok + ln_tok + ctx.tok("evs"),
                mms=mm_acc([(wbuf[:, c * 2048 + mch * 128:
                                  c * 2048 + (mch + 1) * 128],
                             xn_fm.ch(c, t0, 512)) for c in range(DC)]),
                psum=lambda b: b[:, :],
                evac=(lambda mch: lambda e, ps: e.tensor_scalar_add(
                    xm_fm.ch(mch), ps, bup[:, mch:mch + 1]))(mch),
            ))
        sweep(ctx, jobs=jobs, banks=R["pb"][0:3], evac_eng="vector",
              entry=ctx.tok("evs", "evv"))
        xm_tok = ctx.tok("evv")

        if dbg and "dbg_act" in dbg and t == 0:
            @blk.gpsimd
            def _(e, xm_tok=xm_tok):
                ctx.wait(e, xm_tok)
                ctx.inc("out", e.dma_start(
                    dbg["dbg_act"][0:128, 32 * T + 16: 32 * T + 16 + 512],
                    xm_fm.ch(0)), 16)
            @blk.vector
            def _(e):
                ctx.wait(e, ctx.tok("out"))
                ctx.inc("evv", e.nop())
        for c in range(16):
            if dbg and "dbg_act" in dbg and t == 0 and c == 1:
                @blk.gpsimd
                def _(e):
                    ctx.wait(e, ctx.tok("evv"))
                    ctx.inc("out", e.dma_start(
                        dbg["dbg_act"][0:128, 32 * T + 528: 32 * T + 1040],
                        sq[:, 0:512]), 16)
                @blk.vector
                def _(e):
                    ctx.wait(e, ctx.tok("out"))
                    ctx.inc("evv", e.nop())
            @blk.vector
            def _(e, c=c, xm_tok=xm_tok):
                ctx.wait(e, xm_tok + ctx.tok("evs") + small_tok + init_tok)
                e.tensor_copy(xme[0:128, 0:4], halo[0:128, c * 4:(c + 1) * 4])
                ctx.inc("evv", e.tensor_copy(xme[0:128, 4:516], xm_fm.ch(c)))
                ctx.wait(e, ctx.tok("evv"))
                ctx.inc("evv", e.tensor_copy(halo[0:128, c * 4:(c + 1) * 4],
                                             xme[0:128, 512:516]))
                ctx.wait(e, ctx.tok("evv"))
                ctx.inc("evv", e.tensor_scalar_mul(sq[:, 0:512], xme[0:128, 1:513],
                                    convw[:, c * 4:c * 4 + 1]))
                for k in range(1, 4):
                    ctx.wait(e, ctx.tok("evv"))
                    ctx.inc("evv", e.scalar_tensor_tensor(
                        sq[:, 0:512], xme[0:128, 1 + k:513 + k],
                        convw[:, c * 4 + k:c * 4 + k + 1],
                        sq[:, 0:512], ALU.mult, ALU.add))
                ctx.wait(e, ctx.tok("evv"))
                ctx.inc("evv", e.nop())
            @blk.scalar
            def _(e, c=c):
                ctx.wait(e, ctx.tok("evv"))
                instr = e.activation(xm_fm.ch(c), sq[:, 0:512], AF.Silu,
                                     bias=convb[:, c:c + 1])
                ctx.inc("evs", instr)
        act_tok = ctx.tok("evs")

        @blk.gpsimd
        def _(e, t0=t0, act_tok=act_tok):
            ctx.wait(e, act_tok + ctx.tok("ldg2"))
            for c in range(16):
                ctx.inc("ldg2", e.dma_start(
                    act_spill[:, c * T + t0: c * T + t0 + 512],
                    xm_fm.ch(c)), 16)

        for gi in range(2):
            bd_tok = load_w(ctx, d["l3_bd"][gi], bdv, pre=ctx.tok("mm"))
            jobs = []
            for c in range(16):
                def mk_ev(gi=gi, c=c):
                    if gi == 0:
                        return lambda e, ps: e.tensor_copy(f_fm.ch(c), ps)
                    return lambda e, ps: e.tensor_copy(ich(c), ps)
                jobs.append(dict(
                    pre=act_tok + small_tok + bd_tok,
                    mms=mm_acc([(bdv[:, c * 128:(c + 1) * 128],
                                 xm_fm.ch(c))]),
                    psum=lambda b: b[:, :],
                    evac=mk_ev(),
                ))
            sweep(ctx, jobs=jobs, banks=R["pb"][0:3], evac_eng="vector",
                  entry=ctx.tok("evs", "evv"))
        bd_tok = load_w(ctx, d["l3_bd"][2], bdv, pre=ctx.tok("mm"))
        fi_tok = ctx.tok("evv") + bd_tok

        s0 = sq[:, 0:512]; s1 = sq[:, 512:1024]
        s2 = sq[:, 1024:1536]; s3 = sq[:, 1536:2048]
        mtile = xme[0:128, 0:512]
        for c in range(16):
            hjobs = [dict(
                pre=fi_tok + act_tok + ctx.tok("evv"),
                mms=mm_acc([(bdv[:, c * 128:(c + 1) * 128],
                             xm_fm.ch(c))]),
                psum=lambda b: b[:, :],
                evac=lambda e, ps: e.tensor_copy(s3, ps),
            )]
            sweep(ctx, jobs=hjobs, banks=[R["pb"][3]], evac_eng="vector",
                  entry=ctx.tok("evs", "evv"))
            fch = f_fm.ch(c)
            @blk.vector
            def _(e, c=c):
                ctx.wait(e, ctx.tok("evv"))
                instr = e.tensor_single_scalar(mtile, s3, 0.0, ALU.is_ge)
                ctx.inc("evv", instr)
            @blk.scalar
            def _(e, c=c, fch=fch):
                ctx.wait(e, ctx.tok("evv"))
                e.activation(s0, fch, AF.Abs)
                e.activation(s0, s0, AF.Exp, scale=-1.0)
                e.activation(s0, s0, AF.Ln, bias=1.0)
                instr = e.activation(s1, fch, AF.Relu, scale=-1.0)
                ctx.inc("evs", instr)
            @blk.vector
            def _(e, c=c):
                ctx.wait(e, ctx.tok("evs"))
                instr = e.tensor_tensor(s0, s0, s1, ALU.add)
                ctx.inc("evv", instr)
            @blk.scalar
            def _(e, c=c):
                ctx.wait(e, ctx.tok("evv"))
                e.activation(s1, ich(c), AF.Abs)
                e.activation(s1, s1, AF.Exp, scale=-1.0)
                e.activation(s1, s1, AF.Ln, bias=1.0)
                instr = e.activation(s2, ich(c), AF.Relu, scale=-1.0)
                ctx.inc("evs", instr)
            @blk.vector
            def _(e, c=c):
                ctx.wait(e, ctx.tok("evs"))
                ctx.inc("evv", e.tensor_tensor(s1, s1, s2, ALU.add))
                ctx.wait(e, ctx.tok("evv"))
                ctx.inc("evv", e.tensor_tensor(s0, s0, s1, ALU.subtract))
                ctx.wait(e, ctx.tok("evv"))
                ctx.inc("evv", e.nop())
            @blk.scalar
            def _(e, c=c):
                ctx.wait(e, ctx.tok("evv"))
                e.activation(s1, s0, AF.Abs)
                e.activation(s1, s1, AF.Exp, scale=-1.0)
                e.activation(s1, s1, AF.Ln, bias=1.0)
                instr = e.activation(s2, s0, AF.Relu)
                ctx.inc("evs", instr)
            @blk.vector
            def _(e, c=c):
                ctx.wait(e, ctx.tok("evs"))
                instr = e.scalar_tensor_tensor(s2, s2, -1.0, s1,
                                               ALU.mult, ALU.subtract)
                ctx.inc("evv", instr)
            @blk.scalar
            def _(e, c=c, fch=fch):
                ctx.wait(e, ctx.tok("evv"))
                instr = e.activation(fch, s2, AF.Exp, bias=cb(R, 1e-8))
                ctx.inc("evs", instr)
            @blk.vector
            def _(e, c=c):
                ctx.wait(e, ctx.tok("evs"))
                instr = e.scalar_tensor_tensor(s0, s0, 0.0, s1,
                                               ALU.min, ALU.subtract)
                ctx.inc("evv", instr)
            @blk.scalar
            def _(e, c=c):
                ctx.wait(e, ctx.tok("evv"))
                e.activation(s2, s3, AF.Relu)
                e.activation(s2, s2, AF.Ln, bias=cb(R, 0.5 + 1e-8))
                e.activation(s1, s3, AF.Abs)
                e.activation(s1, s1, AF.Exp, scale=-1.0)
                e.activation(s1, s1, AF.Ln, bias=1.0)
                instr = e.activation(s3, s3, AF.Relu, scale=-1.0)
                ctx.inc("evs", instr)
            @blk.vector
            def _(e, c=c):
                ctx.wait(e, ctx.tok("evs"))
                ctx.inc("evv", e.tensor_tensor(s1, s1, s3, ALU.add))
                ctx.wait(e, ctx.tok("evv"))
                ctx.inc("evv", e.tensor_scalar(s1, s1, -1.0, 1e-8,
                                               ALU.mult, ALU.add))
                ctx.wait(e, ctx.tok("evv"))
                ctx.inc("evv", e.tensor_tensor(s3, s2, s1, ALU.subtract))
                ctx.wait(e, ctx.tok("evv"))
                ctx.inc("evv", e.tensor_tensor(s3, mtile, s3, ALU.mult))
                ctx.wait(e, ctx.tok("evv"))
                ctx.inc("evv", e.tensor_tensor(s3, s1, s3, ALU.add))
                ctx.wait(e, ctx.tok("evv"))
                ctx.inc("evv", e.tensor_tensor(s0, s0, s3, ALU.add))
                ctx.wait(e, ctx.tok("evv"))
                ctx.inc("evv", e.nop())
            @blk.scalar
            def _(e, c=c):
                ctx.wait(e, ctx.tok("evv"))
                instr = e.activation(ich(c), s0, AF.Exp)
                ctx.inc("evs", instr)
            @blk.vector
            def _(e, c=c, fch=fch):
                ctx.wait(e, ctx.tok("evs") + ctx.tok("scan"))
                ctx.inc("scan", e.tensor_tensor_scan(
                    s3, fch, ich(c), hstate[0:128, c:c + 1], ALU.mult, ALU.add))
                ctx.wait(e, ctx.tok("scan"))
                ctx.inc("scan", e.tensor_copy(hstate[0:128, c:c + 1],
                                              s3[0:128, 511:512]))
                ctx.wait(e, ctx.tok("scan"))
                ctx.inc("evv", e.nop())

    wupz_tok = load_w(ctx, d["l3_wupT"][:, 2048:4096], wbuf, pre=ctx.tok("mm"))
    for t in range(TC):
        t0 = t * 512
        jobs = []
        for mch in range(16):
            jobs.append(dict(
                pre=wupz_tok + ln_tok,
                mms=mm_acc([(wbuf[:, c * 2048 + mch * 128:
                                  c * 2048 + (mch + 1) * 128],
                             xn_fm.ch(c, t0, 512)) for c in range(DC)]),
                psum=lambda b: b[:, :],
                evac=(lambda mch: lambda e, ps: e.activation(
                    xm_fm.ch(mch), ps, AF.Silu,
                    bias=bup[:, 16 + mch:16 + mch + 1]))(mch),
            ))
        sweep(ctx, jobs=jobs, banks=R["pb"][0:3], evac_eng="scalar",
              entry=ctx.tok("evs", "evv"))
        z_tok = ctx.tok("evs")
        @blk.gpsimd
        def _(e, t0=t0, z_tok=z_tok):
            ctx.wait(e, z_tok + ctx.tok("ldg2"))
            for c in range(16):
                ctx.inc("ldg2", e.dma_start(
                    z_spill[:, c * T + t0: c * T + t0 + 512],
                    xm_fm.ch(c)), 16)

    wd_tok = load_w(ctx, d["l3_wdownT"], wbuf, pre=ctx.tok("mm"))
    sz_fm = FM(R["arena_b"], 16, 512)
    s_fm = FM(R["xn2"], DC, 512)
    for t in range(TC):
        t0 = t * 512
        @blk.gpsimd
        def _(e, t0=t0):
            ctx.wait(e, ctx.tok("mm", "evv", "ldg2", "scan"))
            for c in range(16):
                ctx.inc("ldg2", e.dma_start(
                    xm_fm.ch(c), act_spill[:, c * T + t0: c * T + t0 + 512]), 16)
                ctx.inc("ldg2", e.dma_start(
                    sz_fm.ch(c), z_spill[:, c * T + t0: c * T + t0 + 512]), 16)
        ld_tok = ctx.tok("ldg2")
        @blk.vector
        def _(e, ld_tok=ld_tok):
            ctx.wait(e, ld_tok + ctx.tok("scan"))
            for c in range(16):
                ctx.inc("evv", e.tensor_scalar(xm_fm.ch(c), xm_fm.ch(c),
                                               skip[:, c:c + 1],
                                               hstate[0:128, c:c + 1],
                                               ALU.mult, ALU.add))
                ctx.wait(e, ctx.tok("evv"))
                ctx.inc("evv", e.tensor_tensor(xm_fm.ch(c), xm_fm.ch(c),
                                               sz_fm.ch(c), ALU.mult))
                ctx.wait(e, ctx.tok("evv"))
            ctx.inc("evv", e.nop())
        h_tok = ctx.tok("evv")
        jobs = []
        for mch in range(DC):
            jobs.append(dict(
                pre=wd_tok + h_tok,
                mms=mm_acc([(wbuf[:, k * 1024 + mch * 128:
                                  k * 1024 + (mch + 1) * 128],
                             xm_fm.ch(k)) for k in range(16)]),
                psum=lambda b: b[:, :],
                evac=(lambda mch: lambda e, ps: e.tensor_scalar_add(
                    s_fm.ch(mch), ps, bdown[:, mch:mch + 1]))(mch),
            ))
        sweep(ctx, jobs=jobs, banks=R["pb"][4:6], evac_eng="vector",
              entry=ctx.tok("evs", "evv"))
        @blk.gpsimd
        def _(e, t0=t0):
            ctx.wait(e, ctx.tok("evv"))
            for c in range(DC):
                ctx.inc("ldg2", e.dma_start(
                    s_spill[:, c * T + t0: c * T + t0 + 512], s_fm.ch(c)), 16)
    if dbg and "dbg_s" in dbg:
        @blk.gpsimd
        def _(e):
            ctx.wait(e, ctx.tok("ldg2"))
            ctx.inc("out", e.dma_start(
                dbg["dbg_s"].rearrange("(c p) t -> p c t", p=128),
                s_spill[:].rearrange("p (c t) -> p c t", t=T)), 16)
    if dbg and "dbg_act" in dbg:
        @blk.gpsimd
        def _(e):
            ctx.wait(e, ctx.tok("ldg2"))
            ctx.inc("out", e.dma_start(
                dbg["dbg_act"][0:128, 0:16 * T].rearrange("p (c t) -> p c t", t=T),
                act_spill[:].rearrange("p (c t) -> p c t", t=T)), 16)
            ctx.inc("out", e.dma_start(
                dbg["dbg_act"][0:128, 16 * T:32 * T].rearrange("p (c t) -> p c t", t=T),
                z_spill[:].rearrange("p (c t) -> p c t", t=T)), 16)
            ctx.inc("out", e.dma_start(dbg["dbg_act"][0:128, 32 * T:32 * T + 16],
                                       hstate[:]), 16)

    y_fm = FM(R["xn2"], DC, 512)
    su_fm = FM(R["arena_d"], 11, 512)
    hh_fm = FM(R["arena_b"], 11, 512)
    ya_fm = FM(R["arena_c"], DC, 512)
    cols = R["cols"]
    for t in range(TC):
        t0 = t * 512
        emit_ln(ctx, R, xn_fm, t0, 512, y_fm, 0, rms=True, eps=1e-6,
                in_f32=False, gain_col=lambda c: rmsw[:, c:c + 1])
        y_tok = ctx.tok("ln_done")

        rjobs = []
        for q in range(4):
            rjobs.append(dict(
                pre=small_tok + y_tok,
                mms=mm_acc([(y_fm.ch(c, q * 128, 128),
                             gw_t[:, c * 8:(c + 1) * 8]) for c in range(DC)]),
                psum=lambda b: b[:, 0:8],
                evac=(lambda q: lambda e, ps: e.tensor_copy(
                    R["ltm"][:, q * 8:(q + 1) * 8], ps))(q),
            ))
        sweep(ctx, jobs=rjobs, banks=R["pb"][0:3], evac_eng="vector",
              entry=ctx.tok("evs", "evv"))
        msk1 = sq[:, 0:8]; mskd = sq[:, 8:16]; msk2 = sq[:, 16:24]
        for q in range(4):
            lg = R["ltm"][:, q * 8:(q + 1) * 8]
            @blk.vector
            def _(e, q=q, lg=lg):
                ctx.wait(e, ctx.tok("evv", "evs"))
                ctx.inc("evv", e.tensor_reduce(cols[:, 0:1], lg, AX.X, ALU.max))
                ctx.wait(e, ctx.tok("evv"))
                ctx.inc("evv", e.tensor_scalar(msk1, lg, cols[:, 0:1], 1.0,
                                               ALU.is_ge, ALU.mult))
                ctx.wait(e, ctx.tok("evv"))
                e.scalar_tensor_tensor(mskd, msk1, -1e30, lg, ALU.mult, ALU.add)
                ctx.inc("evv", e.tensor_reduce(cols[:, 1:2], mskd, AX.X, ALU.max))
                ctx.wait(e, ctx.tok("evv"))
                instr = e.tensor_tensor(cols[:, 2:3], cols[:, 1:2], cols[:, 0:1],
                                        ALU.subtract)
                ctx.inc("evv", instr)
            @blk.scalar
            def _(e, q=q):
                ctx.wait(e, ctx.tok("evv"))
                instr = e.activation(cols[:, 3:4], cols[:, 2:3], AF.Exp)
                ctx.inc("evs", instr)
            @blk.vector
            def _(e, q=q, lg=lg):
                ctx.wait(e, ctx.tok("evs"))
                ctx.inc("evv", e.tensor_single_scalar(cols[:, 4:5], cols[:, 3:4],
                                                      1.0, ALU.add))
                ctx.wait(e, ctx.tok("evv"))
                ctx.inc("evv", e.reciprocal(cols[:, 5:6], cols[:, 4:5]))
                ctx.wait(e, ctx.tok("evv"))
                ctx.inc("evv", e.tensor_tensor(cols[:, 6:7], cols[:, 3:4],
                                               cols[:, 5:6], ALU.mult))
                ctx.wait(e, ctx.tok("evv"))
                e.tensor_scalar(msk2, mskd, cols[:, 1:2], cols[:, 6:7],
                                ALU.is_ge, ALU.mult)
                e.tensor_scalar_mul(msk1, msk1, cols[:, 5:6])
                ctx.inc("evv", e.nop())
                ctx.wait(e, ctx.tok("evv"))
                instr = e.tensor_tensor(lg, msk1, msk2, ALU.add)
                ctx.inc("evv", instr)
        @blk.vector
        def _(e):
            ctx.wait(e, ctx.tok("evv"))
            instr = e.tensor_copy(R["ltm_bf"][:], R["ltm"][:])
            ctx.inc("evv", instr)
        tjobs = []
        for q in range(4):
            tjobs.append(dict(
                pre=ctx.tok("evv"),
                mms=[(lambda q: lambda e, ps: e.transpose(
                    ps, R["ltm_bf"][:, q * 8:(q + 1) * 8], R["iden_b"][:]))(q)],
                psum=lambda b: b[0:8, 0:64].bitcast(BF16),
                evac=(lambda q: lambda e, ps: e.tensor_copy(
                    R["cfm"][:, q * 128: (q + 1) * 128], ps))(q),
            ))
        sweep(ctx, jobs=tjobs, banks=R["pb"][2:4], evac_eng="vector",
              entry=ctx.tok("evs", "evv"))
        cfm_tok = ctx.tok("evv")

        for ex in range(8):
            cjobs = [dict(
                pre=cfm_tok,
                mms=mm_acc([(sel_v[0:8, ex * 128:(ex + 1) * 128],
                             R["cfm"][0:8, 0:512])]),
                psum=lambda b: b[:, :],
                evac=lambda e, ps: e.tensor_copy(R["coeffB"], ps),
            )]
            sweep(ctx, jobs=cjobs, banks=[R["pb"][2]], evac_eng="vector",
                  entry=ctx.tok("evs", "evv"))
            coefb_tok = ctx.tok("evv")

            for fi_, (k0, nk) in enumerate([(0, 8), (8, 7), (15, 7)]):
                f0 = k0 * 128
                fw = nk * 128
                slot = fi_ % 2
                sl = wbuf[:, slot * 8192: slot * 8192 + 8192]
                w_tok = load_w(ctx, d["l3_w1T"][ex][:, f0:f0 + fw], wbuf,
                               col0=slot * 8192, pre=ctx.buf_pre(("l3w", slot)))
                jobs = []
                for mch in range(nk):
                    jobs.append(dict(
                        pre=w_tok + y_tok,
                        mms=mm_acc([(sl[:, c * fw + mch * 128:
                                        c * fw + (mch + 1) * 128],
                                     y_fm.ch(c)) for c in range(DC)]),
                        psum=lambda b: b[:, :],
                        evac=(lambda mch: lambda e, ps: e.activation(
                            su_fm.ch(mch), ps, AF.Silu))(mch),
                    ))
                sweep(ctx, jobs=jobs, banks=R["pb"][0:3], evac_eng="scalar",
                      entry=ctx.tok("evs", "evv"))
                ctx.buf_mark(("l3w", slot))
                su_tok = ctx.tok("evs")
                slot2 = (fi_ + 1) % 2
                sl2 = wbuf[:, slot2 * 8192: slot2 * 8192 + 8192]
                w_tok = load_w(ctx, d["l3_w2T"][ex][:, f0:f0 + fw], wbuf,
                               col0=slot2 * 8192, pre=ctx.buf_pre(("l3w", slot2)))
                jobs = []
                for mch in range(nk):
                    jobs.append(dict(
                        pre=w_tok + su_tok,
                        mms=mm_acc([(sl2[:, c * fw + mch * 128:
                                         c * fw + (mch + 1) * 128],
                                     y_fm.ch(c)) for c in range(DC)]),
                        psum=lambda b: b[:, :],
                        evac=(lambda mch: lambda e, ps: e.tensor_tensor(
                            hh_fm.ch(mch), ps, su_fm.ch(mch), ALU.mult))(mch),
                    ))
                sweep(ctx, jobs=jobs, banks=R["pb"][0:3], evac_eng="vector",
                      entry=ctx.tok("evs", "evv"))
                ctx.buf_mark(("l3w", slot2))
                hc_tok = ctx.tok("evv")
                w_tok = load_w(ctx, d["l3_w3T"][ex][f0:f0 + fw, :], wbuf,
                               col0=slot * 8192, pre=ctx.buf_pre(("l3w", slot)),
                               group=4)
                jobs = []
                y3tmp = R["arena_d"].bitcast(F32)
                for mch in range(DC):
                    def mk_ev(ex=ex, fi_=fi_, mch=mch):
                        def f(e, ps):
                            tv = y3tmp[:, 2816 + (mch % 2) * 512:
                                       2816 + (mch % 2) * 512 + 512]
                            e.tensor_tensor(tv, ps, R["coeffB"], ALU.mult)
                            if ex == 0 and fi_ == 0:
                                return e.tensor_copy(ya_fm.ch(mch), tv)
                            return e.tensor_tensor(ya_fm.ch(mch), ya_fm.ch(mch),
                                                   tv, ALU.add)
                        return f
                    jobs.append(dict(
                        pre=w_tok + hc_tok + coefb_tok,
                        mms=mm_acc([(sl[:, k * 1024 + mch * 128:
                                        k * 1024 + (mch + 1) * 128],
                                     hh_fm.ch(k)) for k in range(nk)]),
                        psum=lambda b: b[:, :],
                        evac=mk_ev(),
                    ))
                sweep(ctx, jobs=jobs, banks=R["pb"][0:3], evac_eng="vector",
                      entry=ctx.tok("evs", "evv"))
                ctx.buf_mark(("l3w", slot))

        ya_tok = ctx.tok("evv")
        @blk.vector
        def _(e, t0=t0, ya_tok=ya_tok):
            ctx.wait(e, ya_tok)
            instr = None
            for c in range(DC):
                instr = e.tensor_tensor(ya_fm.ch(c), ya_fm.ch(c),
                                        xn_fm.ch(c, t0, 512), ALU.add)
            ctx.inc("evv", instr)
        @blk.scalar
        def _(e, t0=t0):
            ctx.wait(e, ctx.tok("evv") + ctx.tok("ldg2"))
            instr = None
            for c in range(DC):
                instr = e.activation(y_fm.ch(c), ya_fm.ch(c), AF.Silu)
            ctx.inc("evs", instr)
        @blk.gpsimd
        def _(e, t0=t0):
            ctx.wait(e, ctx.tok("evs"))
            for c in range(DC):
                ctx.inc("ldg2", e.dma_start(
                    g_spill[:, c * T + t0: c * T + t0 + 512], y_fm.ch(c)), 16)
    if dbg and "dbg_g" in dbg:
        @blk.gpsimd
        def _(e):
            ctx.wait(e, ctx.tok("ldg2"))
            ctx.inc("out", e.dma_start(
                dbg["dbg_g"].rearrange("(c p) t -> p c t", p=128),
                g_spill[:].rearrange("p (c t) -> p c t", t=T)), 16)

    emit_combine_spill(ctx, R, g_spill, s_spill)
def build_layer(layer, dbg_names=()):
    nc = bass.Bass("TRN2", target_bir_lowering=False)
    d = {}
    d["iden_bf16"] = nc.dram_tensor("iden_bf16", [128, 128], BF16,
                                    kind="ExternalInput").ap()
    d["causal"] = nc.dram_tensor("causal", [128, 128], F32,
                                 kind="ExternalInput").ap()
    d["sel"] = nc.dram_tensor("sel", [8, 8 * 128], BF16,
                              kind="ExternalInput").ap()
    xin = nc.dram_tensor("xin", [D, T], F32, kind="ExternalInput").ap()
    xout = nc.dram_tensor("xout", [D, T], F32, kind="ExternalOutput").ap()
    dbg = {}
    for nm in dbg_names:
        shape = [8 + 128, T] if nm == "dbg_coeff" else [D, T]
        dt_ = BF16 if nm in ("dbg_att",) else F32
        if nm == "dbg_att":
            shape = [128, 4 * T]
        if nm == "dbg_act":
            shape = [128, 32 * T + 16 + 1024]
            dt_ = F32
        dbg[nm] = nc.dram_tensor(nm, shape, dt_, kind="ExternalOutput").ap()
    g_spill = nc.dram_tensor("g_spill", [128, DC * T], BF16, kind="Internal").ap()
    s_spill = nc.dram_tensor("s_spill", [128, DC * T], BF16, kind="Internal").ap()
    x_spill = nc.dram_tensor("x_spill", [128, DC * T], BF16, kind="Internal").ap()

    if layer == 0:
        l0_declare(nc, d)
    elif layer == 1:
        l1_declare(nc, d)
    elif layer == 2:
        l2_declare(nc, d)
    elif layer == 3:
        l3_declare(nc, d)
    R = None
    with nc.Block() as blk:
        ctx = Ctx(nc, blk)
        for s_ in ["mm", "evs", "evv", "evp", "ldg", "ldg2", "ld", "lds", "sq",
                   "statmm", "lnv", "lns", "ln_done", "out", "comb", "cc",
                   "scan"]:
            ctx.sem(s_)
        R = alloc_resources(nc)
        emit_init(ctx, R, d, xin)
        if layer == 0:
            emit_l0(ctx, R, d, g_spill, dbg)
        elif layer == 1:
            emit_l1(ctx, R, d, g_spill, s_spill, x_spill, dbg)
        elif layer == 2:
            emit_l2(ctx, R, d, g_spill, s_spill, dbg)
        elif layer == 3:
            emit_l3(ctx, R, d, g_spill, s_spill, dbg)
        @blk.sync
        def _(e):
            ctx.wait(e, ctx.tok("comb"))
            instr = e.dma_start(xout.rearrange("(c p) t -> p c t", p=128),
                                R["x"][:].rearrange("p (c t) -> p c t", t=T))
            ctx.inc("out", instr, 16)
            e.wait_ge(ctx.sem("out"), ctx.cnt["out"])
    return nc


def build_full():
    nc = bass.Bass("TRN2", target_bir_lowering=False)
    d = {}
    d["iden_bf16"] = nc.dram_tensor("iden_bf16", [128, 128], BF16,
                                    kind="ExternalInput").ap()
    d["causal"] = nc.dram_tensor("causal", [128, 128], F32,
                                 kind="ExternalInput").ap()
    d["sel"] = nc.dram_tensor("sel", [8, 8 * 128], BF16,
                              kind="ExternalInput").ap()
    xin = nc.dram_tensor("xin", [D, T], F32, kind="ExternalInput").ap()
    xout = nc.dram_tensor("xout", [D, T], F32, kind="ExternalOutput").ap()
    g_spill = nc.dram_tensor("g_spill", [128, DC * T], BF16, kind="Internal").ap()
    s_spill = nc.dram_tensor("s_spill", [128, DC * T], BF16, kind="Internal").ap()
    x_spill = nc.dram_tensor("x_spill", [128, DC * T], BF16, kind="Internal").ap()
    l0_declare(nc, d)
    l1_declare(nc, d)
    l2_declare(nc, d)
    l3_declare(nc, d)
    with nc.Block() as blk:
        ctx = Ctx(nc, blk)
        for s_ in ["mm", "evs", "evv", "evp", "ldg", "ldg2", "ld", "lds", "sq",
                   "statmm", "lnv", "lns", "ln_done", "out", "comb", "cc",
                   "scan"]:
            ctx.sem(s_)
        R = alloc_resources(nc)
        emit_init(ctx, R, d, xin)
        emit_l0(ctx, R, d, g_spill, None)
        emit_l1(ctx, R, d, g_spill, s_spill, x_spill, None)
        emit_l2(ctx, R, d, g_spill, s_spill, None)
        emit_l3(ctx, R, d, g_spill, s_spill, None)
        @blk.sync
        def _(e):
            ctx.wait(e, ctx.tok("comb"))
            instr = e.dma_start(xout.rearrange("(c p) t -> p c t", p=128),
                                R["x"][:].rearrange("p (c t) -> p c t", t=T))
            ctx.inc("out", instr, 16)
            e.wait_ge(ctx.sem("out"), ctx.cnt["out"])
    return nc


_NC_CACHE = {}


def kernel(**inputs):
    common = _prep_common()
    base = dict(common)
    l0_prep(inputs, base)
    l1_prep(inputs, base)
    l3_prep(inputs, base)
    x = np.asarray(inputs["L0_attn_x"], np.float32)
    if "full" not in _NC_CACHE:
        _NC_CACHE["full"] = build_full()
    nc_ = _NC_CACHE["full"]
    in_maps = []
    for b in range(NCORES):
        m = dict(base)
        l2_prep(inputs, m, b)
        m["xin"] = np.ascontiguousarray(x[b].T)
        in_maps.append(m)
    res = run_bass_kernel_spmd(nc_, in_maps, core_ids=list(range(NCORES)))
    out = np.stack([np.asarray(r["xout"], np.float32).T for r in res.results], 0)
    return np.ascontiguousarray(out)


def kernel_unfused(**inputs):
    common = _prep_common()
    base = dict(common)
    l0_prep(inputs, base)
    l1_prep(inputs, base)
    l3_prep(inputs, base)
    l2_maps = []
    for b in range(NCORES):
        m2 = {}
        l2_prep(inputs, m2, b)
        l2_maps.append(m2)

    x = np.asarray(inputs["L0_attn_x"], np.float32)
    xin = [np.ascontiguousarray(x[b].T) for b in range(NCORES)]

    keysets = {
        0: [k for k in base if k.startswith("l0_")],
        1: [k for k in base if k.startswith("l1_")],
        2: None,
        3: [k for k in base if k.startswith("l3_")],
    }
    ckeys = [k for k in common]
    for layer in range(4):
        if layer not in _NC_CACHE:
            _NC_CACHE[layer] = build_layer(layer)
        nc_ = _NC_CACHE[layer]
        in_maps = []
        for b in range(NCORES):
            m = {k: base[k] for k in ckeys}
            if layer == 2:
                m.update(l2_maps[b])
            else:
                for k in keysets[layer]:
                    m[k] = base[k]
            m["xin"] = xin[b]
            in_maps.append(m)
        res = run_bass_kernel_spmd(nc_, in_maps, core_ids=list(range(NCORES)))
        xin = [np.ascontiguousarray(np.asarray(r["xout"], np.float32))
               for r in res.results]

    out = np.stack([xin[b].T for b in range(NCORES)], 0)
    return np.ascontiguousarray(out)
